# version 16
# baseline (speedup 1.0000x reference)
import os
import numpy as np
import ml_dtypes
import concourse.bass as bass
import concourse.mybir as mybir
from concourse.bass_utils import run_bass_kernel_spmd
from contextlib import ExitStack

F32 = mybir.dt.float32
BF16 = mybir.dt.bfloat16
AF = mybir.ActivationFunctionType
ALU = mybir.AluOpType
AX = mybir.AxisListType.X

NCORES = 8
NTOK = 2048
NT = NTOK // 128
D = 1024
DC = D // 128
XSTR = 2052

BF = ml_dtypes.bfloat16
_cut = os.environ.get("L2_CUT", "0")
L2_CUTS = set(int(x) for x in _cut.split(","))
class _L2C:
    def __eq__(self, other):
        return other in L2_CUTS
    def __ne__(self, other):
        return other not in L2_CUTS
    def __contains__(self, other):
        return False
L2_CUT = _L2C()


class Prog:
    Q = ("pe", "act", "dve", "pool", "sp")

    def __init__(self, nc, stack):
        self.nc = nc
        self.stack = stack
        self.q = {e: [] for e in self.Q}
        self._sems = {}
        self._virt = {}
        self._rr = {}

    def sem(self, name):
        if name not in self._sems:
            h = self.stack.enter_context(self.nc.semaphore(name))
            self._sems[name] = [h, 0]
        return self._sems[name]

    DMA_Q = ("sp", "pool")
    NDMASEM = 2

    def _fence(self, waits):
        out = []
        for (n, v) in waits:
            if n in self.DMA_Q:
                if v > 0:
                    for k in range(self.NDMASEM):
                        sn = f"{n}@{k}"
                        if self.count(sn) > 0:
                            out.append((sn, self.count(sn)))
            else:
                out.append((n, v))
        return out

    def op(self, qn, build, waits=(), amt=1, sem=None, noinc=False):
        if noinc:
            wl = [(self.sem(n)[0], v) for (n, v) in self._fence(waits) if v > 0]

            def fno(eng, build=build, wl=wl):
                for (h, v) in wl:
                    eng.wait_ge(h, v)
                build(eng)
            self.q[qn].append(fno)
            return None
        if sem is None and qn == "pool" and amt == 1:
            sem = "poolc"
        guard = None
        if amt == 16 and (sem or qn) in self.DMA_Q:
            base = sem or qn
            k = self._rr.get(base, 0)
            self._rr[base] = (k + 1) % self.NDMASEM
            sem = f"{base}@{k}"
            s = self.sem(sem)
            if s[1] > 0:
                guard = (s[0], s[1])
            self._virt[base] = self._virt.get(base, 0) + 16
        else:
            s = self.sem(sem or qn)
        waits = self._fence(waits)
        s[1] += amt
        target = s[1] if guard is None else self._virt[sem.split("@")[0]]
        wl = [(self.sem(n)[0], v) for (n, v) in waits if v > 0]
        handle = s[0]

        def f(eng, build=build, wl=wl, handle=handle, amt=amt, guard=guard):
            if guard is not None:
                eng.wait_ge(guard[0], guard[1])
            for (h, v) in wl:
                eng.wait_ge(h, v)
            build(eng).then_inc(handle, amt)
        self.q[qn].append(f)
        return target

    def count(self, name):
        if name in self.DMA_Q:
            return self._virt.get(name, 0)
        return self.sem(name)[1]

    def waits(self, qn, waits):
        waits = self._fence(waits)
        wl = [(self.sem(n)[0], v) for (n, v) in waits if v > 0]

        def f(eng, wl=wl):
            for (h, v) in wl:
                eng.wait_ge(h, v)
        self.q[qn].append(f)

    def snap(self):
        return {k: v[1] for k, v in self._sems.items()}

    def barrier(self):
        snap = self.snap()
        for qn in self.Q:
            self.waits(qn, [(n, c) for n, c in snap.items() if n != qn])


def mm(P, out_ap, pairs, waits=()):
    n = len(pairs)
    r = None
    for i, (lh, rh) in enumerate(pairs):
        st, sp = (i == 0), (i == n - 1)
        r = P.op("pe", (lambda t, lh=lh, rh=rh, st=st, sp=sp: t.matmul(
            out_ap, lh, rh, start=st, stop=sp)),
            waits=(waits if i == 0 else ()), noinc=not sp)
    return r



def _prep_weights(inp):
    w = {}
    T = lambda a: np.ascontiguousarray(np.asarray(a).T)
    C = lambda a: np.ascontiguousarray(np.asarray(a))
    w["l0_wqT"] = T(inp["L0_attn_w_q"])
    w["l0_wkT"] = T(inp["L0_attn_w_k"])
    w["l0_wvT"] = T(inp["L0_attn_w_v"])
    w["l0_woT"] = T(inp["L0_attn_w_o"])
    w["l0_mem"] = C(inp["L0_attn_memory"])
    w["l0_mnorm"] = np.ascontiguousarray(
        np.asarray(inp["L0_attn_memory_norm"]).reshape(2, 128).T)
    w["l0_gate"] = np.ascontiguousarray(
        np.tile(np.asarray(inp["L0_attn_gate"]).reshape(1, 1), (128, 1)))
    w["l0_wshT"] = T(inp["L0_ffn_W_shared"])
    w["l0_wexT"] = T(inp["L0_ffn_W_expert"])
    w["l0_wg"] = C(inp["L0_ffn_W_gate"])
    w["l0_bias"] = C(inp["L0_ffn_bias"]).reshape(1, 1024)
    w["l1_win"] = C(inp["L1_attn_W_in"])
    w["l1_bin"] = np.ascontiguousarray(
        np.asarray(inp["L1_attn_b_in"]).reshape(32, 128).T)
    w["l1_wxp"] = C(inp["L1_attn_W_xp"])
    w["l1_bxp"] = C(inp["L1_attn_b_xp"]).reshape(64, 1)
    w["l1_wdt"] = C(inp["L1_attn_W_dt"])
    w["l1_bdt"] = C(inp["L1_attn_b_dt"]).reshape(1, 1024)
    w["l1_wout"] = C(inp["L1_attn_W_out"])
    w["l1_bout"] = C(inp["L1_attn_b_out"]).reshape(1, 1024)
    w["l1_alog"] = C(inp["L1_attn_A_log"]).reshape(64, 1)
    w["l1_D"] = np.ascontiguousarray(
        np.asarray(inp["L1_attn_D"]).reshape(8, 128).T)
    w["l1_nw"] = np.ascontiguousarray(
        np.asarray(inp["L1_ffn_norm_weight"]).reshape(8, 128).T)
    w["l1_rwT"] = T(inp["L1_ffn_router_w"])
    sw1 = np.asarray(inp["L1_ffn_sw1"])
    w["l1_sw1f"] = np.ascontiguousarray(
        sw1.reshape(8, 128, 4, 2, 512).transpose(2, 3, 1, 0, 4).reshape(
            4, 2, 128, 4096))
    w["l1_sb1"] = np.ascontiguousarray(
        np.asarray(inp["L1_ffn_sb1"]).reshape(32, 128).T)
    sw2 = np.asarray(inp["L1_ffn_sw2"])
    w["l1_sw2f"] = np.ascontiguousarray(
        sw2.reshape(4, 8, 128, 2, 512).transpose(0, 3, 2, 1, 4).reshape(
            4, 2, 128, 4096))
    w["l1_sb2"] = C(inp["L1_ffn_sb2"]).reshape(1, 1024)
    ew1 = np.asarray(inp["L1_ffn_ew1"])
    w["l1_ew1f"] = np.ascontiguousarray(
        ew1.reshape(8, 8, 128, 4, 2, 512).transpose(0, 3, 4, 2, 1, 5).reshape(
            8, 4, 2, 128, 4096))
    w["l1_eb1"] = np.ascontiguousarray(np.asarray(
        inp["L1_ffn_eb1"]).reshape(8, 32, 128).transpose(2, 0, 1).reshape(128, 256))
    ew2 = np.asarray(inp["L1_ffn_ew2"])
    w["l1_ew2f"] = np.ascontiguousarray(
        ew2.reshape(8, 4, 8, 128, 2, 512).transpose(0, 1, 4, 3, 2, 5).reshape(
            8, 4, 2, 128, 4096))
    w["l1_eb2"] = C(inp["L1_ffn_eb2"])
    w["l2_wkvT"] = np.ascontiguousarray(np.concatenate(
        [np.asarray(inp["L2_attn_w_K"]).reshape(512, 1024).T,
         np.asarray(inp["L2_attn_w_V"]).reshape(512, 1024).T], axis=1))
    w["l2_wqT"] = T(np.asarray(inp["L2_attn_w_Q"]).reshape(512, 1024))
    w["l2_wgT"] = T(np.asarray(inp["L2_attn_w_G"]).reshape(32, 1024))
    w["l2_bgb"] = np.ascontiguousarray(np.tile(
        np.asarray(inp["L2_attn_b_G"]).reshape(1, 32), (128, 1)))
    w["l2_wo"] = C(inp["L2_attn_w_O"])
    w["l2_ikb"] = np.ascontiguousarray(
        np.asarray(inp["L2_attn_init_K_rec"])[:, 1:32, :].reshape(124, 64).T)
    w["l2_ivb"] = np.ascontiguousarray(
        np.asarray(inp["L2_attn_init_V_rec"])[:, 1:32, :].transpose(1, 0, 2).reshape(31, 256))
    w["l2_wrT"] = T(inp["L2_ffn_W_r"])
    w["l2_wk2T"] = T(inp["L2_ffn_W_k"])
    w["l2_wv2T"] = T(inp["L2_ffn_W_v"])
    w["l2_mur"] = np.ascontiguousarray(
        np.asarray(inp["L2_ffn_mu_r"]).reshape(8, 128).T)
    w["l2_muk"] = np.ascontiguousarray(
        np.asarray(inp["L2_ffn_mu_k"]).reshape(8, 128).T)
    w["l3_wqkvT"] = T(inp["L3_attn_w_qkv"])
    w["l3_woutT"] = T(inp["L3_attn_w_out"])
    w["l3_bout"] = C(inp["L3_attn_b_out"]).reshape(1, 1024)
    w["l3_rwT"] = T(inp["L3_ffn_router_w"])
    w["l3_rb"] = C(inp["L3_ffn_router_bias"]).reshape(1, 16)
    up = np.asarray(inp["L3_ffn_up_proj"])
    w["l3_upf"] = C(up.reshape(16, 8, 128, 512).transpose(0, 2, 1, 3).reshape(
        16, 128, 4096))
    gate = np.asarray(inp["L3_ffn_gate_proj"])
    w["l3_gatef"] = C(gate.reshape(16, 8, 128, 512).transpose(0, 2, 1, 3).reshape(
        16, 128, 4096))
    down = np.asarray(inp["L3_ffn_down_proj"])
    w["l3_downf"] = C(down.reshape(16, 4, 128, 1024).transpose(0, 2, 1, 3).reshape(
        16, 128, 4096))
    shgT = np.asarray(inp["L3_ffn_sh_gate"]).T
    w["l3_shgf"] = C(shgT.reshape(8, 128, 2, 512).transpose(2, 1, 0, 3).reshape(
        2, 128, 4096))
    shuT = np.asarray(inp["L3_ffn_sh_up"]).T
    w["l3_shuf"] = C(shuT.reshape(8, 128, 2, 512).transpose(2, 1, 0, 3).reshape(
        2, 128, 4096))
    shdT = np.asarray(inp["L3_ffn_sh_down"]).T
    w["l3_shdf"] = C(shdT.reshape(2, 4, 128, 1024).transpose(0, 2, 1, 3).reshape(
        2, 128, 4096))
    return w


def _prep_core_meta(c, inp=None):
    m = {}
    batch, pos = c // 4, c % 4
    if inp is not None:
        lg0 = pos * 8
        m["l2_ikm"] = np.ascontiguousarray(np.asarray(
            inp["L2_attn_init_K_rec"])[:, lg0:lg0 + 8, :].reshape(32, 64).T)
        m["l2_ivm"] = np.ascontiguousarray(np.asarray(
            inp["L2_attn_init_V_rec"])[:, lg0:lg0 + 8, :].reshape(32, 64).T)
    pref = np.zeros((8,), np.float32)
    for r in range(8):
        if r // 4 == batch and r < c:
            pref[r] = 1.0
    m["mask_prefix"] = np.ascontiguousarray(np.tile(pref[None, :], (128, 1)))
    m["mask_prefix_inv"] = np.ascontiguousarray(
        np.tile((1.0 - pref)[None, :], (128, 1)))
    prev = np.zeros((8,), np.float32)
    if pos > 0:
        prev[c - 1] = 1.0
    m["mask_prev"] = np.ascontiguousarray(np.tile(prev[None, :], (128, 1)))
    m["is_first"] = np.full((128, 1), 1.0 if pos == 0 else 0.0, np.float32)
    t0 = pos * NTOK
    m["l3_qoff"] = np.ascontiguousarray(
        np.tile((t0 + 128 * np.arange(16, dtype=np.float32))[None, :], (128, 1)))
    return m


W_SHAPES = {
    "l0_wqT": (1024, 256), "l0_wkT": (1024, 256), "l0_wvT": (1024, 256),
    "l0_woT": (256, 1024), "l0_mem": (256, 256), "l0_mnorm": (128, 2),
    "l0_gate": (128, 1), "l0_wshT": (1024, 1024), "l0_wexT": (1024, 1024),
    "l0_wg": (1024, 7), "l0_bias": (1, 1024),
    "l1_win": (1024, 4096), "l1_bin": (128, 32), "l1_wxp": (2048, 64),
    "l1_bxp": (64, 1), "l1_wdt": (2048, 1024), "l1_bdt": (1, 1024),
    "l1_wout": (2048, 1024), "l1_bout": (1, 1024), "l1_alog": (64, 1),
    "l1_D": (128, 8), "l1_nw": (128, 8), "l1_rwT": (1024, 8),
    "l1_sw1f": (4, 2, 128, 4096), "l1_sb1": (128, 32),
    "l1_sw2f": (4, 2, 128, 4096),
    "l1_sb2": (1, 1024), "l1_ew1f": (8, 4, 2, 128, 4096), "l1_eb1": (128, 256),
    "l1_ew2f": (8, 4, 2, 128, 4096), "l1_eb2": (8, 1024),
    "l2_wkvT": (1024, 1024), "l2_wqT": (1024, 512), "l2_wgT": (1024, 32),
    "l2_bgb": (128, 32), "l2_wo": (512, 1024), "l2_ikm": (64, 32),
    "l2_ivm": (64, 32), "l2_ikb": (64, 124), "l2_ivb": (31, 256),
    "l2_wrT": (1024, 1024), "l2_wk2T": (1024, 1024), "l2_wv2T": (1024, 1024),
    "l2_mur": (128, 8), "l2_muk": (128, 8),
    "l3_wqkvT": (1024, 1536), "l3_woutT": (512, 1024), "l3_bout": (1, 1024),
    "l3_rwT": (1024, 16), "l3_rb": (1, 16), "l3_upf": (16, 128, 4096),
    "l3_gatef": (16, 128, 4096), "l3_downf": (16, 128, 4096),
    "l3_shgf": (2, 128, 4096), "l3_shuf": (2, 128, 4096),
    "l3_shdf": (2, 128, 4096),
}
L0_W = [k for k in W_SHAPES if k.startswith("l0_")]
L1_W = [k for k in W_SHAPES if k.startswith("l1_")]
L2_W = [k for k in W_SHAPES if k.startswith("l2_")]
L3_W = [k for k in W_SHAPES if k.startswith("l3_")]
LAYER_W = {0: L0_W, 1: L1_W, 2: L2_W, 3: L3_W}
META = ("mask_prefix", "mask_prefix_inv", "mask_prev", "is_first", "l3_qoff")
META_SHAPES = {"mask_prefix": (128, 8), "mask_prefix_inv": (128, 8),
               "mask_prev": (128, 8), "is_first": (128, 1), "l3_qoff": (128, 16)}



def build_program(layers):
    stack = ExitStack()
    nc = bass.Bass("TRN2", target_bir_lowering=False)
    P = Prog(nc, stack)

    dr = {}
    dr["x_in"] = nc.dram_tensor("x_in", [NTOK, D], F32, kind="ExternalInput").ap()
    dr["x_out"] = nc.dram_tensor("x_out", [NTOK, D], F32, kind="ExternalOutput").ap()
    for nm in META:
        dr[nm] = nc.dram_tensor(nm, list(META_SHAPES[nm]), F32,
                                kind="ExternalInput").ap()
    for L in layers:
        for nm in LAYER_W[L]:
            dr[nm] = nc.dram_tensor(nm, list(W_SHAPES[nm]), F32,
                                    kind="ExternalInput").ap()

    def S(n, sh, dt):
        return stack.enter_context(nc.sbuf_tensor(n, list(sh), dt))

    sb = {}
    sb["X"] = S("X", (128, NT * D), F32)
    sb["XNT"] = S("XNT", (128, DC * XSTR), BF16)
    sb["IDB"] = S("IDB", (128, 128), BF16)
    sb["IDF"] = S("IDF", (128, 128), F32)
    sb["TRI01"] = S("TRI01", (128, 128), BF16)
    sb["ONESC"] = S("ONESC", (128, 1), BF16)
    sb["ONESF"] = S("ONESF", (128, 1), F32)
    sb["ONESRB"] = S("ONESRB", (1, 128), BF16)
    sb["ONESRF"] = S("ONESRF", (1, 128), F32)
    sb["SCR1"] = S("SCR1", (128, 1024), F32)
    sb["SCRB"] = S("SCRB", (128, 1024), BF16)
    sb["STAT"] = S("STAT", (128, 8), F32)
    sb["MPREF"] = S("MPREF", (128, 8), F32)
    sb["MPREFI"] = S("MPREFI", (128, 8), F32)
    sb["MPREV"] = S("MPREV", (128, 8), F32)
    sb["ISF"] = S("ISF", (128, 1), F32)
    sb["QOFF"] = S("QOFF", (128, 16), F32)
    sb["EPS5"] = S("EPS5", (128, 1), F32)
    sb["EPS8"] = S("EPS8", (128, 1), F32)
    sb["ONE2K"] = S("ONE2K", (1, NTOK), BF16)
    sb["ONE1"] = S("ONE1", (128, 1), F32)
    sb["PLN"] = stack.enter_context(nc.psum_tensor("PLN", [128, 1024], BF16))

    P.op("sp", lambda e: e.dma_start(
        out=sb["X"][:].rearrange("p (t d) -> p t d", t=NT),
        in_=dr["x_in"].rearrange("(t p) d -> p t d", p=128)),
        amt=16)
    for nm, key in (("mask_prefix", "MPREF"), ("mask_prefix_inv", "MPREFI"),
                    ("mask_prev", "MPREV"),
                    ("is_first", "ISF"), ("l3_qoff", "QOFF")):
        P.op("sp", lambda e, t=sb[key], d_=dr[nm]: e.dma_start(out=t[:], in_=d_),
             amt=16)

    c0 = P.op("pool", lambda e: e.memset(sb["SCR1"][:, 0:128], 1.0))
    P.op("pool", lambda e: e.affine_select(
        sb["IDF"][:], sb["SCR1"][:, 0:128], pattern=[[1, 128]],
        compare_op=ALU.is_equal, fill=0.0, base=0, channel_multiplier=-1),
        waits=[("poolc", c0)])
    P.op("pool", lambda e: e.tensor_copy(sb["IDB"][:], sb["IDF"][:]),
         waits=[("poolc", P.count("poolc"))])
    P.op("pool", lambda e: e.affine_select(
        sb["TRI01"][:], sb["SCR1"][:, 0:128], pattern=[[1, 128]],
        compare_op=ALU.is_ge, fill=0.0, base=0, channel_multiplier=-1),
        waits=[("poolc", P.count("poolc"))])
    P.op("pool", lambda e: e.memset(sb["ONESC"][:], 1.0))
    P.op("pool", lambda e: e.memset(sb["ONESF"][:], 1.0))
    P.op("pool", lambda e: e.memset(sb["ONESRB"][:], 1.0))
    P.op("pool", lambda e: e.memset(sb["ONESRF"][:], 1.0))
    P.op("pool", lambda e: e.memset(sb["EPS5"][:], 1e-5))
    P.op("pool", lambda e: e.memset(sb["EPS8"][:], 1e-8))
    P.op("pool", lambda e: e.memset(sb["ONE2K"][:], 1.0))
    P.op("pool", lambda e: e.memset(sb["ONE1"][:], 1.0))
    P.barrier()

    if 1 in layers:
        dr["l1_wb1"] = nc.dram_tensor("l1_wb1", [8, 4, 2, 128, 4096], BF16).ap()
        dr["l1_wb2"] = nc.dram_tensor("l1_wb2", [8, 4, 2, 128, 4096], BF16).ap()
        dr["l1_wbs1"] = nc.dram_tensor("l1_wbs1", [4, 2, 128, 4096], BF16).ap()
        dr["l1_wbs2"] = nc.dram_tensor("l1_wbs2", [4, 2, 128, 4096], BF16).ap()
    if 3 in layers:
        dr["l3_wb3g"] = nc.dram_tensor("l3_wb3g", [18, 128, 4096], BF16).ap()
        dr["l3_wb3u"] = nc.dram_tensor("l3_wb3u", [18, 128, 4096], BF16).ap()
        dr["l3_wb3d"] = nc.dram_tensor("l3_wb3d", [18, 128, 4096], BF16).ap()

    for L in layers:
        if L == 1:
            rms2 = S("RMS2", (128, NT), F32)
            emit_layernorm(P, sb, L, rms2=rms2)
            P.barrier()
            emit_l1(P, sb, dr, rms2)
        elif L == 0:
            emit_layernorm(P, sb, L)
            P.barrier()
            emit_l0(P, sb, dr)
        elif L == 2:
            emit_layernorm(P, sb, L)
            P.barrier()
            emit_l2(P, sb, dr)
        elif L == 3:
            emit_layernorm(P, sb, L)
            P.barrier()
            emit_l3(P, sb, dr)
        else:
            raise NotImplementedError(L)
        P.barrier()

    st = P.op("sp", lambda e: e.dma_start(
        out=dr["x_out"].rearrange("(t p) d -> p t d", p=128),
        in_=sb["X"][:].rearrange("p (t d) -> p t d", t=NT)),
        amt=16)
    P.waits("sp", [("sp", st)])

    with nc.Block() as block:
        @block.tensor
        def _(t):
            for f in P.q["pe"]:
                f(t)

        @block.scalar
        def _(s):
            for f in P.q["act"]:
                f(s)

        @block.vector
        def _(v):
            for f in P.q["dve"]:
                f(v)

        @block.gpsimd
        def _(g):
            for f in P.q["pool"]:
                f(g)

        @block.sync
        def _(s):
            for f in P.q["sp"]:
                f(s)

    return nc, stack



def emit_layernorm(P, sb, L, rms2=None):
    X, XNT, STAT, SCR1, SCRB, IDB, PLN = (
        sb["X"], sb["XNT"], sb["STAT"], sb["SCR1"], sb["SCRB"], sb["IDB"], sb["PLN"])
    prev_copies = P.count("act")
    prev_tp = P.count("pe")
    for t in range(NT):
        xt = X[:, t * D:(t + 1) * D]
        d0 = P.op("dve", lambda v, xt=xt: v.reduce_sum(STAT[:, 0:1], xt, axis=AX),
                  waits=[("act", prev_copies)])
        d1 = P.op("dve", lambda v: v.tensor_scalar_mul(
            STAT[:, 1:2], STAT[:, 0:1], 1.0 / D), waits=[("dve", d0)])
        d2 = P.op("dve", lambda v, xt=xt: v.tensor_scalar(
            SCR1[:], xt, STAT[:, 1:2], None, ALU.subtract), waits=[("dve", d1)])
        a0 = P.op("act", lambda s: s.activation(SCRB[:], SCR1[:], AF.Square,
                                                accum_out=STAT[:, 2:3]),
                  waits=[("dve", d2)])
        a1 = P.op("act", lambda s: s.activation(STAT[:, 3:4], STAT[:, 2:3], AF.Ln,
                                                bias=sb["EPS5"][:, 0:1],
                                                scale=1.0 / D),
                  waits=[("act", a0)])
        a2 = P.op("act", lambda s: s.activation(STAT[:, 4:5], STAT[:, 3:4], AF.Exp,
                                                scale=-0.5), waits=[("act", a1)])
        if rms2 is not None:
            r0 = P.op("dve", lambda v: v.tensor_scalar_mul(
                STAT[:, 5:6], STAT[:, 2:3], 1.0 / D), waits=[("act", a2)])
            r1 = P.op("dve", lambda v: v.tensor_mul(
                STAT[:, 6:7], STAT[:, 4:5], STAT[:, 4:5]), waits=[("dve", r0)])
            r2 = P.op("dve", lambda v: v.tensor_mul(
                STAT[:, 6:7], STAT[:, 5:6], STAT[:, 6:7]), waits=[("dve", r1)])
            r3 = P.op("act", lambda s: s.activation(
                STAT[:, 7:8], STAT[:, 6:7], AF.Ln, bias=sb["EPS8"][:, 0:1]),
                waits=[("dve", r2)])
            r4 = P.op("act", lambda s, t=t: s.activation(
                rms2[:, t:t + 1], STAT[:, 7:8], AF.Exp, scale=-0.5),
                waits=[("act", r3)])
        d3 = P.op("dve", lambda v: v.tensor_scalar(
            SCRB[:], SCR1[:], STAT[:, 4:5], None, ALU.mult),
            waits=[("act", a2), ("pe", prev_tp)])
        for dc in range(DC):
            w = []
            if dc == 0:
                w = [("dve", d3), ("act", prev_copies)]
            prev_tp = P.op("pe", lambda t_, dc=dc: t_.transpose(
                PLN[:, dc * 128:(dc + 1) * 128],
                SCRB[:, dc * 128:(dc + 1) * 128], IDB[:]), waits=w)
        for dc in range(DC):
            dst = XNT[:, dc * XSTR + 1 + t * 128: dc * XSTR + 1 + (t + 1) * 128]
            prev_copies = P.op("act", lambda s, dst=dst, dc=dc: s.copy(
                dst, PLN[:, dc * 128:(dc + 1) * 128]),
                waits=([("pe", prev_tp)] if dc == 0 else []))



def emit_l0(P, sb, dr):
    nc = P.nc
    X, XNT = sb["X"], sb["XNT"]
    IDF, IDB, TRI01 = sb["IDF"], sb["IDB"], sb["TRI01"]
    ONESC, ONESF, ONESRB, ONESRF = (sb["ONESC"], sb["ONESF"], sb["ONESRB"],
                                    sb["ONESRF"])
    MPREF = sb["MPREF"]
    SCR1, SCRB = sb["SCR1"], sb["SCRB"]
    DM, NDD = 256, 2

    with ExitStack() as ls:
        SS = lambda n, sh, dt: ls.enter_context(nc.sbuf_tensor(n, list(sh), dt))
        PS = lambda n, sh, dt: ls.enter_context(nc.psum_tensor(n, list(sh), dt))

        S_TM = SS("l0_S_TM", (128, NT * D), BF16)
        WG = SS("l0_WG", (128, DC * 7), BF16)
        BR = SS("l0_BR", (1, 1024), BF16)
        GC = SS("l0_GC", (128, 4), F32)
        F1 = SS("l0_F1", (1, 1), F32)
        GMU = SS("l0_GMU", (128, NT), F32)
        EP7A = SS("l0_EP7A", (128, 7), F32)
        EP7B = SS("l0_EP7B", (128, 7), F32)
        EP7C = SS("l0_EP7C", (128, 7), F32)
        EPX = SS("l0_EPX", (128, 8), F32)
        D16 = SS("l0_D16", (128, 512), BF16)

        cin = nc.dram_tensor("l0_cin", [128, 514], F32)
        cout = nc.dram_tensor("l0_cout", [1024, 514], F32)

        PA = PS("l0_PA", (128, 512), F32)
        PB = PS("l0_PB", (128, 512), F32)
        PC = PS("l0_PC", (128, 512), F32)
        PD = PS("l0_PD", (128, 512), F32)
        PT = PS("l0_PT", (128, 256), BF16)

        lsA = ExitStack()
        SA = lambda n, sh, dt: lsA.enter_context(nc.sbuf_tensor(n, list(sh), dt))
        WQT = SA("l0_WQT", (128, DC * DM), BF16)
        WKT = SA("l0_WKT", (128, DC * DM), BF16)
        WVT = SA("l0_WVT", (128, DC * DM), BF16)
        WOT = SA("l0_WOT", (128, NDD * D), BF16)
        MEM = SA("l0_MEM", (128, NDD * DM), BF16)
        MN = SA("l0_MN", (128, NDD), BF16)
        KVF = SA("l0_KVF", (128, NDD * DM), F32)
        KVB = SA("l0_KVB", (128, NDD * DM), BF16)
        KCF = SA("l0_KCF", (128, NDD), F32)
        KCB = SA("l0_KCB", (128, NDD), BF16)
        SMT = SA("l0_SMT", (128, 128), BF16)
        CTM = SA("l0_CTM", (128, 256), BF16)
        EP = SA("l0_EP", (1, 512), F32)
        EPC = SA("l0_EPC", (128, 2), F32)
        CFM = SA("l0_CFM", (128, NDD * NTOK), BF16)
        PAY = SA("l0_PAY", (128, 514), F32)
        GB = SA("l0_GB", (128, 514), F32)
        PFXA = SA("l0_PFXA", (128, 514), F32)

        def wload(dst, name):
            kdim, n = W_SHAPES[name]
            c = kdim // 128
            return P.op("pool", lambda e, dst=dst, name=name, c=c: e.dma_start(
                out=dst[:].rearrange("p (c n) -> p c n", c=c),
                in_=dr[name].rearrange("(c p) n -> p c n", p=128)),
                amt=16)
        wload(WQT, "l0_wqT")
        wload(WKT, "l0_wkT")
        wload(WVT, "l0_wvT")
        wload(WOT, "l0_woT")
        wload(MEM, "l0_mem")
        P.op("pool", lambda e: e.dma_start(out=MN[:], in_=dr["l0_mnorm"]), amt=16)
        wload(WG, "l0_wg")
        P.op("pool", lambda e: e.dma_start(out=BR[:], in_=dr["l0_bias"]), amt=16)
        wA = P.op("pool", lambda e: e.dma_start(out=GC[:, 2:3], in_=dr["l0_gate"]),
                  amt=16)

        m0 = mm(P, PC[0:1, 300:301],
                [(MN[:, 0:1], ONESC[:]), (MN[:, 1:2], ONESC[:])],
                waits=[("pool", wA)])
        f0 = P.op("dve", lambda v: v.tensor_scalar(
            F1[:], PC[0:1, 300:301], 1e-6, None, ALU.is_ge), waits=[("pe", m0)])
        f1 = mm(P, PD[:, 2:3], [(ONESRF[:], F1[:])], waits=[("dve", f0)])
        g0 = P.op("act", lambda s: s.activation(GC[:, 3:4], GC[:, 2:3], AF.Sigmoid),
                  waits=[("pool", wA)])
        g1 = P.op("dve", lambda v: v.tensor_mul(GC[:, 0:1], GC[:, 3:4], PD[:, 2:3]),
                  waits=[("pe", f1), ("act", g0)])
        g2 = P.op("dve", lambda v: v.tensor_scalar(
            GC[:, 1:2], GC[:, 3:4], -1.0, 1.0, ALU.mult, ALU.add),
            waits=[("dve", g1)])

        if True:
            SQ = SA("l0_SQ", (128, NDD * NTOK), BF16)
            SK = SA("l0_SK", (128, NDD * NTOK), BF16)
            SKC = SA("l0_SKC", (128, DM), BF16)
            VT = SA("l0_VT", (128, NT * DM), BF16)

            def sigma_fm(dst, wt, pbank):
                for dd in range(NDD):
                    for tt in range(4):
                        t0 = tt * 512
                        pairs = [(wt[:, dc * DM + dd * 128: dc * DM + (dd + 1) * 128],
                                  XNT[:, dc * XSTR + 1 + t0: dc * XSTR + 1 + t0 + 512])
                                 for dc in range(DC)]
                        e0 = mm(P, pbank[:], pairs,
                                waits=[("pool", wA), ("dve", P.count("dve"))])
                        e1 = P.op("dve", lambda v, pb=pbank: v.tensor_scalar(
                            SCR1[:, 0:512], pb[:], 0.0, None, ALU.min),
                            waits=[("pe", e0), ("act", P.count("act"))])
                        e2 = P.op("act", lambda s: s.activation(
                            SCRB[:, 0:512], SCR1[:, 0:512], AF.Exp), waits=[("dve", e1)])
                        dsl = dst[:, dd * NTOK + t0: dd * NTOK + t0 + 512]
                        P.op("dve", lambda v, pb=pbank, dsl=dsl: v.scalar_tensor_tensor(
                            dsl, pb[:], 0.0, SCRB[:, 0:512], ALU.max, ALU.add),
                            waits=[("act", e2)])
            sigma_fm(SQ, WQT, PA)
            sigma_fm(SK, WKT, PB)

            for t in range(NT):
                pairs_v = [(XNT[:, dc * XSTR + 1 + t * 128: dc * XSTR + 1 + (t + 1) * 128],
                            WVT[:, dc * DM:(dc + 1) * DM]) for dc in range(DC)]
                v0 = mm(P, PB[:, 0:256], pairs_v, waits=[("dve", P.count("dve"))])
                P.op("dve", lambda v, t=t: v.tensor_copy(
                    VT[:, t * DM:(t + 1) * DM], PB[:, 0:256]), waits=[("pe", v0)])

            def skc_tp(c, extra_waits=()):
                w0 = list(extra_waits) + [("act", P.count("act"))]
                tpa = P.op("pe", lambda t: t.transpose(
                    PT[:, 0:128], SK[:, 0 * NTOK + c * 128:0 * NTOK + (c + 1) * 128],
                    IDB[:]), waits=w0)
                tpb = P.op("pe", lambda t: t.transpose(
                    PT[:, 128:256], SK[:, 1 * NTOK + c * 128:1 * NTOK + (c + 1) * 128],
                    IDB[:]))
                ca = P.op("act", lambda s: s.copy(SKC[:, 0:128], PT[:, 0:128]),
                          waits=[("pe", tpb)])
                cb = P.op("act", lambda s: s.copy(SKC[:, 128:256], PT[:, 128:256]),
                          waits=[("act", ca)])
                return cb

            u0 = 0
            vt_all = P.count("dve")
            for c in range(NT):
                cb = skc_tp(c)
                for dd in range(NDD):
                    w0 = ([("act", cb)] if dd == 0 else [])
                    if c == 0 and dd == 0:
                        w0.append(("dve", vt_all))
                    u0 = P.op("pe", lambda t, c=c, dd=dd: t.matmul(
                        PD[:, dd * 256:(dd + 1) * 256],
                        SKC[:, dd * 128:(dd + 1) * 128],
                        VT[:, c * DM:(c + 1) * DM],
                        start=(c == 0), stop=(c == NT - 1),
                        skip_group_check=True),
                        waits=w0, noinc=(c < NT - 1))
            P.op("dve", lambda v: v.tensor_copy(PAY[:, 0:512], PD[:, 0:512]),
                 waits=[("pe", u0)])
            for dd in range(NDD):
                P.op("dve", lambda v, dd=dd: v.reduce_sum(
                    PAY[:, 512 + dd:513 + dd],
                    SK[:, dd * NTOK:(dd + 1) * NTOK], axis=AX))
            pay_done = P.count("dve")
            s0 = P.op("sp", lambda e: e.dma_start(out=cin.ap(), in_=PAY[:]),
                      waits=[("dve", pay_done)], amt=16)
            cc0 = P.op("pool", lambda e: e.collective_compute(
                "AllGather", ALU.bypass, replica_groups=[list(range(NCORES))],
                ins=[cin.ap().opt()], outs=[cout.ap().opt()]),
                waits=[("sp", s0)], sem="cc")
            last_d = 0
            for r in range(NCORES):
                gslot = GB[:]
                gw = [("cc", cc0)] if r == 0 else [("dve", last_d)]
                gr = P.op("sp", lambda e, r=r, gslot=gslot: e.dma_start(
                    out=gslot, in_=cout.ap()[r * 128:(r + 1) * 128, :]),
                    waits=gw, amt=16)
                if r == 0:
                    last_d = P.op("dve", lambda v, gslot=gslot: v.tensor_scalar(
                        PFXA[:], gslot, MPREF[:, 0:1], None, ALU.mult),
                        waits=[("sp", gr)])
                else:
                    last_d = P.op("dve",
                                  lambda v, gslot=gslot, r=r:
                                  v.scalar_tensor_tensor(
                                      PFXA[:], gslot, MPREF[:, r:r + 1], PFXA[:],
                                      ALU.mult, ALU.add),
                                  waits=[("sp", gr)])
            PFX = PFXA
            k0 = P.op("dve", lambda v: v.tensor_copy(KVF[:], PFX[:, 0:512]),
                      waits=[("dve", last_d)])
            P.op("dve", lambda v: v.tensor_copy(KVB[:], KVF[:]), waits=[("dve", k0)])
            k2 = P.op("dve", lambda v: v.tensor_copy(KCF[:], PFX[:, 512:514]),
                      waits=[("dve", k0)])
            P.op("dve", lambda v: v.tensor_copy(KCB[:], KCF[:]), waits=[("dve", k2)])

            kv_add = P.count("dve")
            cfm_cp = P.count("act")
            for c in range(NT):
                sqs = [SQ[:, dd * NTOK + c * 128: dd * NTOK + (c + 1) * 128]
                       for dd in range(NDD)]
                sks = [SK[:, dd * NTOK + c * 128: dd * NTOK + (c + 1) * 128]
                       for dd in range(NDD)]
                st0 = mm(P, PA[:, 0:128], [(sks[0], sqs[0]), (sks[1], sqs[1])],
                         waits=[("dve", P.count("dve"))])
                sm0 = P.op("dve", lambda v: v.tensor_mul(
                    SMT[:], PA[:, 0:128], TRI01[:]), waits=[("pe", st0)])
                n0 = mm(P, PB[:, 0:256],
                        [(sqs[0], KVB[:, 0:256]), (sqs[1], KVB[:, 256:512]),
                         (SMT[:], VT[:, c * DM:(c + 1) * DM])],
                        waits=[("dve", sm0)])
                n1 = mm(P, PB[:, 256:512],
                        [(sqs[0], MEM[:, 0:256]), (sqs[1], MEM[:, 256:512])])
                d0_ = mm(P, PC[0:1, 0:128],
                         [(KCB[:, 0:1], sqs[0]), (KCB[:, 1:2], sqs[1]),
                          (ONESC[:], SMT[:])],
                         waits=[("act", P.count("act")), ("dve", P.count("dve"))])
                d1_ = mm(P, PC[0:1, 128:256],
                         [(MN[:, 0:1], sqs[0]), (MN[:, 1:2], sqs[1])])
                e0 = P.op("dve", lambda v: v.tensor_scalar(
                    EP[:, 0:128], PC[0:1, 0:128], 1e-6, None, ALU.max),
                    waits=[("pe", d1_)])
                e1 = P.op("dve", lambda v: v.tensor_scalar(
                    EP[:, 128:256], PC[0:1, 128:256], 1e-6, None, ALU.max),
                    waits=[("dve", e0)])
                e2 = P.op("dve", lambda v: v.reciprocal(EP[:, 256:384], EP[:, 0:128]),
                          waits=[("dve", e1)])
                e3 = P.op("dve", lambda v: v.reciprocal(EP[:, 384:512],
                                                        EP[:, 128:256]),
                          waits=[("dve", e2)])
                t0_ = P.op("pe", lambda t: t.transpose(
                    PC[:, 256:257], EP[0:1, 256:384], IDF[0:1, 0:1]),
                    waits=[("dve", e3)])
                t1_ = P.op("pe", lambda t: t.transpose(
                    PC[:, 257:258], EP[0:1, 384:512], IDF[0:1, 0:1]))
                a0 = P.op("act", lambda s: s.copy(EPC[:, 0:1], PC[:, 256:257]),
                          waits=[("pe", t1_)])
                a1 = P.op("act", lambda s: s.copy(EPC[:, 1:2], PC[:, 257:258]),
                          waits=[("act", a0)])
                q0 = P.op("dve", lambda v: v.tensor_scalar(
                    SCR1[:, 0:256], PB[:, 0:256], EPC[:, 0:1], GC[:, 1:2],
                    ALU.mult, ALU.mult),
                    waits=[("pe", n1), ("act", a1)])
                q1 = P.op("dve", lambda v: v.tensor_scalar(
                    SCR1[:, 256:512], PB[:, 256:512], EPC[:, 1:2], GC[:, 0:1],
                    ALU.mult, ALU.mult), waits=[("dve", q0)])
                q2 = P.op("dve", lambda v: v.tensor_add(
                    CTM[:], SCR1[:, 0:256], SCR1[:, 256:512]),
                    waits=[("dve", q1), ("act", cfm_cp)])
                tp0 = P.op("pe", lambda t: t.transpose(
                    PT[:, 0:128], CTM[:, 0:128], IDB[:]), waits=[("dve", q2)])
                tp1 = P.op("pe", lambda t: t.transpose(
                    PT[:, 128:256], CTM[:, 128:256], IDB[:]))
                for dd in range(NDD):
                    dst = CFM[:, dd * NTOK + c * 128: dd * NTOK + (c + 1) * 128]
                    cfm_cp = P.op("act", lambda s, dst=dst, dd=dd: s.copy(
                        dst, PT[:, dd * 128:(dd + 1) * 128]),
                        waits=([("pe", tp1)] if dd == 0 else []))
                cb = skc_tp(c, extra_waits=[("dve", kv_add)])
                for dd in range(NDD):
                    u0 = mm(P, PD[:, dd * 256:(dd + 1) * 256],
                            [(SKC[:, dd * 128:(dd + 1) * 128],
                              VT[:, c * DM:(c + 1) * DM])],
                            waits=[("act", cb), ("dve", kv_add)] if dd == 0 else ())
                    kv_add = P.op("dve", lambda v, dd=dd: v.scalar_tensor_tensor(
                        KVF[:, dd * 256:(dd + 1) * 256], PD[:, dd * 256:(dd + 1) * 256],
                        1.0, KVF[:, dd * 256:(dd + 1) * 256], ALU.mult, ALU.add),
                        waits=[("pe", u0)])
                    kv_add = P.op("dve", lambda v, dd=dd: v.tensor_copy(
                        KVB[:, dd * 256:(dd + 1) * 256],
                        KVF[:, dd * 256:(dd + 1) * 256]), waits=[("dve", kv_add)])
                    kv_add = P.op("dve", lambda v, c=c, dd=dd: v.reduce_sum(
                        EPX[:, dd:dd + 1],
                        SK[:, dd * NTOK + c * 128:dd * NTOK + (c + 1) * 128], axis=AX),
                        waits=[("dve", kv_add)])
                    kv_add = P.op("dve", lambda v, dd=dd: v.scalar_tensor_tensor(
                        KCF[:, dd:dd + 1], EPX[:, dd:dd + 1], 1.0,
                        KCF[:, dd:dd + 1], ALU.mult, ALU.add),
                        waits=[("dve", kv_add)])
                    kv_add = P.op("dve", lambda v, dd=dd: v.tensor_copy(
                        KCB[:, dd:dd + 1], KCF[:, dd:dd + 1]),
                        waits=[("dve", kv_add)])

            sv = P.count("dve")
            for t in range(NT):
                for h in range(2):
                    wo = mm(P, PA[:, 0:512],
                            [(CFM[:, dd * NTOK + t * 128: dd * NTOK + (t + 1) * 128],
                              WOT[:, dd * D + h * 512: dd * D + h * 512 + 512])
                             for dd in range(NDD)],
                            waits=[("dve", sv), ("act", P.count("act"))])
                    dst = S_TM[:, t * D + h * 512: t * D + h * 512 + 512]
                    sv = P.op("dve", lambda v, dst=dst: v.tensor_copy(dst, PA[:, 0:512]),
                              waits=[("pe", wo)])

        lsA.close()
        P.barrier()
        with ExitStack() as lsB:
            SB_ = lambda n, sh, dt: lsB.enter_context(nc.sbuf_tensor(n, list(sh), dt))
            WSH = SB_("l0_WSH", (128, DC * D), BF16)
            WEX = SB_("l0_WEX", (128, DC * D), BF16)
            wB1 = P.op("pool", lambda e: e.dma_start(
                out=WSH[:].rearrange("p (c n) -> p c n", c=DC),
                in_=dr["l0_wshT"].rearrange("(c p) n -> p c n", p=128)),
                amt=16)
            wB = P.op("pool", lambda e: e.dma_start(
                out=WEX[:].rearrange("p (c n) -> p c n", c=DC),
                in_=dr["l0_wexT"].rearrange("(c p) n -> p c n", p=128)),
                amt=16)

            zprev_pb = P.count("dve")
            zprev_pa = P.count("dve")
            for t in range(NT):
                gm = mm(P, PC[:, 0:7],
                        [(XNT[:, dc * XSTR + 1 + t * 128: dc * XSTR + 1 + (t + 1) * 128],
                          WG[:, dc * 7:(dc + 1) * 7]) for dc in range(DC)],
                        waits=[("dve", P.count("dve"))])
                r0 = P.op("dve", lambda v: v.reduce_max(EPX[:, 0:1], PC[:, 0:7],
                                                        axis=AX), waits=[("pe", gm)])
                r1 = P.op("dve", lambda v: v.tensor_scalar_mul(
                    EPX[:, 1:2], EPX[:, 0:1], -1.0), waits=[("dve", r0)])
                r2 = P.op("act", lambda s: s.activation(
                    EP7A[:], PC[:, 0:7], AF.Exp, bias=EPX[:, 1:2],
                    accum_out=EPX[:, 2:3]), waits=[("dve", r1)])
                r3 = P.op("dve", lambda v: v.reciprocal(EPX[:, 3:4], EPX[:, 2:3]),
                          waits=[("act", r2)])
                r4 = P.op("dve", lambda v: v.tensor_scalar(
                    EP7B[:], EP7A[:], EPX[:, 3:4], None, ALU.mult),
                    waits=[("dve", r3)])
                r5 = P.op("dve", lambda v: v.reduce_max(EPX[:, 4:5], EP7B[:], axis=AX),
                          waits=[("dve", r4)])
                r6 = P.op("dve", lambda v: v.tensor_scalar(
                    EP7A[:], EP7B[:], EPX[:, 4:5], None, ALU.is_ge),
                    waits=[("dve", r5)])
                r7 = P.op("dve", lambda v: v.scalar_tensor_tensor(
                    EP7C[:], EP7A[:], -1e9, EP7B[:], ALU.mult, ALU.add),
                    waits=[("dve", r6)])
                r8 = P.op("dve", lambda v: v.reduce_max(EPX[:, 5:6], EP7C[:], axis=AX),
                          waits=[("dve", r7)])
                r9 = P.op("dve", lambda v: v.tensor_add(EPX[:, 6:7], EPX[:, 4:5],
                                                        EPX[:, 5:6]),
                          waits=[("dve", r8)])
                r10 = P.op("dve", lambda v: v.reciprocal(EPX[:, 7:8], EPX[:, 6:7]),
                           waits=[("dve", r9)])
                r11 = P.op("dve", lambda v, t=t: v.tensor_mul(
                    GMU[:, t:t + 1], EPX[:, 6:7], EPX[:, 7:8]), waits=[("dve", r10)])

                for h in range(2):
                    me = mm(P, PB[:, 0:512],
                            [(XNT[:, dc * XSTR + 1 + t * 128: dc * XSTR + 1 + (t + 1) * 128],
                              WEX[:, dc * D + h * 512: dc * D + h * 512 + 512])
                             for dc in range(DC)],
                            waits=[("pool", wB), ("dve", zprev_pb)])
                    sh = mm(P, PA[:, 0:512],
                            [(XNT[:, dc * XSTR + 1 + t * 128: dc * XSTR + 1 + (t + 1) * 128],
                              WSH[:, dc * D + h * 512: dc * D + h * 512 + 512])
                             for dc in range(DC)] +
                            [(ONESRB[:], BR[0:1, h * 512:h * 512 + 512])],
                            waits=[("dve", zprev_pa)])
                    z0 = P.op("dve", lambda v, t=t: v.tensor_scalar(
                        SCRB[:, 0:512], PB[:, 0:512], GMU[:, t:t + 1], None,
                        ALU.mult),
                        waits=[("pe", me), ("dve", r11), ("act", P.count("act"))])
                    zprev_pb = z0
                    z1 = P.op("dve", lambda v: v.scalar_tensor_tensor(
                        SCR1[:, 0:512], PA[:, 0:512], 1.0, SCRB[:, 0:512],
                        ALU.mult, ALU.add),
                        waits=[("pe", sh), ("dve", z0)])
                    zprev_pa = z1
                    z2 = P.op("act", lambda s: s.activation(
                        SCRB[:, 512:1024], SCR1[:, 0:512], AF.Silu),
                        waits=[("dve", z1)])
                    xsl = X[:, t * D + h * 512: t * D + h * 512 + 512]
                    ssl = S_TM[:, t * D + h * 512: t * D + h * 512 + 512]
                    z3 = P.op("dve", lambda v, ssl=ssl: v.tensor_mul(
                        D16[:], SCRB[:, 512:1024], ssl), waits=[("act", z2)])
                    P.op("dve", lambda v, xsl=xsl: v.tensor_add(xsl, xsl, D16[:]),
                         waits=[("dve", z3)])



LAUNCH_GROUPS = [[0, 1, 2, 3]]


def kernel(**inputs):
    x = np.ascontiguousarray(
        np.asarray(inputs["L0_attn_hidden_states"], dtype=np.float32).reshape(
            NCORES * NTOK, D))
    w = _prep_weights(inputs)
    metas = [_prep_core_meta(c, inputs) for c in range(NCORES)]

    for group in LAUNCH_GROUPS:
        nc, stack = build_program(group)
        in_maps = []
        for c in range(NCORES):
            m = {"x_in": x[c * NTOK:(c + 1) * NTOK]}
            m.update({k: v for k, v in metas[c].items()
                      if (2 in group) or not k.startswith("l2_")})
            for L in group:
                for nm in LAYER_W[L]:
                    if nm in w:
                        m[nm] = w[nm]
            in_maps.append(m)
        res = run_bass_kernel_spmd(nc, in_maps, core_ids=list(range(NCORES)))
        stack.close()
        x = np.ascontiguousarray(
            np.concatenate([res.results[c]["x_out"] for c in range(NCORES)], axis=0))
    return x.reshape(2, 8192, 1024).astype(np.float32)



def emit_l1(P, sb, dr, rms2):
    nc = P.nc
    X, XNT = sb["X"], sb["XNT"]
    IDF, IDB = sb["IDF"], sb["IDB"]
    ONESC, ONESF, ONESRB, ONESRF = (sb["ONESC"], sb["ONESF"], sb["ONESRB"],
                                    sb["ONESRF"])
    MPREF, MPREFI = sb["MPREF"], sb["MPREFI"]
    SCR1, SCRB = sb["SCR1"], sb["SCRB"]
    NI = 2048
    NIC = NI // 128

    with ExitStack() as ls:
        SS = lambda n, sh, dt: ls.enter_context(nc.sbuf_tensor(n, list(sh), dt))
        PS = lambda n, sh, dt: ls.enter_context(nc.psum_tensor(n, list(sh), dt))

        BIN = SS("l1_BIN", (128, 32), F32)
        BXP = SS("l1_BXP", (64, 1), F32)
        NA = SS("l1_NA", (64, 1), F32)
        MDH = SS("l1_MDH", (128, 2 * NT), F32)
        MDT = SS("l1_MDT", (128, NT), F32)
        MDR16 = SS("l1_MDR16", (16, 128), F32)
        WXP = SS("l1_WXP", (128, NIC * 64), BF16)
        DIAGD = SS("l1_DIAGD", (128, DC * 128), BF16)
        DCOL = SS("l1_DCOL", (128, 8), F32)
        PAY2 = SS("l1_PAY2", (64, 2), F32)
        GB2 = SS("l1_GB2", (64, 2), F32)
        SINIT = SS("l1_SINIT", (64, 1), F32)
        TMP64 = SS("l1_TMP64", (64, 2), F32)

        md_dram = nc.dram_tensor("l1_md", [NTOK], F32)
        s_dram = nc.dram_tensor("l1_sdram", [NTOK, D], BF16)
        cin1 = nc.dram_tensor("l1_cin", [64, 2], F32)
        cout1 = nc.dram_tensor("l1_cout", [512, 2], F32)


        P.op("pool", lambda e: e.dma_start(out=BIN[:], in_=dr["l1_bin"]), amt=16)
        P.op("pool", lambda e: e.dma_start(out=BXP[:], in_=dr["l1_bxp"]), amt=16)
        P.op("pool", lambda e: e.dma_start(out=DCOL[:], in_=dr["l1_D"]), amt=16)
        P.op("pool", lambda e: e.dma_start(
            out=WXP[:].rearrange("p (c n) -> p c n", c=NIC),
            in_=dr["l1_wxp"].rearrange("(c p) n -> p c n", p=128)), amt=16)
        al = P.op("pool", lambda e: e.dma_start(out=NA[:], in_=dr["l1_alog"]),
                  amt=16)
        sml = P.count("pool")
        na0 = P.op("act", lambda s: s.activation(NA[:], NA[:], AF.Exp),
                   waits=[("pool", sml)])
        na1 = P.op("dve", lambda v: v.tensor_scalar_mul(NA[:], NA[:], -1.0),
                   waits=[("act", na0)])
        for dc in range(DC):
            P.op("dve", lambda v, dc=dc: v.tensor_scalar(
                DIAGD[:, dc * 128:(dc + 1) * 128], IDB[:], DCOL[:, dc:dc + 1],
                None, ALU.mult), waits=[("pool", sml)])
        sini = P.op("pool", lambda e: e.memset(SINIT[:], 0.0))

        with ExitStack() as lsA:
            SA = lambda n, sh, dt: lsA.enter_context(nc.sbuf_tensor(n, list(sh), dt))
            XIT = SA("l1_XIT", (128, NIC * NTOK), BF16)
            SSB = SA("l1_SSB", (128, NTOK), BF16)
            PJ = lsA.enter_context(nc.psum_tensor("l1_PJ", [128, 2048], F32))
            PSC = lsA.enter_context(nc.psum_tensor("l1_PSC", [128, 512], F32))
            PSD = lsA.enter_context(nc.psum_tensor("l1_PSD", [128, 512], F32))

            with ExitStack() as ls1:
                WINJ = ls1.enter_context(nc.sbuf_tensor("l1_WINJ", [128, 2 * 1024], BF16))
                ev = P.count("act")
                wj_prev = 0
                for j in range(NIC):
                    slot = WINJ[:, (j % 2) * 1024:(j % 2) * 1024 + 1024]
                    wj = P.op("pool", lambda e, j=j, slot=slot: e.dma_start(
                        out=slot.rearrange("p (c n) -> p c n", c=DC),
                        in_=dr["l1_win"][:, j * 128:(j + 1) * 128].rearrange(
                            "(c p) n -> p c n", p=128)),
                        waits=[("pe", wj_prev)], amt=16)
                    for tt in range(4):
                        pairs = [(slot[:, dc * 128:(dc + 1) * 128],
                                  XNT[:, dc * XSTR + 1 + tt * 512: dc * XSTR + 1 + tt * 512 + 512])
                                 for dc in range(DC)]
                        wj_prev = mm(P, PJ[:, tt * 512:(tt + 1) * 512], pairs,
                                     waits=[("pool", wj), ("act", ev)])
                    for tt in range(4):
                        dst = XIT[:, j * NTOK + tt * 512: j * NTOK + tt * 512 + 512]
                        ev = P.op("act", lambda s, dst=dst, tt=tt, j=j: s.activation(
                            dst, PJ[:, tt * 512:(tt + 1) * 512], AF.Silu,
                            bias=BIN[:, j:j + 1]),
                            waits=([("pe", wj_prev)] if tt == 0 else []))

            P.barrier()
            with ExitStack() as lsD:
                WDTH = lsD.enter_context(nc.sbuf_tensor("l1_WDTH", [128, NIC * 512], BF16))
                BDTR = lsD.enter_context(nc.sbuf_tensor("l1_BDTR", [1, 1024], BF16))
                bd = P.op("pool", lambda e: e.dma_start(out=BDTR[:], in_=dr["l1_bdt"]),
                          amt=16)
                P.waits("pe", [("pool", bd)])
                for h in range(2):
                    wd = P.op("pool", lambda e, h=h: e.dma_start(
                        out=WDTH[:].rearrange("p (c n) -> p c n", c=NIC),
                        in_=dr["l1_wdt"][:, h * 512:(h + 1) * 512].rearrange(
                            "(c p) n -> p c n", p=128)),
                        waits=[("pe", P.count("pe"))], amt=16)
                    dprev = P.count("dve")
                    for t in range(NT):
                        pairs = [(XIT[:, k * NTOK + t * 128: k * NTOK + (t + 1) * 128],
                                  WDTH[:, k * 512:(k + 1) * 512]) for k in range(NIC)]
                        pairs.append((ONESRB[:], BDTR[0:1, h * 512:h * 512 + 512]))
                        dm = mm(P, PSC[:], pairs,
                                waits=[("pool", wd), ("act", P.count("act")),
                                       ("dve", dprev)])
                        ab = P.op("act", lambda s: s.activation(
                            SCR1[:, 0:512], PSC[:], AF.Abs), waits=[("pe", dm)])
                        ex = P.op("act", lambda s: s.activation(
                            SCR1[:, 512:1024], SCR1[:, 0:512], AF.Exp, scale=-1.0),
                            waits=[("act", ab)])
                        ln = P.op("act", lambda s: s.activation(
                            SCRB[:, 0:512], SCR1[:, 512:1024], AF.Ln,
                            bias=sb["ONE1"][:, 0:1]), waits=[("act", ex)])
                        dprev = P.op("dve", lambda v, h=h, t=t: v.scalar_tensor_tensor(
                            SCRB[:, 512:1024], PSC[:], 0.0, SCRB[:, 0:512],
                            ALU.max, ALU.add,
                            accum_out=MDH[:, h * NT + t:h * NT + t + 1]),
                            waits=[("act", ln)])
            md0 = P.op("dve", lambda v: v.tensor_add(
                MDT[:], MDH[:, 0:NT], MDH[:, NT:2 * NT]),
                waits=[("dve", P.count("dve"))])
            md1 = P.op("dve", lambda v: v.tensor_scalar_mul(MDT[:], MDT[:], 1.0 / 1024),
                       waits=[("dve", md0)])
            tp0 = P.op("pe", lambda t: t.transpose(PSD[0:16, 0:128], MDT[:], IDF[:]),
                       waits=[("dve", md1)])
            cp0 = P.op("act", lambda s: s.copy(MDR16[:], PSD[0:16, 0:128]),
                       waits=[("pe", tp0)])
            dm0 = P.op("sp", lambda e: e.dma_start(
                out=md_dram.ap().rearrange("(t p) -> t p", p=128), in_=MDR16[:]),
                waits=[("act", cp0)], amt=16)

            P.barrier()
            with ExitStack() as lsS:
                S2 = lambda n, sh, dt: lsS.enter_context(nc.sbuf_tensor(n, list(sh), dt))
                A_SC = S2("l1_ASC", (64, NTOK), BF16)
                BX = S2("l1_BX", (64, NTOK), F32)
                MDROW = S2("l1_MDROW", (1, NTOK), F32)
                ROW5 = S2("l1_ROW5", (1, 512), F32)
                ROW5B = S2("l1_ROW5B", (1, 512), F32)
                SINB = S2("l1_SINB", (64, 1), BF16)
                dm1 = P.op("sp", lambda e: e.dma_start(
                    out=MDROW[:], in_=md_dram.ap().rearrange("(a b) -> a b", a=1)),
                    waits=[("sp", dm0)], amt=16)
                for tt in range(4):
                    b0 = mm(P, PSC[0:64, 0:512],
                            [(ONESRF[0:1, 0:64], MDROW[0:1, tt * 512:tt * 512 + 512])],
                            waits=[("sp", dm1), ("dve", na1),
                                   ("act", P.count("act"))])
                    P.op("act", lambda s, tt=tt: s.activation(
                        A_SC[:, tt * 512:tt * 512 + 512], PSC[0:64, 0:512], AF.Exp,
                        scale=NA[:, 0:1]), waits=[("pe", b0)])
                for tt in range(4):
                    pairs = [(WXP[:, k * 64:(k + 1) * 64],
                              XIT[:, k * NTOK + tt * 512: k * NTOK + tt * 512 + 512])
                             for k in range(NIC)]
                    b1 = mm(P, PSD[0:64, 0:512], pairs,
                            waits=[("dve", P.count("dve")), ("pool", sml)])
                    P.op("dve", lambda v, tt=tt: v.tensor_scalar(
                        BX[:, tt * 512:tt * 512 + 512], PSD[0:64, 0:512],
                        BXP[:, 0:1], None, ALU.add),
                        waits=[("pe", b1)])
                ac0 = P.op("dve", lambda v: v.reduce_sum(
                    TMP64[0:1, 0:1], MDROW[:], axis=AX),
                    waits=[("sp", dm1), ("dve", P.count("dve"))])
                ac1 = mm(P, PSC[0:64, 0:1], [(ONESRF[0:1, 0:64], TMP64[0:1, 0:1])],
                         waits=[("dve", ac0), ("act", P.count("act"))])
                ac2 = P.op("dve", lambda v: v.tensor_mul(
                    TMP64[:, 0:1], NA[:], PSC[0:64, 0:1]), waits=[("pe", ac1)])
                ac3 = P.op("act", lambda s: s.activation(
                    PAY2[:, 0:1], TMP64[:, 0:1], AF.Exp), waits=[("dve", ac2)])
                sc0 = P.op("dve", lambda v: v.tensor_tensor_scan(
                    BX[:], A_SC[:], BX[:], 0.0, ALU.mult, ALU.add),
                    waits=[("act", P.count("act")), ("dve", P.count("dve"))])
                pb0 = P.op("dve", lambda v: v.tensor_copy(
                    PAY2[:, 1:2], BX[:, NTOK - 1:NTOK]), waits=[("dve", sc0)])
                cs0 = P.op("sp", lambda e: e.dma_start(out=cin1.ap(), in_=PAY2[:]),
                           waits=[("act", ac3), ("dve", P.count("dve"))], amt=16)
                cc1 = P.op("pool", lambda e: e.collective_compute(
                    "AllGather", ALU.bypass, replica_groups=[list(range(NCORES))],
                    ins=[cin1.ap().opt()], outs=[cout1.ap().opt()]),
                    waits=[("sp", cs0)], sem="cc")
                gprev = 0
                for r in range(NCORES):
                    gr = P.op("sp", lambda e, r=r: e.dma_start(
                        out=GB2[:], in_=cout1.ap()[r * 64:(r + 1) * 64, :]),
                        waits=([("cc", P.count("cc"))] if r == 0 else
                               [("dve", gprev)]), amt=16)
                    c0_ = P.op("dve", lambda v: v.tensor_mul(
                        TMP64[:, 0:1], GB2[:, 0:1], SINIT[:]),
                        waits=[("sp", gr), ("poolc", sini)])
                    c1_ = P.op("dve", lambda v: v.tensor_add(
                        TMP64[:, 0:1], TMP64[:, 0:1], GB2[:, 1:2]),
                        waits=[("dve", c0_)])
                    c2_ = P.op("dve", lambda v, r=r: v.tensor_scalar(
                        TMP64[:, 1:2], SINIT[:], MPREFI[0:64, r:r + 1], None,
                        ALU.mult), waits=[("dve", c1_)])
                    gprev = P.op("dve", lambda v, r=r: v.scalar_tensor_tensor(
                        SINIT[:], TMP64[:, 0:1], MPREF[0:64, r:r + 1],
                        TMP64[:, 1:2], ALU.mult, ALU.add), waits=[("dve", c2_)])
                cm0 = P.op("dve", lambda v: v.tensor_tensor_scan(
                    MDROW[:], sb["ONE2K"][:], MDROW[:], 0.0, ALU.mult, ALU.add),
                    waits=[("dve", P.count("dve"))])
                sb0 = P.op("dve", lambda v: v.tensor_copy(SINB[:], SINIT[:]),
                           waits=[("dve", gprev)])
                for tt in range(4):
                    s0 = mm(P, PSD[0:1, 0:512],
                            [(ONESF[0:64, 0:1], BX[:, tt * 512:tt * 512 + 512])],
                            waits=[("dve", P.count("dve")),
                                   ("act", P.count("act"))])
                    p0 = mm(P, PSC[0:64, 0:512],
                            [(ONESRF[0:1, 0:64], MDROW[0:1, tt * 512:tt * 512 + 512])],
                            waits=[("dve", cm0), ("act", P.count("act"))])
                    p1 = P.op("act", lambda s: s.activation(
                        A_SC[:, 0:512], PSC[0:64, 0:512], AF.Exp,
                        scale=NA[:, 0:1]), waits=[("pe", p0)])
                    p2 = mm(P, PJ[0:1, 0:512], [(SINB[:], A_SC[0:64, 0:512])],
                            waits=[("act", p1), ("dve", sb0)])
                    p3 = P.op("act", lambda s: s.copy(ROW5B[:], PJ[0:1, 0:512]),
                              waits=[("pe", p2)])
                    p4 = P.op("dve", lambda v: v.tensor_add(
                        ROW5[:], PSD[0:1, 0:512], ROW5B[:]),
                        waits=[("act", p3), ("pe", s0)])
                    q0 = mm(P, PSC[:, 0:512], [(ONESRF[:], ROW5[:])],
                            waits=[("dve", p4), ("act", P.count("act"))])
                    P.op("act", lambda s, tt=tt: s.copy(
                        SSB[:, tt * 512:tt * 512 + 512], PSC[:, 0:512]),
                        waits=[("pe", q0)])

            P.barrier()
            with ExitStack() as lsR:
                WINJ = lsR.enter_context(nc.sbuf_tensor("l1_WINJ2", [128, 2 * 1024], BF16))
                RSJ = lsR.enter_context(nc.sbuf_tensor("l1_RSJ", [128, NTOK], BF16))
                wj_prev = 0
                rv = P.count("dve")
                for j in range(NIC):
                    slot = WINJ[:, (j % 2) * 1024:(j % 2) * 1024 + 1024]
                    wj = P.op("pool", lambda e, j=j, slot=slot: e.dma_start(
                        out=slot.rearrange("p (c n) -> p c n", c=DC),
                        in_=dr["l1_win"][:, (16 + j) * 128:(17 + j) * 128].rearrange(
                            "(c p) n -> p c n", p=128)),
                        waits=[("pe", wj_prev)], amt=16)
                    for tt in range(4):
                        pairs = [(slot[:, dc * 128:(dc + 1) * 128],
                                  XNT[:, dc * XSTR + 1 + tt * 512: dc * XSTR + 1 + tt * 512 + 512])
                                 for dc in range(DC)]
                        wj_prev = mm(P, PJ[:, tt * 512:(tt + 1) * 512], pairs,
                                     waits=[("pool", wj), ("act", P.count("act")),
                                            ("dve", rv)])
                    for tt in range(4):
                        P.op("act", lambda s, tt=tt, j=j: s.activation(
                            RSJ[:, tt * 512:tt * 512 + 512],
                            PJ[:, tt * 512:(tt + 1) * 512], AF.Silu,
                            bias=BIN[:, 16 + j:17 + j]),
                            waits=([("pe", wj_prev)] if tt == 0 else []))
                    r0 = P.op("dve", lambda v, j=j: v.tensor_mul(
                        XIT[:, j * NTOK:(j + 1) * NTOK],
                        XIT[:, j * NTOK:(j + 1) * NTOK], RSJ[:]),
                        waits=[("act", P.count("act"))])
                    rv = P.op("dve", lambda v, j=j: v.tensor_mul(
                        XIT[:, j * NTOK:(j + 1) * NTOK],
                        XIT[:, j * NTOK:(j + 1) * NTOK], SSB[:]),
                        waits=[("dve", r0)])

            P.barrier()
            with ExitStack() as lsW:
                WOUTH = lsW.enter_context(
                    nc.sbuf_tensor("l1_WOUTH", [128, NIC * 512], BF16))
                BOUTR = lsW.enter_context(
                    nc.sbuf_tensor("l1_BOUTR", [1, 1024], BF16))
                SOUT = lsW.enter_context(
                    nc.sbuf_tensor("l1_SOUT", [128, 1024], BF16))
                DGB = lsW.enter_context(
                    nc.sbuf_tensor("l1_DGB", [128, 512], BF16))
                bo = P.op("pool", lambda e: e.dma_start(out=BOUTR[:], in_=dr["l1_bout"]),
                          amt=16)
                P.waits("pe", [("pool", bo)])
                for h in range(2):
                    wh = P.op("pool", lambda e, h=h: e.dma_start(
                        out=WOUTH[:].rearrange("p (c n) -> p c n", c=NIC),
                        in_=dr["l1_wout"][:, h * 512:(h + 1) * 512].rearrange(
                            "(c p) n -> p c n", p=128)),
                        waits=[("pe", P.count("pe"))], amt=16)
                    sv = P.count("dve")
                    dgv = P.count("act")
                    for t in range(NT):
                        w0 = [("pool", wh), ("dve", sv)]
                        for k in range(NIC):
                            P.op("pe", lambda t_, k=k, t=t: t_.matmul(
                                PSC[:], XIT[:, k * NTOK + t * 128: k * NTOK + (t + 1) * 128],
                                WOUTH[:, k * 512:(k + 1) * 512],
                                start=(k == 0), stop=False),
                                waits=(w0 if k == 0 else ()), noinc=True)
                        wlast = P.op("pe", lambda t_, h=h: t_.matmul(
                            PSC[:], ONESRB[:], BOUTR[0:1, h * 512:h * 512 + 512],
                            start=False, stop=True))
                        for i, dc in enumerate(range(4 * h, 4 * h + 4)):
                            wd0 = [("act", dgv)] if i == 0 else []
                            P.op("pe", lambda t_, dc=dc, i=i, t=t: t_.matmul(
                                PSD[:, i * 128:(i + 1) * 128],
                                XNT[:, dc * XSTR + 1 + t * 128: dc * XSTR + 1 + (t + 1) * 128],
                                DIAGD[:, dc * 128:(dc + 1) * 128],
                                start=True, stop=True), waits=wd0, noinc=(i < 3))
                        dlast = P.count("pe")
                        dgv = P.op("act", lambda s: s.copy(DGB[:], PSD[:]),
                                   waits=[("pe", dlast)])
                        slot = SOUT[:, (t % 2) * 512:(t % 2) * 512 + 512]
                        sv = P.op("dve", lambda v, slot=slot: v.scalar_tensor_tensor(
                            slot, PSC[:], 1.0, DGB[:], ALU.mult, ALU.add),
                            waits=[("pe", wlast), ("act", dgv),
                                   ("sp", P.count("sp"))])
                        P.op("sp", lambda e, t=t, h=h, slot=slot: e.dma_start(
                            out=s_dram.ap()[t * 128:(t + 1) * 128,
                                            h * 512:h * 512 + 512],
                            in_=slot), waits=[("dve", sv)], amt=16)

    P.barrier()
    emit_l1_ffn(P, sb, dr, rms2, s_dram, md_dram)


def emit_l1_ffn(P, sb, dr, rms2, s_dram, md_dram):
    nc = P.nc
    X, XNT = sb["X"], sb["XNT"]
    IDF, IDB = sb["IDF"], sb["IDB"]
    ONESRB, ONESRF = sb["ONESRB"], sb["ONESRF"]
    SCR1, SCRB = sb["SCR1"], sb["SCRB"]

    x_spill = nc.dram_tensor("l1_xspill", [NTOK, D], F32)

    with ExitStack() as lf:
        SF = lambda n, sh, dt: lf.enter_context(nc.sbuf_tensor(n, list(sh), dt))

        NW = SF("l1_NW", (128, 8), F32)
        RW = SF("l1_RW", (128, DC * 8), BF16)
        SB1 = SF("l1_SB1", (128, 32), F32)
        EB1 = SF("l1_EB1", (128, 256), F32)
        SB2R = SF("l1_SB2R", (1, 1024), BF16)
        EB2 = SF("l1_EB2", (8, 1024), BF16)
        COMB = SF("l1_COMB", (128, 8), BF16)
        P8F = SF("l1_P8F", (128, 8), F32)
        GE8 = SF("l1_GE8", (128, 8), F32)
        EPX = SF("l1_EPXF", (128, 8), F32)
        CF = SF("l1_CF", (8, 1024), BF16)
        CBA = SF("l1_CBA", (128, 64), F32)
        OA = SF("l1_OA", (128, 8 * 1024), F32)
        HGRP = SF("l1_HGRP", (128, 8 * 1024), BF16)
        EW1G = SF("l1_EW1G", (128, 8 * 1024), BF16)
        EW2G = SF("l1_EW2G", (128, 8 * 1024), BF16)
        SIN2 = SF("l1_SIN2", (128, 1024), BF16)

        ld0 = P.op("pool", lambda e: e.dma_start(out=NW[:], in_=dr["l1_nw"]), amt=16)
        P.op("pool", lambda e: e.dma_start(
            out=RW[:].rearrange("p (c n) -> p c n", c=DC),
            in_=dr["l1_rwT"].rearrange("(c p) n -> p c n", p=128)), amt=16)
        P.op("pool", lambda e: e.dma_start(out=SB1[:], in_=dr["l1_sb1"]), amt=16)
        P.op("pool", lambda e: e.dma_start(out=EB1[:], in_=dr["l1_eb1"]), amt=16)
        P.op("pool", lambda e: e.dma_start(out=SB2R[:], in_=dr["l1_sb2"]), amt=16)
        ld1 = P.op("pool", lambda e: e.dma_start(out=EB2[:], in_=dr["l1_eb2"]), amt=16)
        xs = P.op("sp", lambda e: e.dma_start(
            out=x_spill.ap().rearrange("(t p) d -> p t d", p=128),
            in_=X[:].rearrange("p (t d) -> p t d", t=NT)),
            waits=[("dve", P.count("dve")), ("act", P.count("act"))], amt=16)

        with ExitStack() as lr:
            RB16 = lr.enter_context(nc.sbuf_tensor("l1_RB16", [16, 128], F32))
            RSROW = lr.enter_context(nc.sbuf_tensor("l1_RSROW", [1, NTOK], BF16))
            PR0 = lr.enter_context(nc.psum_tensor("l1_PR0", [128, 512], F32))
            rs_dram = nc.dram_tensor("l1_rs", [NTOK], F32)
            tp = P.op("pe", lambda t: t.transpose(PR0[0:16, 0:128], rms2[:], IDF[:]),
                      waits=[("dve", P.count("dve")), ("act", P.count("act"))])
            cp = P.op("act", lambda s: s.copy(RB16[:], PR0[0:16, 0:128]),
                      waits=[("pe", tp)])
            d0 = P.op("sp", lambda e: e.dma_start(
                out=rs_dram.ap().rearrange("(t p) -> t p", p=128), in_=RB16[:]),
                waits=[("act", cp)], amt=16)
            d1 = P.op("pool", lambda e: e.dma_start(
                out=RSROW[:], in_=rs_dram.ap().rearrange("(a b) -> a b", a=1)),
                waits=[("sp", d0)], amt=16)
            nprev = P.count("dve")
            for tt in range(4):
                q0 = mm(P, PR0[:, 0:512],
                        [(ONESRB[:], RSROW[0:1, tt * 512:tt * 512 + 512])],
                        waits=[("pool", P.count("pool")), ("dve", nprev)])
                for dc in range(DC):
                    sl = XNT[:, dc * XSTR + 1 + tt * 512: dc * XSTR + 1 + tt * 512 + 512]
                    nprev = P.op("dve", lambda v, sl=sl, dc=dc: v.scalar_tensor_tensor(
                        sl, sl, NW[:, dc:dc + 1], PR0[:, 0:512], ALU.mult, ALU.mult),
                        waits=([("pe", q0), ("pool", ld0)] if dc == 0 else
                               [("dve", nprev)]))
            P.barrier()

        PRX = lf.enter_context(nc.psum_tensor("l1_PRX", [128, 512], F32))
        PB2 = sb["PLN"]
        elist = list(range(8)) + ["sh"]
        steps = [(e_, g) for e_ in elist for g in range(4)]

        semn = lambda e_: "castsh" if e_ == "sh" else f"cast{e_ % 2}"

        def _cast_chunk(e_, g, which):
            if which == 1:
                src = dr["l1_sw1f"][g] if e_ == "sh" else dr["l1_ew1f"][e_, g]
                dst = dr["l1_wbs1"][g] if e_ == "sh" else dr["l1_wb1"][e_, g]
            else:
                src = dr["l1_sw2f"][g] if e_ == "sh" else dr["l1_ew2f"][e_, g]
                dst = dr["l1_wbs2"][g] if e_ == "sh" else dr["l1_wb2"][e_, g]
            return P.op("pool", lambda e, dst=dst, src=src: e.dma_start(
                out=dst, in_=src), amt=16, sem=semn(e_))

        cast_done = {}
        for half in range(2):
            ho = half * 1024
            if half == 0:
                for wch in (1, 2):
                    for gg in range(4):
                        lastc = _cast_chunk(0, gg, wch)
                cast_done[0] = lastc

            def wload1h(e_, g, jh, extra=()):
                src = (dr["l1_wbs1"][g, jh] if e_ == "sh"
                       else dr["l1_wb1"][e_, g, jh])
                return P.op("pool", lambda e, src=src, jh=jh: e.dma_start(
                    out=EW1G[:, jh * 4096:(jh + 1) * 4096], in_=src),
                    waits=list(extra) + [("pe", ew1_rd[jh])], amt=16,
                    sem=f"ew1s{jh}")

            def wload2h(e_, g, hh, extra=()):
                src = (dr["l1_wbs2"][g, hh] if e_ == "sh"
                       else dr["l1_wb2"][e_, g, hh])
                return P.op("pool", lambda e, src=src, hh=hh: e.dma_start(
                    out=EW2G[:, hh * 4096:(hh + 1) * 4096], in_=src),
                    waits=list(extra) + [("pe", ew2_rd[hh])], amt=16,
                    sem=f"ew2s{hh}")

            def cwait(e_):
                return [(semn(e_), cast_done[e_])] if half == 0 else []

            l1s = [[None, None] for _ in steps]
            l2s = [[None, None] for _ in steps]
            rel = prev_pe if half == 1 else 0
            ew1_rd = [rel, rel]
            ew2_rd = [rel, rel]
            l1s[0][0] = wload1h(steps[0][0], steps[0][1], 0, extra=cwait(0))
            l1s[0][1] = wload1h(steps[0][0], steps[0][1], 1, extra=cwait(0))
            l2s[0][0] = wload2h(steps[0][0], steps[0][1], 0, extra=cwait(0))
            l2s[0][1] = wload2h(steps[0][0], steps[0][1], 1, extra=cwait(0))

            if True:
                PR = PRX
                if half == 0:
                    rprev, acop = P.count("dve"), P.count("act")
                else:
                    rprev, acop = exp_end
                for t8 in range(8):
                    toff = ho + t8 * 128
                    pairs = [(XNT[:, dc * XSTR + 1 + toff: dc * XSTR + 1 + toff + 128],
                              RW[:, dc * 8:(dc + 1) * 8]) for dc in range(DC)]
                    g0 = mm(P, PR[:, 0:8], pairs, waits=[("dve", rprev)])
                    a0 = P.op("act", lambda s: s.activation(P8F[:], PR[:, 0:8],
                                                            AF.Sigmoid),
                              waits=[("pe", g0)])
                    m0 = P.op("dve", lambda v: v.reduce_max(EPX[:, 0:1], P8F[:],
                                                            axis=AX),
                              waits=[("act", a0)])
                    m1 = P.op("dve", lambda v: v.tensor_scalar(
                        GE8[:], P8F[:], EPX[:, 0:1], None, ALU.is_ge),
                        waits=[("dve", m0)])
                    m2 = P.op("dve", lambda v: v.scalar_tensor_tensor(
                        GE8[:], GE8[:], -1e9, P8F[:], ALU.mult, ALU.add),
                        waits=[("dve", m1)])
                    m3 = P.op("dve", lambda v: v.reduce_max(EPX[:, 1:2], GE8[:],
                                                            axis=AX),
                              waits=[("dve", m2)])
                    m4 = P.op("dve", lambda v: v.tensor_add(EPX[:, 2:3], EPX[:, 0:1],
                                                            EPX[:, 1:2]),
                              waits=[("dve", m3)])
                    m5 = P.op("dve", lambda v: v.tensor_scalar_add(
                        EPX[:, 2:3], EPX[:, 2:3], 1e-6), waits=[("dve", m4)])
                    m6 = P.op("dve", lambda v: v.reciprocal(EPX[:, 3:4], EPX[:, 2:3]),
                              waits=[("dve", m5)])
                    m7 = P.op("dve", lambda v: v.tensor_scalar(
                        GE8[:], P8F[:], EPX[:, 1:2], None, ALU.is_ge),
                        waits=[("dve", m6)])
                    m8 = P.op("dve", lambda v: v.tensor_scalar(
                        P8F[:], P8F[:], EPX[:, 3:4], None, ALU.mult),
                        waits=[("dve", m7)])
                    m9 = P.op("dve", lambda v: v.tensor_mul(COMB[:], P8F[:], GE8[:]),
                              waits=[("dve", m8)])
                    m9b = P.op("dve", lambda v, t8=t8: v.tensor_mul(
                        CBA[:, t8 * 8:(t8 + 1) * 8], P8F[:], GE8[:]),
                        waits=[("dve", m9)])
                    tq = P.op("pe", lambda t_: t_.transpose(
                        PB2[0:8, 0:128], COMB[:], IDB[:]),
                        waits=[("dve", m9), ("act", acop)])
                    acop = P.op("act", lambda s, t8=t8: s.copy(
                        CF[:, t8 * 128:(t8 + 1) * 128], PB2[0:8, 0:128]),
                        waits=[("pe", tq)])
                    rprev = m9b
                cfdone = P.count("act")

            with ExitStack() as lxp:
                PH = lxp.enter_context(nc.psum_tensor(f"l1_PH{half}", [128, 2 * 1024], F32))
                PO = lxp.enter_context(nc.psum_tensor(f"l1_PO{half}", [128, 2 * 512], F32))
                PRC = PRX

                phslot_rd = [0, 0]
                poslot_rd = [0, 0]
                oa_init = [False] * 16
                operv = P.count("dve")
                pend = []
                ctgt = None
                for si, (e_, g) in enumerate(steps):
                    if half == 0 and g == 0:
                        ctgt = ((e_ + 1) if isinstance(e_, int) and e_ < 7
                                else ("sh" if e_ == 7 else None))
                        pend = ([(gg, wch) for wch in (1, 2)
                                 for gg in range(4)] if ctgt is not None else [])
                    hbase = 0
                    for jj in range(8):
                        j = g * 8 + jj
                        ph = PH[:, (jj % 2) * 1024:(jj % 2) * 1024 + 1024]
                        for q in range(2):
                            w = ()
                            if q == 0:
                                w = [("act", phslot_rd[jj % 2])]
                                if jj == 0:
                                    w.append(("ew1s0", l1s[si][0]))
                                if jj == 4:
                                    w.append(("ew1s1", l1s[si][1]))
                            pq = [(EW1G[:, (jj // 4) * 4096 + dc * 512 +
                                        (jj % 4) * 128:
                                        (jj // 4) * 4096 + dc * 512 +
                                        (jj % 4) * 128 + 128],
                                   XNT[:, dc * XSTR + 1 + ho + q * 512:
                                       dc * XSTR + 1 + ho + q * 512 + 512])
                                  for dc in range(DC)]
                            h0 = mm(P, ph[:, q * 512:(q + 1) * 512], pq, waits=w)
                        bias = (SB1[:, j:j + 1] if e_ == "sh"
                                else EB1[:, e_ * 32 + j:e_ * 32 + j + 1])
                        ha = P.op("act", lambda s, jj=jj, bias=bias, ph=ph, hbase=hbase:
                                  s.activation(
                            HGRP[:, hbase + jj * 1024: hbase + (jj + 1) * 1024],
                            ph, AF.Relu, bias=bias), waits=[("pe", h0)])
                        phslot_rd[jj % 2] = ha
                        if jj == 3:
                            ew1_rd[0] = P.count("pe")
                            if si + 1 < len(steps):
                                l1s[si + 1][0] = wload1h(
                                    steps[si + 1][0], steps[si + 1][1], 0,
                                    extra=cwait(steps[si + 1][0]))
                        if jj == 7:
                            ew1_rd[1] = P.count("pe")
                            if si + 1 < len(steps):
                                l1s[si + 1][1] = wload1h(
                                    steps[si + 1][0], steps[si + 1][1], 1,
                                    extra=cwait(steps[si + 1][0]))
                    hall_act = P.count("act")
                    hall_dve = P.count("dve")
                    for hh in range(2):
                        for t8 in range(8):
                            po = PO[:, (t8 % 2) * 512:(t8 % 2) * 512 + 512]
                            pairs = [(HGRP[:, hbase + jj * 1024 + t8 * 128:
                                          hbase + jj * 1024 + (t8 + 1) * 128],
                                      EW2G[:, hh * 4096 + jj * 512:
                                           hh * 4096 + jj * 512 + 512])
                                     for jj in range(8)]
                            if e_ == "sh" and g == 0:
                                pairs.append((CF[:, t8 * 128:(t8 + 1) * 128],
                                              EB2[:, hh * 512:hh * 512 + 512]))
                                pairs.append((ONESRB[:],
                                              SB2R[0:1, hh * 512:hh * 512 + 512]))
                            o0 = mm(P, po, pairs,
                                    waits=[("act", hall_act), ("dve", hall_dve),
                                           (f"ew2s{hh}", l2s[si][hh]),
                                           ("dve", poslot_rd[t8 % 2])])
                            osl = OA[:, t8 * 1024 + hh * 512: t8 * 1024 + hh * 512 + 512]
                            idx = t8 * 2 + hh
                            cb = (None if e_ == "sh"
                                  else CBA[:, t8 * 8 + e_: t8 * 8 + e_ + 1])
                            if not oa_init[idx]:
                                if cb is None:
                                    operv = P.op("dve", lambda v, osl=osl, po=po:
                                                 v.tensor_copy(osl, po),
                                                 waits=[("pe", o0)])
                                else:
                                    operv = P.op("dve", lambda v, osl=osl, po=po,
                                                 cb=cb: v.tensor_scalar(
                                        osl, po, cb, None, ALU.mult),
                                        waits=[("pe", o0)])
                                oa_init[idx] = True
                            else:
                                if cb is None:
                                    operv = P.op("dve", lambda v, osl=osl, po=po:
                                                 v.scalar_tensor_tensor(
                                        osl, po, 1.0, osl, ALU.mult, ALU.add),
                                        waits=[("pe", o0)])
                                else:
                                    operv = P.op("dve", lambda v, osl=osl, po=po,
                                                 cb=cb: v.scalar_tensor_tensor(
                                        osl, po, cb, osl, ALU.mult, ALU.add),
                                        waits=[("pe", o0)])
                            poslot_rd[t8 % 2] = operv
                        if hh == 0:
                            ew2_rd[0] = P.count("pe")
                            if si + 1 < len(steps):
                                l2s[si + 1][0] = wload2h(
                                    steps[si + 1][0], steps[si + 1][1], 0,
                                    extra=cwait(steps[si + 1][0]))
                    ew2_rd[1] = P.count("pe")
                    if si + 1 < len(steps):
                        l2s[si + 1][1] = wload2h(
                            steps[si + 1][0], steps[si + 1][1], 1,
                            extra=cwait(steps[si + 1][0]))
                    if half == 0 and pend:
                        for _ in range(3 if g < 2 else 2):
                            if pend:
                                gg, wch = pend.pop(0)
                                lastc = _cast_chunk(ctgt, gg, wch)
                        if not pend:
                            cast_done[ctgt] = lastc
            exp_end = (P.count("dve"), P.count("act"))
            prev_pe = P.count("pe")

            xr = P.op("sp", lambda e, half=half: e.dma_start(
                out=X[:].rearrange("p (t d) -> p t d", t=NT)[:, half * 8:(half + 1) * 8, :],
                in_=x_spill.ap().rearrange("(t p) d -> p t d", p=128)[:, half * 8:(half + 1) * 8, :]),
                waits=[("sp", xs)], amt=16)
            spv = P.count("sp")
            for t8 in range(8):
                for hh in range(2):
                    osl = OA[:, t8 * 1024 + hh * 512: t8 * 1024 + hh * 512 + 512]
                    t_ = (ho + t8 * 128) // 128
                    slot = SIN2[:, hh * 512:hh * 512 + 512]
                    sl = P.op("sp", lambda e, t_=t_, hh=hh, slot=slot: e.dma_start(
                        out=slot, in_=s_dram.ap()[t_ * 128:(t_ + 1) * 128,
                                                  hh * 512:hh * 512 + 512]),
                        waits=[("dve", P.count("dve")), ("sp", xr)], amt=16)
                    gsl = P.op("act", lambda s, osl=osl: s.activation(
                        SCRB[:, 0:512], osl, AF.Silu),
                        waits=[("dve", P.count("dve")), ("act", P.count("act"))])
                    xsl = X[:, t_ * D + hh * 512: t_ * D + hh * 512 + 512]
                    z3 = P.op("dve", lambda v, slot=slot: v.tensor_mul(
                        SCRB[:, 512:1024], SCRB[:, 0:512], slot),
                        waits=[("act", gsl), ("sp", sl)])
                    P.op("dve", lambda v, xsl=xsl: v.tensor_add(
                        xsl, xsl, SCRB[:, 512:1024]), waits=[("dve", z3)])
            if half == 1:
                P.barrier()



def emit_l2(P, sb, dr):
    nc = P.nc
    X, XNT = sb["X"], sb["XNT"]
    IDF, IDB = sb["IDF"], sb["IDB"]
    ONESRB, ONESRF = sb["ONESRB"], sb["ONESRF"]
    MPREV, ISF = sb["MPREV"], sb["ISF"]
    SCR1, SCRB, STAT = sb["SCR1"], sb["SCRB"], sb["STAT"]
    E, H = 4, 8
    KSW = 2176

    s_dram = nc.dram_tensor("l2_sdram", [NTOK, D], BF16)
    cin = nc.dram_tensor("l2_cin", [128, 512], BF16)
    cout = nc.dram_tensor("l2_cout", [1024, 512], BF16)
    vb_dram = nc.dram_tensor("l2_vb", [31, 256], BF16)

    with ExitStack() as lo:
        SO = lambda n, sh, dt: lo.enter_context(nc.sbuf_tensor(n, list(sh), dt))
        KS = SO("l2_KS", (64, E * KSW), BF16)
        VSK = SO("l2_VSK", (128, E * 17 * 64), BF16)

        with ExitStack() as la:
            SA = lambda n, sh, dt: la.enter_context(nc.sbuf_tensor(n, list(sh), dt))
            PA_ = lambda n, sh, dt: la.enter_context(nc.psum_tensor(n, list(sh), dt))
            WKV = SA("l2_WKV", (128, DC * 1024), BF16)
            WGT = SA("l2_WGT", (128, DC * 32), BF16)
            BGB = SA("l2_BGB", (128, 32), F32)
            KCH = SA("l2_KCH", (128, 512), BF16)
            VCH = SA("l2_VCH", (128, 512), BF16)
            GCH = SA("l2_GCH", (128, 32), F32)
            GKTM = SA("l2_GKTM", (128, NT * 256), BF16)
            GVTM = SA("l2_GVTM", (128, NT * 256), BF16)
            ATM = SA("l2_ATM", (128, 64), F32)
            ACM = SA("l2_ACM", (4, NTOK), BF16)
            ABC = SA("l2_ABC", (64, NTOK), F32)
            GKC = SA("l2_GKC", (64, 256), F32)
            GVC = SA("l2_GVC", (64, 256), F32)
            VSF = SA("l2_VSF", (64, 256), F32)
            IKM = SA("l2_IKM", (64, 32), F32)
            IVM = SA("l2_IVM", (64, 32), F32)
            IKB = SA("l2_IKB", (64, 124), BF16)
            IVB = SA("l2_IVB", (31, 256), BF16)
            SEL4 = SA("l2_SEL4", (4, 64), BF16)
            IOT4 = SA("l2_IOT4", (4, 64), F32)
            PKY = SA("l2_PKY", (64, 124), BF16)
            XB8 = SA("l2_XB8", (128, 8), BF16)
            KBS = SA("l2_KBS", (64, 124), BF16)
            VBS = SA("l2_VBS", (31, 256), BF16)
            GBL = SA("l2_GBL", (128, 512), BF16)

            PK = PA_("l2_PK", (128, 512), F32)
            PV = PA_("l2_PV", (128, 512), F32)
            PG = PA_("l2_PG", (128, 64), F32)
            PTR = PA_("l2_PTR", (128, 128), F32)
            PTB = PA_("l2_PTB", (128, 128), BF16)

            def wload3(dst, name, c):
                return P.op("pool", lambda e, dst=dst, name=name, c=c: e.dma_start(
                    out=dst[:].rearrange("p (c n) -> p c n", c=c),
                    in_=dr[name].rearrange("(c p) n -> p c n", p=128)), amt=16)
            wload3(WKV, "l2_wkvT", DC)
            wload3(WGT, "l2_wgT", DC)
            P.op("pool", lambda e: e.dma_start(out=BGB[:], in_=dr["l2_bgb"]), amt=16)
            P.op("pool", lambda e: e.dma_start(out=IKM[:], in_=dr["l2_ikm"]), amt=16)
            P.op("pool", lambda e: e.dma_start(out=IVM[:], in_=dr["l2_ivm"]), amt=16)
            P.op("pool", lambda e: e.dma_start(out=IKB[:], in_=dr["l2_ikb"]), amt=16)
            wA = P.op("pool", lambda e: e.dma_start(out=IVB[:], in_=dr["l2_ivb"]),
                      amt=16)
            i4 = P.op("pool", lambda e: e.iota(
                IOT4[:], pattern=[[0, 64]], base=0, channel_multiplier=1,
                allow_small_or_imprecise_dtypes=True))
            zz = 0
            for e_ in range(E):
                P.op("pool", lambda g, e_=e_: g.memset(
                    KS[:, e_ * KSW: e_ * KSW + 128], 0.0))
                zz = P.op("pool", lambda g, e_=e_: g.memset(
                    VSK[:, e_ * 1088: e_ * 1088 + 64], 0.0))
            P.waits("pe", [("pool", wA)])

            gk_d = P.count("dve")
            for c in range(0 if 13 in L2_CUTS else NT):
                xsl = [XNT[:, dc * XSTR + 1 + c * 128: dc * XSTR + 1 + (c + 1) * 128]
                       for dc in range(DC)]
                k0 = mm(P, PK[:], [(xsl[dc], WKV[:, dc * 1024: dc * 1024 + 512])
                                   for dc in range(DC)],
                        waits=[("act", P.count("act"))])
                if L2_CUT == 16:
                    gk_d = P.count("dve")
                    continue
                v0 = mm(P, PV[:], [(xsl[dc], WKV[:, dc * 1024 + 512:(dc + 1) * 1024])
                                   for dc in range(DC)])
                if L2_CUT != 15:
                    g0 = mm(P, PG[:, 0:32], [(xsl[dc], WGT[:, dc * 32:(dc + 1) * 32])
                                             for dc in range(DC)])
                ck = P.op("act", lambda s: s.copy(KCH[:], PK[:]),
                          waits=[("pe", k0), ("dve", gk_d)])
                if L2_CUT == 17:
                    gk_d = P.count("dve")
                    continue
                cv = P.op("act", lambda s: s.copy(VCH[:], PV[:]),
                          waits=[("act", ck), ("pe", v0)])
                if (14 in L2_CUTS) or (15 in L2_CUTS):
                    gk_d = P.count("dve")
                    continue
                gb = P.op("dve", lambda v: v.tensor_add(SCR1[:, 0:32], PG[:, 0:32],
                                                        BGB[:]),
                          waits=[("pe", g0), ("dve", gk_d)])
                gs = P.op("act", lambda s: s.activation(GCH[:], SCR1[:, 0:32],
                                                        AF.Sigmoid),
                          waits=[("dve", gb)])
                if L2_CUT == 10:
                    gk_d = P.count("dve")
                    continue
                aa = 0
                for e_ in range(E):
                    aa = P.op("dve", lambda v, e_=e_, c=c: v.reduce_sum(
                        ATM[:, c * 4 + e_:c * 4 + e_ + 1], GCH[:, e_:32:4], axis=AX),
                        waits=([("act", gs)] if e_ == 0 else []))
                aa = P.op("dve", lambda v, c=c: v.tensor_scalar(
                    ATM[:, c * 4:(c + 1) * 4], ATM[:, c * 4:(c + 1) * 4],
                    1.0, -1.0, ALU.min, ALU.mult), waits=[("dve", aa)])
                aa = P.op("dve", lambda v, c=c: v.tensor_scalar_add(
                    ATM[:, c * 4:(c + 1) * 4], ATM[:, c * 4:(c + 1) * 4], 1.0),
                    waits=[("dve", aa)])
                if L2_CUT == 11:
                    gk_d = P.count("dve")
                    continue
                for e_ in range(E):
                    dsl = GKTM[:, c * 256 + e_ * 64: c * 256 + (e_ + 1) * 64]
                    vsl = GVTM[:, c * 256 + e_ * 64: c * 256 + (e_ + 1) * 64]
                    for h in range(H):
                        kslh = KCH[:, h * 64:(h + 1) * 64]
                        vslh = VCH[:, h * 64:(h + 1) * 64]
                        gcol = GCH[:, h * 4 + e_:h * 4 + e_ + 1]
                        if h == 0:
                            gk_d = P.op("dve", lambda v, dsl=dsl, kslh=kslh,
                                        gcol=gcol: v.tensor_scalar(
                                dsl, kslh, gcol, None, ALU.mult),
                                waits=[("act", gs)] if e_ == 0 else [])
                            gk_d = P.op("dve", lambda v, vsl=vsl, vslh=vslh,
                                        gcol=gcol: v.tensor_scalar(
                                vsl, vslh, gcol, None, ALU.mult))
                        else:
                            gk_d = P.op("dve", lambda v, dsl=dsl, kslh=kslh,
                                        gcol=gcol: v.scalar_tensor_tensor(
                                dsl, kslh, gcol, dsl, ALU.mult, ALU.add),
                                waits=[("dve", gk_d)])
                            gk_d = P.op("dve", lambda v, vsl=vsl, vslh=vslh,
                                        gcol=gcol: v.scalar_tensor_tensor(
                                vsl, vslh, gcol, vsl, ALU.mult, ALU.add),
                                waits=[("dve", gk_d)])
                if L2_CUT == 12:
                    continue
                ta = P.op("pe", lambda t, c=c: t.transpose(
                    PTR[0:4, 0:128], ATM[:, c * 4:(c + 1) * 4], IDF[:]),
                    waits=[("dve", aa), ("act", P.count("act"))])
                P.op("act", lambda s, c=c: s.copy(
                    ACM[:, c * 128:(c + 1) * 128], PTR[0:4, 0:128]),
                    waits=[("pe", ta)])

            if L2_CUT == 5:
                return
            scn = P.count("dve")
            for e_ in range(E):
                se = P.op("pool", lambda g, e_=e_: g.tensor_scalar(
                    SEL4[:], IOT4[:], float(e_), None, ALU.is_equal),
                    waits=[("poolc", i4), ("pe", P.count("pe"))])
                for tt in range(4):
                    ab = mm(P, PK[0:64, 0:512],
                            [(SEL4[:], ACM[:, tt * 512:(tt + 1) * 512])],
                            waits=[("poolc", se), ("act", P.count("act")),
                                   ("dve", P.count("dve"))])
                    P.op("act", lambda s, tt=tt: s.copy(
                        ABC[:, tt * 512:(tt + 1) * 512], PK[0:64, 0:512]),
                        waits=[("pe", ab)])
                abdone = P.count("act")
                if L2_CUT == 6:
                    continue
                for l in range(8):
                    for c2 in range(2):
                        cchunk = 2 * l + c2
                        tk = P.op("pe", lambda t, cchunk=cchunk, e_=e_: t.transpose(
                            PTB[0:64, 0:128],
                            GKTM[:, cchunk * 256 + e_ * 64: cchunk * 256 + (e_ + 1) * 64],
                            IDB[:]),
                            waits=[("dve", P.count("dve")), ("pool", P.count("pool")),
                                   ("act", P.count("act"))])
                        ck_ = P.op("act", lambda s, c2=c2: s.copy(
                            GKC[0:64, c2 * 128:(c2 + 1) * 128], PTB[0:64, 0:128]),
                            waits=[("pe", tk)])
                        tv = P.op("pe", lambda t, cchunk=cchunk, e_=e_: t.transpose(
                            PTB[0:64, 0:128],
                            GVTM[:, cchunk * 256 + e_ * 64: cchunk * 256 + (e_ + 1) * 64],
                            IDB[:]), waits=[("act", ck_)])
                        cv_ = P.op("act", lambda s, c2=c2: s.copy(
                            GVC[0:64, c2 * 128:(c2 + 1) * 128], PTB[0:64, 0:128]),
                            waits=[("pe", tv)])
                    if L2_CUT == 7:
                        continue
                    kdst = KS[0:64, e_ * KSW + 128 + l * 256: e_ * KSW + 128 + (l + 1) * 256]
                    s1 = P.op("dve", lambda v, kdst=kdst, e_=e_, l=l: v.tensor_tensor_scan(
                        kdst, ABC[0:64, l * 256:(l + 1) * 256], GKC[:],
                        IKM[:, e_ * 8 + l:e_ * 8 + l + 1], ALU.mult, ALU.add),
                        waits=[("act", P.count("act")), ("pool", wA)])
                    s2 = P.op("dve", lambda v, e_=e_, l=l: v.tensor_tensor_scan(
                        VSF[:], ABC[0:64, l * 256:(l + 1) * 256], GVC[:],
                        IVM[:, e_ * 8 + l:e_ * 8 + l + 1], ALU.mult, ALU.add),
                        waits=[("dve", s1)])
                    for c2 in range(2):
                        tvs = P.op("pe", lambda t, c2=c2: t.transpose(
                            PTR[0:128, 0:64], VSF[0:64, c2 * 128:(c2 + 1) * 128],
                            IDF[0:64, 0:64]),
                            waits=[("dve", s2), ("act", P.count("act"))])
                        P.op("act", lambda s, e_=e_, l=l, c2=c2: s.copy(
                            VSK[:, e_ * 1088 + (2 * l + 1 + c2) * 64:
                                e_ * 1088 + (2 * l + 2 + c2) * 64],
                            PTR[0:128, 0:64]), waits=[("pe", tvs)])

            if L2_CUT == 4:
                return
            zc = P.op("pool", lambda g: g.memset(GBL[:], 0.0))
            zc2 = P.op("sp", lambda e: e.dma_start(out=cin.ap(), in_=GBL[:]),
                       waits=[("poolc", zc)], amt=16)
            pk0 = 0
            for e_ in range(E):
                pk0 = P.op("dve", lambda v, e_=e_: v.tensor_copy(
                    PKY[:, e_ * 31:(e_ + 1) * 31],
                    KS[0:64, e_ * KSW + 2145: e_ * KSW + 2176]),
                    waits=[("act", P.count("act")), ("dve", P.count("dve"))])
            xb = P.op("dve", lambda v: v.tensor_copy(
                XB8[:].rearrange("p (c n) -> p c n", c=8),
                XNT[:].rearrange("p (c n) -> p c n", c=DC)[:, :, 2048:2049]),
                waits=[("dve", pk0)])
            c0_ = P.op("sp", lambda e: e.dma_start(
                out=cin.ap()[0:64, 0:124], in_=PKY[:]),
                waits=[("dve", pk0), ("sp", zc2)], amt=16)
            c1_ = P.op("sp", lambda e: e.dma_start(
                out=cin.ap()[0:128, 384:392], in_=XB8[:]),
                waits=[("dve", xb)], amt=16)
            cvv = c1_
            for e_ in range(E):
                cvv = P.op("sp", lambda e, e_=e_: e.dma_start(
                    out=cin.ap()[0:31, 128 + e_ * 64:128 + (e_ + 1) * 64],
                    in_=VSK[97:128, e_ * 1088 + 16 * 64: e_ * 1088 + 17 * 64]),
                    waits=[("act", P.count("act"))], amt=16)
            cc2 = P.op("pool", lambda e: e.collective_compute(
                "AllGather", ALU.bypass, replica_groups=[list(range(NCORES))],
                ins=[cin.ap().opt()], outs=[cout.ap().opt()]),
                waits=[("sp", cvv)], sem="cc")
            gprev = 0
            for r in range(NCORES):
                gr = P.op("sp", lambda e, r=r: e.dma_start(
                    out=GBL[:], in_=cout.ap()[r * 128:(r + 1) * 128, :]),
                    waits=([("cc", P.count("cc"))] if r == 0 else
                           [("dve", gprev)]), amt=16)
                if r == 0:
                    P.op("dve", lambda v: v.tensor_scalar(
                        KBS[:], GBL[0:64, 0:124], MPREV[0:64, 0:1], None, ALU.mult),
                        waits=[("sp", gr)])
                    P.op("dve", lambda v: v.tensor_scalar(
                        VBS[:], GBL[0:31, 128:384], MPREV[0:31, 0:1], None, ALU.mult))
                    gprev = P.op("dve", lambda v: v.tensor_scalar(
                        XB8[:], GBL[:, 384:392], MPREV[:, 0:1], None, ALU.mult))
                else:
                    P.op("dve", lambda v, r=r: v.scalar_tensor_tensor(
                        KBS[:], GBL[0:64, 0:124], MPREV[0:64, r:r + 1], KBS[:],
                        ALU.mult, ALU.add), waits=[("sp", gr)])
                    P.op("dve", lambda v, r=r: v.scalar_tensor_tensor(
                        VBS[:], GBL[0:31, 128:384], MPREV[0:31, r:r + 1], VBS[:],
                        ALU.mult, ALU.add))
                    gprev = P.op("dve", lambda v, r=r: v.scalar_tensor_tensor(
                        XB8[:], GBL[:, 384:392], MPREV[:, r:r + 1], XB8[:],
                        ALU.mult, ALU.add))
            gprev = P.op("dve", lambda v: v.scalar_tensor_tensor(
                KBS[:], IKB[:], ISF[0:64, 0:1], KBS[:], ALU.mult, ALU.add),
                waits=[("dve", gprev), ("pool", wA)])
            gprev = P.op("dve", lambda v: v.scalar_tensor_tensor(
                VBS[:], IVB[:], ISF[0:31, 0:1], VBS[:], ALU.mult, ALU.add),
                waits=[("dve", gprev)])
            for e_ in range(E):
                gprev = P.op("dve", lambda v, e_=e_: v.tensor_copy(
                    KS[0:64, e_ * KSW + 97: e_ * KSW + 128],
                    KBS[:, e_ * 31:(e_ + 1) * 31]), waits=[("dve", gprev)])
            gprev = P.op("dve", lambda v: v.tensor_copy(
                XNT[:].rearrange("p (c n) -> p c n", c=DC)[:, :, 0:1],
                XB8[:].rearrange("p (c n) -> p c n", c=8)), waits=[("dve", gprev)])
            vb0 = P.op("sp", lambda e: e.dma_start(out=vb_dram.ap(), in_=VBS[:]),
                       waits=[("dve", gprev)], amt=16)
            for e_ in range(E):
                vb0 = P.op("sp", lambda e, e_=e_: e.dma_start(
                    out=VSK[97:128, e_ * 1088: e_ * 1088 + 64],
                    in_=vb_dram.ap()[:, e_ * 64:(e_ + 1) * 64]),
                    waits=([("sp", vb0), ("poolc", zz)] if e_ == 0 else []), amt=16)
            if os.environ.get("L2DUMP"):
                dks = nc.dram_tensor("dbg_ks", [64, E * KSW], BF16,
                                     kind="ExternalOutput").ap()
                dvs = nc.dram_tensor("dbg_vsk", [128, E * 17 * 64], BF16,
                                     kind="ExternalOutput").ap()
                dgk = nc.dram_tensor("dbg_gk", [128, NT * 256], BF16,
                                     kind="ExternalOutput").ap()
                dac = nc.dram_tensor("dbg_acm", [4, NTOK], BF16,
                                     kind="ExternalOutput").ap()
                dd = P.op("sp", lambda e: e.dma_start(out=dks, in_=KS[:]),
                          waits=[("dve", P.count("dve")), ("act", P.count("act"))],
                          amt=16)
                dd = P.op("sp", lambda e: e.dma_start(out=dvs, in_=VSK[:]), amt=16)
                dd = P.op("sp", lambda e: e.dma_start(out=dgk, in_=GKTM[:]), amt=16)
                dd = P.op("sp", lambda e: e.dma_start(out=dac, in_=ACM[:]), amt=16)
                P.waits("sp", [("sp", dd)])
            P.waits("pe", [("sp", vb0)])
        P.barrier()

        if L2_CUT == 1:
            return
        with ExitStack() as lb:
            SB_ = lambda n, sh, dt: lb.enter_context(nc.sbuf_tensor(n, list(sh), dt))
            PB_ = lambda n, sh, dt: lb.enter_context(nc.psum_tensor(n, list(sh), dt))
            QF = SB_("l2_QF", (64, H * NTOK), BF16)
            WQTS = SB_("l2_WQTS", (128, DC * 512), BF16)
            MSKB = SB_("l2_MSKB", (128, 256), F32)
            SM = SB_("l2_SM", (128, 1024), F32)
            PSCL = SB_("l2_PSCL", (128, 1024), BF16)
            PTS = SB_("l2_PTS", (128, 256), BF16)
            YF2 = SB_("l2_YF2", (128, 4 * NTOK), BF16)
            WO = SB_("l2_WO", (128, 4 * 1024), BF16)
            SOUT = SB_("l2_SOUT", (128, 1024), BF16)

            PS1 = PB_("l2_PS1", (128, 512), F32)
            PS2 = PB_("l2_PS2", (128, 512), F32)
            PTA = PB_("l2_PTA", (128, 256), BF16)
            POU = PB_("l2_POU", (64, 128), F32)
            PQ = PB_("l2_PQ", (64, 512), F32)
            PST = PB_("l2_PST", (128, 512), F32)

            wq = P.op("pool", lambda e: e.dma_start(
                out=WQTS[:].rearrange("p (c n) -> p c n", c=DC),
                in_=dr["l2_wqT"].rearrange("(c p) n -> p c n", p=128)), amt=16)
            wB = P.op("pool", lambda e: e.dma_start(
                out=WO[:].rearrange("p (c n) -> p c n", c=4),
                in_=dr["l2_wo"].rearrange("(c p) n -> p c n", p=128)), amt=16)
            z0 = P.op("pool", lambda e: e.memset(SCR1[:, 0:256], 0.0))
            z1 = P.op("pool", lambda e: e.affine_select(
                MSKB[:], SCR1[:, 0:256], pattern=[[1, 256]], compare_op=ALU.is_ge,
                fill=-1e30, base=-97, channel_multiplier=-1), waits=[("poolc", z0)])
            z2 = P.op("pool", lambda e: e.affine_select(
                MSKB[:], MSKB[:], pattern=[[-1, 256]], compare_op=ALU.is_ge,
                fill=-1e30, base=128, channel_multiplier=1), waits=[("poolc", z1)])

            for h in range(H):
                for tt in range(4):
                    q0 = mm(P, PQ[:, 0:512],
                            [(WQTS[:, dc * 512 + h * 64: dc * 512 + (h + 1) * 64],
                              XNT[:, dc * XSTR + 1 + tt * 512: dc * XSTR + 1 + tt * 512 + 512])
                             for dc in range(DC)],
                            waits=[("act", P.count("act")), ("pool", wq)])
                    P.op("act", lambda s, h=h, tt=tt: s.activation(
                        QF[0:64, h * NTOK + tt * 512: h * NTOK + tt * 512 + 512],
                        PQ[0:64, 0:512], AF.Copy, scale=0.125), waits=[("pe", q0)])

            for c in range(NT):
                for h in range(H):
                    lq = QF[0:64, h * NTOK + c * 128: h * NTOK + (c + 1) * 128]
                    smw = P.count("dve")
                    ps = []
                    for e_ in range(E):
                        bank = PS1 if e_ < 2 else PS2
                        out = bank[:, (e_ % 2) * 256:(e_ % 2) * 256 + 256]
                        r0 = mm(P, out, [(lq, KS[0:64, e_ * KSW + 128 * c:
                                                  e_ * KSW + 128 * c + 256])],
                                waits=[("dve", smw)] if e_ == 0 else [])
                        ps.append((out, r0))
                    sm0 = 0
                    for e_ in range(E):
                        sm0 = P.op("dve", lambda v, e_=e_, o=ps[e_][0]: v.tensor_add(
                            SM[:, e_ * 256:(e_ + 1) * 256], o, MSKB[:]),
                            waits=[("pe", ps[e_][1]), ("poolc", z2)])
                    mx = P.op("dve", lambda v: v.reduce_max(STAT[:, 0:1], SM[:],
                                                            axis=AX),
                              waits=[("dve", sm0)])
                    ng = P.op("dve", lambda v: v.tensor_scalar_mul(
                        STAT[:, 1:2], STAT[:, 0:1], -1.0), waits=[("dve", mx)])
                    ex = P.op("act", lambda s: s.activation(
                        PSCL[:], SM[:], AF.Exp, bias=STAT[:, 1:2],
                        accum_out=STAT[:, 2:3]), waits=[("dve", ng)])
                    rc = P.op("dve", lambda v: v.reciprocal(STAT[:, 3:4],
                                                            STAT[:, 2:3]),
                              waits=[("act", ex)])
                    sc = P.op("dve", lambda v: v.tensor_scalar(
                        PSCL[:], PSCL[:], STAT[:, 3:4], None, ALU.mult),
                        waits=[("dve", rc)])
                    pvlast = None
                    for e_ in range(E):
                        t0 = P.op("pe", lambda t, e_=e_: t.transpose(
                            PTA[:, 0:128], PSCL[:, e_ * 256: e_ * 256 + 128],
                            IDB[:]),
                            waits=[("dve", sc)] if e_ == 0 else
                                  [("act", P.count("act"))])
                        t1 = P.op("pe", lambda t, e_=e_: t.transpose(
                            PTA[:, 128:256], PSCL[:, e_ * 256 + 128:(e_ + 1) * 256],
                            IDB[:]))
                        a0 = P.op("act", lambda s: s.copy(PTS[:, 0:128],
                                                          PTA[:, 0:128]),
                                  waits=[("pe", t1)])
                        a1 = P.op("act", lambda s: s.copy(PTS[:, 128:256],
                                                          PTA[:, 128:256]),
                                  waits=[("act", a0)])
                        for c2 in range(2):
                            st_ = (e_ == 0 and c2 == 0)
                            sp_ = (e_ == E - 1 and c2 == 1)
                            pvlast = P.op("pe", lambda t, e_=e_, c2=c2, st_=st_,
                                          sp_=sp_, c=c: t.matmul(
                                POU[:],
                                VSK[:, e_ * 1088 + (c + c2) * 64:
                                    e_ * 1088 + (c + c2 + 1) * 64],
                                PTS[:, c2 * 128:(c2 + 1) * 128],
                                start=st_, stop=sp_, skip_group_check=True),
                                waits=[("act", a1)] if c2 == 0 else (),
                                noinc=not sp_)
                    P.op("act", lambda s, h=h, c=c: s.copy(
                        YF2[(h % 2) * 64:(h % 2) * 64 + 64,
                            (h // 2) * NTOK + c * 128:(h // 2) * NTOK + (c + 1) * 128],
                        POU[:]), waits=[("pe", pvlast)])

            if os.environ.get("L2DUMP"):
                dqf = nc.dram_tensor("dbg_qf", [64, H * NTOK], BF16,
                                     kind="ExternalOutput").ap()
                dyf = nc.dram_tensor("dbg_yf", [128, 4 * NTOK], BF16,
                                     kind="ExternalOutput").ap()
                dd2 = P.op("sp", lambda e: e.dma_start(out=dqf, in_=QF[:]),
                           waits=[("act", P.count("act")), ("pe", P.count("pe"))],
                           amt=16)
                dd2 = P.op("sp", lambda e: e.dma_start(out=dyf, in_=YF2[:]), amt=16)
                P.waits("sp", [("sp", dd2)])
            sv = P.count("dve")
            spv = P.count("sp")
            for t in range(NT):
                for hh in range(2):
                    w0 = mm(P, PST[:],
                            [(YF2[:, p * NTOK + t * 128: p * NTOK + (t + 1) * 128],
                              WO[:, p * 1024 + hh * 512: p * 1024 + hh * 512 + 512])
                             for p in range(4)],
                            waits=[("pool", wB), ("dve", sv),
                                   ("act", P.count("act"))])
                    slot = SOUT[:, (t % 2) * 512:(t % 2) * 512 + 512]
                    sv = P.op("dve", lambda v, slot=slot: v.tensor_copy(slot, PST[:]),
                              waits=[("pe", w0), ("sp", spv)])
                    spv = P.op("sp", lambda e, t=t, hh=hh, slot=slot: e.dma_start(
                        out=s_dram.ap()[t * 128:(t + 1) * 128,
                                        hh * 512:hh * 512 + 512],
                        in_=slot), waits=[("dve", sv)], amt=16)
        P.barrier()

    if (0 in L2_CUTS) or (3 in L2_CUTS):
        emit_l2_ffn(P, sb, dr, s_dram)


def emit_l2_ffn(P, sb, dr, s_dram):
    nc = P.nc
    X, XNT = sb["X"], sb["XNT"]
    SCR1, SCRB = sb["SCR1"], sb["SCRB"]

    with ExitStack() as lc:
        SC = lambda n, sh, dt: lc.enter_context(nc.sbuf_tensor(n, list(sh), dt))
        PC_ = lambda n, sh, dt: lc.enter_context(nc.psum_tensor(n, list(sh), dt))
        KACT = SC("l2_KACT", (128, DC * NTOK), BF16)
        TMPB = SC("l2_TMPB", (128, NTOK), BF16)
        MUR = SC("l2_MUR", (128, 8), F32)
        MUK = SC("l2_MUK", (128, 8), F32)
        OMUR = SC("l2_OMUR", (128, 8), F32)
        OMUK = SC("l2_OMUK", (128, 8), F32)
        WJ = SC("l2_WJ", (128, 2 * 1024), BF16)
        WRH = SC("l2_WRH", (128, DC * 512), BF16)
        WVH = SC("l2_WVH", (128, DC * 512), BF16)
        RT = SC("l2_RT", (128, 512), BF16)
        SIN = SC("l2_SIN", (128, 1024), BF16)

        PC4 = PC_("l2_PC4", (128, 2048), F32)
        PSr = PC_("l2_PSr", (128, 512), F32)
        PSv = PC_("l2_PSv", (128, 512), F32)

        mu0 = P.op("pool", lambda e: e.dma_start(out=MUR[:], in_=dr["l2_mur"]),
                   amt=16)
        mu1 = P.op("pool", lambda e: e.dma_start(out=MUK[:], in_=dr["l2_muk"]),
                   amt=16)
        o0 = P.op("dve", lambda v: v.tensor_scalar(
            OMUR[:], MUR[:], -1.0, 1.0, ALU.mult, ALU.add), waits=[("pool", mu1)])
        o1 = P.op("dve", lambda v: v.tensor_scalar(
            OMUK[:], MUK[:], -1.0, 1.0, ALU.mult, ALU.add), waits=[("dve", o0)])

        lsk = ExitStack()
        XKT = lsk.enter_context(nc.sbuf_tensor("l2_XKT", [128, DC * NTOK], BF16))
        mprev = o1
        for dc in range(DC):
            prev = XNT[:, dc * XSTR: dc * XSTR + NTOK]
            cur = XNT[:, dc * XSTR + 1: dc * XSTR + 1 + NTOK]
            m2 = P.op("dve", lambda v, prev=prev, dc=dc: v.tensor_scalar(
                TMPB[:], prev, OMUK[:, dc:dc + 1], None, ALU.mult),
                waits=[("dve", mprev)])
            mprev = P.op("dve", lambda v, cur=cur, dc=dc: v.scalar_tensor_tensor(
                XKT[:, dc * NTOK:(dc + 1) * NTOK], cur, MUK[:, dc:dc + 1], TMPB[:],
                ALU.mult, ALU.add), waits=[("dve", m2)])

        kv = P.count("dve")
        for j in range(DC):
            slot = WJ[:, (j % 2) * 1024:(j % 2) * 1024 + 1024]
            wj = P.op("pool", lambda e, j=j, slot=slot: e.dma_start(
                out=slot.rearrange("p (c n) -> p c n", c=DC),
                in_=dr["l2_wk2T"][:, j * 128:(j + 1) * 128].rearrange(
                    "(c p) n -> p c n", p=128)),
                waits=[("pe", P.count("pe"))], amt=16)
            wlast = 0
            for tt in range(4):
                pairs = [(slot[:, dc * 128:(dc + 1) * 128],
                          XKT[:, dc * NTOK + tt * 512: dc * NTOK + tt * 512 + 512])
                         for dc in range(DC)]
                wlast = mm(P, PC4[:, tt * 512:(tt + 1) * 512], pairs,
                           waits=[("pool", wj), ("dve", mprev),
                                  ("act", P.count("act")), ("dve", kv)])
            for tt in range(4):
                r0 = P.op("act", lambda s, tt=tt: s.activation(
                    SCRB[:, 0:512], PC4[:, tt * 512:(tt + 1) * 512], AF.Relu),
                    waits=([("pe", wlast)] if tt == 0 else []) +
                          [("dve", P.count("dve"))])
                kv = P.op("dve", lambda v, j=j, tt=tt: v.tensor_mul(
                    KACT[:, j * NTOK + tt * 512: j * NTOK + tt * 512 + 512],
                    SCRB[:, 0:512], SCRB[:, 0:512]), waits=[("act", r0)])

        lsk.close()
        P.barrier()
        XRT = lc.enter_context(nc.sbuf_tensor("l2_XRT", [128, DC * NTOK], BF16))
        mprev = P.count("dve")
        for dc in range(DC):
            prev = XNT[:, dc * XSTR: dc * XSTR + NTOK]
            cur = XNT[:, dc * XSTR + 1: dc * XSTR + 1 + NTOK]
            m0 = P.op("dve", lambda v, prev=prev, dc=dc: v.tensor_scalar(
                TMPB[:], prev, OMUR[:, dc:dc + 1], None, ALU.mult),
                waits=[("dve", mprev)])
            mprev = P.op("dve", lambda v, cur=cur, dc=dc: v.scalar_tensor_tensor(
                XRT[:, dc * NTOK:(dc + 1) * NTOK], cur, MUR[:, dc:dc + 1], TMPB[:],
                ALU.mult, ALU.add), waits=[("dve", m0)])

        for hh in range(2):
            wr = P.op("pool", lambda e, hh=hh: e.dma_start(
                out=WRH[:].rearrange("p (c n) -> p c n", c=DC),
                in_=dr["l2_wrT"][:, hh * 512:(hh + 1) * 512].rearrange(
                    "(c p) n -> p c n", p=128)),
                waits=[("pe", P.count("pe"))], amt=16)
            wv = P.op("pool", lambda e, hh=hh: e.dma_start(
                out=WVH[:].rearrange("p (c n) -> p c n", c=DC),
                in_=dr["l2_wv2T"][:, hh * 512:(hh + 1) * 512].rearrange(
                    "(c p) n -> p c n", p=128)), amt=16)
            cprev = P.count("dve")
            spv = P.count("sp")
            for t in range(NT):
                r0 = mm(P, PSr[:],
                        [(XRT[:, dc * NTOK + t * 128: dc * NTOK + (t + 1) * 128],
                          WRH[:, dc * 512:(dc + 1) * 512]) for dc in range(DC)],
                        waits=[("pool", wr), ("act", P.count("act")),
                               ("dve", cprev)])
                ra = P.op("act", lambda s: s.activation(RT[:], PSr[:], AF.Sigmoid),
                          waits=[("pe", r0)])
                v0 = mm(P, PSv[:],
                        [(KACT[:, dc * NTOK + t * 128: dc * NTOK + (t + 1) * 128],
                          WVH[:, dc * 512:(dc + 1) * 512]) for dc in range(DC)],
                        waits=[("pool", wv)])
                g0 = P.op("dve", lambda v: v.tensor_mul(SCR1[:, 0:512], PSv[:],
                                                        RT[:]),
                          waits=[("pe", v0), ("act", ra)])
                ga = P.op("act", lambda s: s.activation(SCRB[:, 0:512],
                                                        SCR1[:, 0:512], AF.Silu),
                          waits=[("dve", g0)])
                sl = P.op("sp", lambda e, t=t, hh=hh: e.dma_start(
                    out=SIN[:, (t % 2) * 512:(t % 2) * 512 + 512],
                    in_=s_dram.ap()[t * 128:(t + 1) * 128, hh * 512:hh * 512 + 512]),
                    waits=[("dve", cprev)] if t < 2 else [("dve", cprev)], amt=16)
                _dbg = os.environ.get("L2DBG", "")
                if _dbg == "g":
                    d0 = P.op("dve", lambda v, t=t: v.tensor_copy(
                        SCRB[:, 512:1024], SCRB[:, 0:512]),
                        waits=[("act", ga), ("sp", sl)])
                elif _dbg == "s":
                    d0 = P.op("dve", lambda v, t=t: v.tensor_copy(
                        SCRB[:, 512:1024], SIN[:, (t % 2) * 512:(t % 2) * 512 + 512]),
                        waits=[("act", ga), ("sp", sl)])
                else:
                    d0 = P.op("dve", lambda v, t=t: v.tensor_mul(
                        SCRB[:, 512:1024], SCRB[:, 0:512],
                        SIN[:, (t % 2) * 512:(t % 2) * 512 + 512]),
                        waits=[("act", ga), ("sp", sl)])
                xsl = X[:, t * D + hh * 512: t * D + hh * 512 + 512]
                cprev = P.op("dve", lambda v, xsl=xsl: v.tensor_add(
                    xsl, xsl, SCRB[:, 512:1024]), waits=[("dve", d0)])



def emit_l3(P, sb, dr):
    nc = P.nc
    X, XNT = sb["X"], sb["XNT"]
    IDB, IDF = sb["IDB"], sb["IDF"]
    ONESRB = sb["ONESRB"]
    QOFF = sb["QOFF"]
    SCR1, SCRB = sb["SCR1"], sb["SCRB"]
    SCALE = 512.0 ** -0.5
    T0MAX = 3 * NTOK
    GROUPS = [[0, 1, 2, 3], [4, 5, 6, 7]]

    s_dram = nc.dram_tensor("l3_sdram", [NTOK, D], BF16)
    cin3 = nc.dram_tensor("l3_cin", [128, 16384], BF16)
    cout3 = nc.dram_tensor("l3_cout", [512, 16384], BF16)

    with ExitStack() as lo:
        SO = lambda n, sh, dt: lo.enter_context(nc.sbuf_tensor(n, list(sh), dt))
        QF3 = SO("l3_QF3", (128, 4 * NTOK), BF16)

        with ExitStack() as la:
            SA = lambda n, sh, dt: la.enter_context(nc.sbuf_tensor(n, list(sh), dt))
            WQKV = SA("l3_WQKV", (128, DC * 1536), BF16)
            KF3 = SA("l3_KF3", (128, 4 * NTOK), BF16)
            VTM3 = SA("l3_VTM3", (128, NT * 512), BF16)
            PQ3 = la.enter_context(nc.psum_tensor("l3_PQ3", [128, 512], F32))
            PK3 = la.enter_context(nc.psum_tensor("l3_PK3", [128, 512], F32))

            wq = P.op("pool", lambda e: e.dma_start(
                out=WQKV[:].rearrange("p (c n) -> p c n", c=DC),
                in_=dr["l3_wqkvT"].rearrange("(c p) n -> p c n", p=128)), amt=16)
            for ci in range(4):
                for tt in range(4):
                    q0 = mm(P, PQ3[:],
                            [(WQKV[:, dc * 1536 + ci * 128: dc * 1536 + (ci + 1) * 128],
                              XNT[:, dc * XSTR + 1 + tt * 512: dc * XSTR + 1 + tt * 512 + 512])
                             for dc in range(DC)],
                            waits=[("pool", wq), ("act", P.count("act"))])
                    P.op("act", lambda s, ci=ci, tt=tt: s.activation(
                        QF3[:, ci * NTOK + tt * 512: ci * NTOK + tt * 512 + 512],
                        PQ3[:], AF.Copy, scale=SCALE), waits=[("pe", q0)])
                    k0 = mm(P, PK3[:],
                            [(WQKV[:, dc * 1536 + 512 + ci * 128: dc * 1536 + 512 + (ci + 1) * 128],
                              XNT[:, dc * XSTR + 1 + tt * 512: dc * XSTR + 1 + tt * 512 + 512])
                             for dc in range(DC)],
                            waits=[("act", P.count("act"))])
                    P.op("act", lambda s, ci=ci, tt=tt: s.copy(
                        KF3[:, ci * NTOK + tt * 512: ci * NTOK + tt * 512 + 512],
                        PK3[:]), waits=[("pe", k0)])
            for t in range(NT):
                v0 = mm(P, PQ3[:],
                        [(XNT[:, dc * XSTR + 1 + t * 128: dc * XSTR + 1 + (t + 1) * 128],
                          WQKV[:, dc * 1536 + 1024: dc * 1536 + 1536])
                         for dc in range(DC)],
                        waits=[("act", P.count("act"))])
                P.op("act", lambda s, t=t: s.copy(
                    VTM3[:, t * 512:(t + 1) * 512], PQ3[:]), waits=[("pe", v0)])
            kd = P.op("sp", lambda e: e.dma_start(
                out=cin3.ap()[:, 0:4 * NTOK], in_=KF3[:]),
                waits=[("act", P.count("act"))], amt=16)
            kd = P.op("sp", lambda e: e.dma_start(
                out=cin3.ap()[:, 4 * NTOK:8 * NTOK], in_=VTM3[:]), amt=16)
            cc3 = P.op("pool", lambda e: e.collective_compute(
                "AllGather", ALU.bypass, replica_groups=GROUPS,
                ins=[cin3.ap().opt()], outs=[cout3.ap().opt()]),
                waits=[("sp", kd)], sem="cc")
            P.waits("sp", [("cc", P.count("cc"))])
        P.barrier()

        with ExitStack() as lb:
            SB_ = lambda n, sh, dt: lb.enter_context(nc.sbuf_tensor(n, list(sh), dt))
            KBUF = SB_("l3_KBUF", (128, 4 * 2048), BF16)
            VBUF = SB_("l3_VBUF", (128, 4 * 2048), BF16)
            OACC = SB_("l3_OACC", (128, 2 * 512), F32)
            SM3 = SB_("l3_SM3", (128, 2 * 512), F32)
            MG = SB_("l3_MG", (128, 2 * 512), F32)
            CIOT = SB_("l3_CIOT", (128, 512), F32)
            PSCL3 = SB_("l3_PSCL3", (128, 6 * 512), BF16)
            PTS3 = SB_("l3_PTS3", (128, 2 * 512), BF16)
            FST = SB_("l3_FST", (128, 32), F32)
            FSTB = SB_("l3_FSTB", (128, 2), F32)
            OB = SB_("l3_OB", (128, 512), BF16)
            OUTF = SB_("l3_OUTF", (128, 512), BF16)
            WT3 = SB_("l3_WT3", (128, 4 * 1024), BF16)
            BOUT3 = SB_("l3_BOUT3", (1, 1024), BF16)
            SOUT = SB_("l3_SOUT", (128, 1024), BF16)

            PS3 = lb.enter_context(nc.psum_tensor("l3_PS3", [128, 2 * 512], F32))
            PT3 = lb.enter_context(nc.psum_tensor("l3_PT3", [128, 2 * 512], BF16))
            PV3 = lb.enter_context(nc.psum_tensor("l3_PV3", [128, 2 * 512], F32))
            PSO = lb.enter_context(nc.psum_tensor("l3_PSO", [128, 512], F32))

            wt = P.op("pool", lambda e: e.dma_start(
                out=WT3[:].rearrange("p (c n) -> p c n", c=4),
                in_=dr["l3_woutT"].rearrange("(c p) n -> p c n", p=128)), amt=16)
            wt2 = P.op("pool", lambda e: e.dma_start(out=BOUT3[:], in_=dr["l3_bout"]),
                       amt=16)
            ci0 = P.op("pool", lambda e: e.iota(
                CIOT[:], pattern=[[1, 512]], base=0, channel_multiplier=-1,
                allow_small_or_imprecise_dtypes=True))

            castq = []
            for e_ in range(18):
                for dst, srcs in (("l3_wb3g", ("l3_gatef", "l3_shgf")),
                                  ("l3_wb3u", ("l3_upf", "l3_shuf")),
                                  ("l3_wb3d", ("l3_downf", "l3_shdf"))):
                    csrc = (dr[srcs[0]][e_] if e_ < 16 else dr[srcs[1]][e_ - 16])
                    castq.append((dr[dst][e_], csrc))

            spv = P.count("sp")
            kb_rd = [0, 0, 0, 0]
            ps_rd = [0, 0]
            pv_rd = [0, 0]
            pscl_rd = [[0, 0, 0], [0, 0, 0]]
            pts_rd = [0, 0]
            pt3_rd = [0, 0]
            exprev = [0, 0]
            kvld = {}
            exv = {}

            def kload(sb_):
                kslot = sb_ % 4
                kb = KBUF[:, kslot * 2048:(kslot + 1) * 2048]
                vb = VBUF[:, kslot * 2048:(kslot + 1) * 2048]
                r = sb_ // 4
                col = (sb_ % 4) * 512
                P.op("sp", lambda e, r=r, col=col, kb=kb: e.dma_start(
                    out=kb.rearrange("p (c n) -> p c n", c=4),
                    in_=cout3.ap()[r * 128:(r + 1) * 128, 0:4 * NTOK].rearrange(
                        "p (c n) -> p c n", c=4)[:, :, col:col + 512]),
                    waits=[("pe", kb_rd[kslot])], amt=16, sem=f"kv{kslot}")
                kvld[sb_] = P.op("sp", lambda e, r=r, col=col, vb=vb:
                                 e.dma_start(
                    out=vb,
                    in_=cout3.ap()[r * 128:(r + 1) * 128,
                                   4 * NTOK + (col // 128) * 512:
                                   4 * NTOK + (col // 128) * 512 + 2048]),
                    amt=16, sem=f"kv{kslot}")

            def fstep_a(s, i, sb_):
                fs = s * 16
                acol = fs + 8 + (sb_ % 3)
                kslot = sb_ % 4
                kb = KBUF[:, kslot * 2048:(kslot + 1) * 2048]
                sm3 = SM3[:, s * 512:(s + 1) * 512]
                mgs = MG[:, s * 512:(s + 1) * 512]
                ps3 = PS3[:, s * 512:(s + 1) * 512]
                s0 = mm(P, ps3,
                        [(QF3[:, ci * NTOK + i * 128: ci * NTOK + (i + 1) * 128],
                          kb[:, ci * 512:(ci + 1) * 512]) for ci in range(4)],
                        waits=[(f"kv{kslot}", kvld[sb_]),
                               ("dve", ps_rd[s])])
                th = P.op("dve", lambda v, i=i, sb_=sb_: v.tensor_scalar(
                    FSTB[:, s:s + 1], QOFF[:, i:i + 1], float(-512.0 * sb_), None,
                    ALU.add), waits=[("poolc", ci0)])
                mg = P.op("dve", lambda v: v.tensor_scalar(
                    mgs, CIOT[:], FSTB[:, s:s + 1], None, ALU.is_gt),
                    waits=[("dve", th)])
                sm = P.op("dve", lambda v, ps3=ps3: v.scalar_tensor_tensor(
                    sm3, mgs, -1e30, ps3, ALU.mult, ALU.add),
                    waits=[("dve", mg), ("pe", s0), ("act", exprev[s])])
                ps_rd[s] = sm
                m0 = P.op("dve", lambda v: v.reduce_max(FST[:, fs + 2:fs + 3], sm3,
                                                        axis=AX),
                          waits=[("dve", sm)])
                if sb_ == 0:
                    mn = P.op("dve", lambda v: v.tensor_copy(
                        FST[:, fs + 3:fs + 4], FST[:, fs + 2:fs + 3]),
                        waits=[("dve", m0)])
                else:
                    mn = P.op("dve", lambda v: v.tensor_max(
                        FST[:, fs + 3:fs + 4], FST[:, fs + 0:fs + 1],
                        FST[:, fs + 2:fs + 3]),
                        waits=[("dve", m0)])
                ng = P.op("dve", lambda v: v.tensor_scalar_mul(
                    FST[:, fs + 5:fs + 6], FST[:, fs + 3:fs + 4], -1.0),
                    waits=[("dve", mn), ("act", exprev[s])])
                pscl = PSCL3[:, (3 * s + sb_ % 3) * 512:
                             (3 * s + sb_ % 3) * 512 + 512]
                ex = P.op("act", lambda s_, pscl=pscl: s_.activation(
                    pscl, sm3, AF.Exp, bias=FST[:, fs + 5:fs + 6],
                    accum_out=FST[:, fs + 6:fs + 7]),
                    waits=[("dve", ng), ("pe", pscl_rd[s][sb_ % 3])])
                exprev[s] = ex
                if sb_ == 0:
                    dn = P.op("dve", lambda v: v.tensor_copy(
                        FST[:, fs + 1:fs + 2], FST[:, fs + 6:fs + 7]),
                        waits=[("act", ex)])
                else:
                    al0 = P.op("dve", lambda v, acol=acol: v.tensor_sub(
                        FST[:, acol:acol + 1], FST[:, fs + 0:fs + 1],
                        FST[:, fs + 3:fs + 4]),
                        waits=[("act", ex)])
                    al1 = P.op("act", lambda s_, acol=acol: s_.activation(
                        FST[:, acol:acol + 1], FST[:, acol:acol + 1], AF.Exp),
                        waits=[("dve", al0)])
                    dn = P.op("dve", lambda v, acol=acol: v.scalar_tensor_tensor(
                        FST[:, fs + 1:fs + 2], FST[:, fs + 1:fs + 2],
                        FST[:, acol:acol + 1], FST[:, fs + 6:fs + 7],
                        ALU.mult, ALU.add), waits=[("act", al1)])
                P.op("dve", lambda v: v.tensor_copy(FST[:, fs + 0:fs + 1],
                                                    FST[:, fs + 3:fs + 4]),
                     waits=[("dve", dn)])
                exv[(s, sb_)] = ex

            def fstep_b(s, i, sb_):
                fs = s * 16
                acol = fs + 8 + (sb_ % 3)
                kslot = sb_ % 4
                vb = VBUF[:, kslot * 2048:(kslot + 1) * 2048]
                pscl = PSCL3[:, (3 * s + sb_ % 3) * 512:
                             (3 * s + sb_ % 3) * 512 + 512]
                ex = exv.pop((s, sb_))
                pts = PTS3[:, s * 512:(s + 1) * 512]
                for j in range(4):
                    t0 = P.op("pe", lambda t, j=j, pscl=pscl: t.transpose(
                        PT3[:, s * 512 + j * 128: s * 512 + (j + 1) * 128],
                        pscl[:, j * 128:(j + 1) * 128], IDB[:]),
                        waits=([("act", ex), ("act", pt3_rd[s])] if j == 0
                               else []),
                        noinc=(j < 3))
                pscl_rd[s][sb_ % 3] = t0
                a0 = P.op("act", lambda s_, pts=pts: s_.copy(
                    pts, PT3[:, s * 512:(s + 1) * 512]),
                    waits=[("pe", t0), ("pe", pts_rd[s])])
                pt3_rd[s] = a0
                pv3 = PV3[:, s * 512:(s + 1) * 512]
                pvl = None
                for j in range(4):
                    pvl = P.op("pe", lambda t, j=j, vb=vb, pts=pts, pv3=pv3:
                               t.matmul(
                        pv3, pts[:, j * 128:(j + 1) * 128],
                        vb[:, j * 512:(j + 1) * 512],
                        start=(j == 0), stop=(j == 3)),
                        waits=([("act", a0), ("dve", pv_rd[s])]
                               if j == 0 else ()),
                        noinc=(j < 3))
                pts_rd[s] = pvl
                kb_rd[sb_ % 4] = pvl
                oac = OACC[:, s * 512:(s + 1) * 512]
                if sb_ == 0:
                    ou = P.op("dve", lambda v, pv3=pv3, oac=oac: v.tensor_copy(
                        oac, pv3), waits=[("pe", pvl)])
                else:
                    ou = P.op("dve", lambda v, pv3=pv3, oac=oac, acol=acol:
                              v.scalar_tensor_tensor(
                        oac, oac, FST[:, acol:acol + 1], pv3,
                        ALU.mult, ALU.add), waits=[("pe", pvl)])
                pv_rd[s] = ou

            def fepilogue(s, i):
                nonlocal spv
                fs = s * 16
                oac = OACC[:, s * 512:(s + 1) * 512]
                rc = P.op("dve", lambda v: v.reciprocal(FST[:, fs + 7:fs + 8],
                                                        FST[:, fs + 1:fs + 2]),
                          waits=[("dve", P.count("dve"))])
                ob = P.op("dve", lambda v, oac=oac: v.tensor_scalar(
                    OB[:], oac, FST[:, fs + 7:fs + 8], None, ALU.mult),
                    waits=[("dve", rc)])
                for j in range(4):
                    tq = P.op("pe", lambda t, j=j: t.transpose(
                        PT3[:, s * 512 + j * 128: s * 512 + (j + 1) * 128],
                        OB[:, j * 128:(j + 1) * 128], IDB[:]),
                        waits=([("dve", ob), ("act", P.count("act"))] if j == 0
                               else []))
                oc = P.op("act", lambda s_: s_.copy(
                    OUTF[:], PT3[:, s * 512:(s + 1) * 512]),
                    waits=[("pe", P.count("pe"))])
                pt3_rd[s] = oc
                for hh in range(2):
                    w0 = mm(P, PSO[:],
                            [(OUTF[:, ci * 128:(ci + 1) * 128],
                              WT3[:, ci * 1024 + hh * 512: ci * 1024 + hh * 512 + 512])
                             for ci in range(4)] +
                            [(ONESRB[:], BOUT3[0:1, hh * 512:hh * 512 + 512])],
                            waits=[("pool", wt2), ("act", oc),
                                   ("dve", P.count("dve"))])
                    slot = SOUT[:, hh * 512:hh * 512 + 512]
                    sv = P.op("dve", lambda v, slot=slot: v.tensor_copy(
                        slot, PSO[:]), waits=[("pe", w0), ("sp", spv)])
                    spv = P.op("sp", lambda e, i=i, hh=hh, slot=slot: e.dma_start(
                        out=s_dram.ap()[i * 128:(i + 1) * 128,
                                        hh * 512:hh * 512 + 512],
                        in_=slot), waits=[("dve", sv)], amt=16)

            for ia in range(0, NT, 2):
                ib = ia + 1
                first = True
                for _ in range(7):
                    if castq:
                        cdst, csrc = castq.pop(0)
                        P.op("pool", lambda e, d=cdst, src=csrc: e.dma_start(
                            out=d, in_=src),
                            waits=([("dve", P.count("dve"))] if first else []),
                            amt=16, sem="cast3")
                        first = False
                na = (T0MAX + 128 * (ia + 1) + 511) // 512
                nb = (T0MAX + 128 * (ib + 1) + 511) // 512
                for k in range(max(na, nb)):
                    kload(k)
                    if k < na:
                        fstep_a(0, ia, k)
                    if k < nb:
                        fstep_a(1, ib, k)
                    if k >= 2 and k - 2 < na:
                        fstep_b(0, ia, k - 2)
                    if k >= 2 and k - 2 < nb:
                        fstep_b(1, ib, k - 2)
                for kk in range(max(0, max(na, nb) - 2), max(na, nb)):
                    if (0, kk) in exv:
                        fstep_b(0, ia, kk)
                    if (1, kk) in exv:
                        fstep_b(1, ib, kk)
                fepilogue(0, ia)
                fepilogue(1, ib)
            sb["cast3_tot"] = P.count("cast3")
        P.barrier()

    emit_l3_ffn(P, sb, dr, s_dram)


def emit_l3_ffn(P, sb, dr, s_dram):
    nc = P.nc
    X, XNT = sb["X"], sb["XNT"]
    ONESRB = sb["ONESRB"]
    SCRB = sb["SCRB"]

    with ExitStack() as lc:
        SC = lambda n, sh, dt: lc.enter_context(nc.sbuf_tensor(n, list(sh), dt))
        RW3 = SC("l3_RW3", (128, DC * 16), BF16)
        RB3 = SC("l3_RB3", (1, 16), BF16)
        Z16 = SC("l3_Z16", (128, 16), F32)
        GE16 = SC("l3_GE16", (128, 16), F32)
        EPF = SC("l3_EPF", (128, 8), F32)
        CBA3 = SC("l3_CBA3", (128, 256), F32)
        OA3 = SC("l3_OA3", (128, 8 * 1024), F32)
        HID = SC("l3_HID", (128, 4 * 1024), BF16)
        WUP = SC("l3_WUP", (128, 2 * 4096), BF16)
        WGT3 = SC("l3_WGT3", (128, 2 * 4096), BF16)
        WDN = SC("l3_WDN", (128, 2 * 4096), BF16)
        SIN3 = SC("l3_SIN3", (128, 1024), BF16)

        PR3 = lc.enter_context(nc.psum_tensor("l3_PR3", [128, 512], F32))
        PU3 = lc.enter_context(nc.psum_tensor("l3_PU3", [128, 2 * 512], F32))
        PG3 = lc.enter_context(nc.psum_tensor("l3_PG3", [128, 2 * 512], F32))
        PO3 = lc.enter_context(nc.psum_tensor("l3_PO3", [128, 2 * 512], F32))

        rl = P.op("pool", lambda e: e.dma_start(
            out=RW3[:].rearrange("p (c n) -> p c n", c=DC),
            in_=dr["l3_rwT"].rearrange("(c p) n -> p c n", p=128)), amt=16)
        rl2 = P.op("pool", lambda e: e.dma_start(out=RB3[:], in_=dr["l3_rb"]),
                   amt=16)

        wg_rd = [0, 0]
        wu_rd = [0, 0]
        wd_rd = [0, 0]
        c3t = sb.get("cast3_tot", 0)

        def wload3(e_, extra=()):
            s_ = e_ % 2
            a = P.op("pool", lambda e, e_=e_, s_=s_: e.dma_start(
                out=WGT3[:, s_ * 4096:(s_ + 1) * 4096], in_=dr["l3_wb3g"][e_]),
                waits=list(extra) + [("pe", wg_rd[s_])], amt=16, sem=f"w3g{s_}")
            b = P.op("pool", lambda e, e_=e_, s_=s_: e.dma_start(
                out=WUP[:, s_ * 4096:(s_ + 1) * 4096], in_=dr["l3_wb3u"][e_]),
                waits=[("pe", wu_rd[s_])], amt=16, sem=f"w3u{s_}")
            c = P.op("pool", lambda e, e_=e_, s_=s_: e.dma_start(
                out=WDN[:, s_ * 4096:(s_ + 1) * 4096], in_=dr["l3_wb3d"][e_]),
                waits=[("pe", wd_rd[s_])], amt=16, sem=f"w3d{s_}")
            return (a, b, c)

        seq = [(half, e_) for half in range(2) for e_ in range(18)]
        ldsmap = {}
        ldsmap[0] = wload3(0, extra=[("cast3", c3t)])
        ldsmap[1] = wload3(1)

        for t in range(NT):
            g0 = mm(P, PR3[:, 0:16],
                    [(XNT[:, dc * XSTR + 1 + t * 128: dc * XSTR + 1 + (t + 1) * 128],
                      RW3[:, dc * 16:(dc + 1) * 16]) for dc in range(DC)] +
                    [(ONESRB[:], RB3[:])],
                    waits=[("pool", rl2), ("dve", P.count("dve"))])
            z0 = P.op("dve", lambda v: v.tensor_copy(Z16[:], PR3[:, 0:16]),
                      waits=[("pe", g0)])
            m0 = P.op("dve", lambda v: v.reduce_max(EPF[:, 0:1], Z16[:], axis=AX),
                      waits=[("dve", z0)])
            m1 = P.op("dve", lambda v: v.tensor_scalar(
                GE16[:], Z16[:], EPF[:, 0:1], None, ALU.is_ge), waits=[("dve", m0)])
            m2 = P.op("dve", lambda v: v.scalar_tensor_tensor(
                GE16[:], GE16[:], -1e9, Z16[:], ALU.mult, ALU.add),
                waits=[("dve", m1)])
            m3 = P.op("dve", lambda v: v.reduce_max(EPF[:, 1:2], GE16[:], axis=AX),
                      waits=[("dve", m2)])
            c0 = P.op("dve", lambda v: v.tensor_scalar(
                GE16[:], Z16[:], EPF[:, 1:2], None, ALU.is_ge),
                waits=[("dve", m3)])
            d0 = P.op("dve", lambda v: v.tensor_sub(EPF[:, 2:3], EPF[:, 1:2],
                                                    EPF[:, 0:1]),
                      waits=[("dve", c0)])
            d1 = P.op("act", lambda s: s.activation(EPF[:, 3:4], EPF[:, 2:3],
                                                    AF.Exp),
                      waits=[("dve", d0)])
            d2 = P.op("dve", lambda v: v.tensor_scalar_add(EPF[:, 3:4],
                                                           EPF[:, 3:4], 1.0),
                      waits=[("act", d1)])
            d3 = P.op("dve", lambda v: v.reciprocal(EPF[:, 4:5], EPF[:, 3:4]),
                      waits=[("dve", d2)])
            n0 = P.op("dve", lambda v: v.tensor_scalar_mul(EPF[:, 5:6],
                                                           EPF[:, 0:1], -1.0),
                      waits=[("dve", d3)])
            e0 = P.op("act", lambda s: s.activation(
                Z16[:], Z16[:], AF.Exp, bias=EPF[:, 5:6]), waits=[("dve", n0)])
            c1 = P.op("dve", lambda v: v.tensor_scalar(
                Z16[:], Z16[:], EPF[:, 4:5], None, ALU.mult), waits=[("act", e0)])
            P.op("dve", lambda v, t=t: v.tensor_mul(
                CBA3[:, t * 16:(t + 1) * 16], Z16[:], GE16[:]),
                waits=[("dve", c1)])

        pg_rd = [0, 0]
        pu_rd = [0, 0]
        po_rd = [0, 0]
        scrb_rd = [0, 0]
        oa_last = 0

        def emit_combine(half):
            ov = oa_last
            for t8 in range(8):
                for hh in range(2):
                    t_ = (half * 1024 + t8 * 128) // 128
                    osl = OA3[:, t8 * 1024 + hh * 512: t8 * 1024 + hh * 512 + 512]
                    slot = SIN3[:, hh * 512:hh * 512 + 512]
                    sl = P.op("sp", lambda e, t_=t_, hh=hh, slot=slot: e.dma_start(
                        out=slot, in_=s_dram.ap()[t_ * 128:(t_ + 1) * 128,
                                                  hh * 512:hh * 512 + 512]),
                        waits=[("dve", ov)], amt=16)
                    ga = P.op("act", lambda s, osl=osl: s.activation(
                        SCRB[:, 0:512], osl, AF.Silu),
                        waits=[("dve", ov), ("act", P.count("act"))])
                    z3 = P.op("dve", lambda v, slot=slot: v.tensor_mul(
                        SCRB[:, 512:1024], SCRB[:, 0:512], slot),
                        waits=[("act", ga), ("sp", sl)])
                    xsl = X[:, t_ * D + hh * 512: t_ * D + hh * 512 + 512]
                    ov = P.op("dve", lambda v, xsl=xsl: v.tensor_add(
                        xsl, xsl, SCRB[:, 512:1024]), waits=[("dve", z3)])
            scrb_rd[0] = ov
            scrb_rd[1] = ov
            return ov

        oa_init = [False] * 16
        for si, (half, e_) in enumerate(seq):
            if half == 1 and e_ == 0:
                emit_combine(0)
                oa_init = [False] * 16
            ho = half * 1024
            slot = e_ % 2
            shared = e_ >= 16
            if si + 1 < len(seq) and (si + 1) not in ldsmap:
                ldsmap[si + 1] = wload3(seq[si + 1][1])
            lg, lu, ld_ = ldsmap[si]
            for hc in range(4):
                for tt in range(2):
                    ps = (hc * 2 + tt) % 2
                    xsl = [XNT[:, dc * XSTR + 1 + ho + tt * 512:
                               dc * XSTR + 1 + ho + tt * 512 + 512]
                           for dc in range(DC)]
                    gw = [("act", pg_rd[ps])]
                    uw = [("dve", pu_rd[ps])]
                    if hc == 0 and tt == 0:
                        gw.append((f"w3g{slot}", lg))
                        uw.append((f"w3u{slot}", lu))
                    gm = mm(P, PG3[:, ps * 512:(ps + 1) * 512],
                            [(WGT3[:, slot * 4096 + dc * 512 + hc * 128:
                                   slot * 4096 + dc * 512 + (hc + 1) * 128],
                              xsl[dc]) for dc in range(DC)], waits=gw)
                    um = mm(P, PU3[:, ps * 512:(ps + 1) * 512],
                            [(WUP[:, slot * 4096 + dc * 512 + hc * 128:
                                  slot * 4096 + dc * 512 + (hc + 1) * 128],
                              xsl[dc]) for dc in range(DC)], waits=uw)
                    sg = P.op("act", lambda s, ps=ps: s.activation(
                        SCRB[:, ps * 512:(ps + 1) * 512],
                        PG3[:, ps * 512:(ps + 1) * 512], AF.Silu),
                        waits=[("pe", gm), ("dve", scrb_rd[ps])])
                    dst = HID[:, hc * 1024 + tt * 512: hc * 1024 + tt * 512 + 512]
                    hm = P.op("dve", lambda v, dst=dst, ps=ps: v.tensor_mul(
                        dst, PU3[:, ps * 512:(ps + 1) * 512],
                        SCRB[:, ps * 512:(ps + 1) * 512]),
                        waits=[("pe", um), ("act", sg)])
                    pg_rd[ps] = sg
                    pu_rd[ps] = hm
                    scrb_rd[ps] = hm
            wg_rd[slot] = P.count("pe")
            wu_rd[slot] = P.count("pe")
            hall_dve = P.count("dve")
            for t8 in range(8):
                for hh in range(2):
                    ps = (t8 * 2 + hh) % 2
                    dw = [("dve", hall_dve), ("dve", po_rd[ps])]
                    if t8 == 0 and hh == 0:
                        dw.append((f"w3d{slot}", ld_))
                    o0 = mm(P, PO3[:, ps * 512:(ps + 1) * 512],
                            [(HID[:, hc * 1024 + t8 * 128:
                                  hc * 1024 + (t8 + 1) * 128],
                              WDN[:, slot * 4096 + hc * 1024 + hh * 512:
                                  slot * 4096 + hc * 1024 + hh * 512 + 512])
                             for hc in range(4)], waits=dw)
                    osl = OA3[:, t8 * 1024 + hh * 512: t8 * 1024 + hh * 512 + 512]
                    idx = t8 * 2 + hh
                    cb = (None if shared
                          else CBA3[:, (ho // 128 + t8) * 16 + e_:
                                    (ho // 128 + t8) * 16 + e_ + 1])
                    po = PO3[:, ps * 512:(ps + 1) * 512]
                    if not oa_init[idx]:
                        if cb is None:
                            oa_last = P.op("dve", lambda v, osl=osl, po=po:
                                           v.tensor_copy(osl, po),
                                           waits=[("pe", o0)])
                        else:
                            oa_last = P.op("dve", lambda v, osl=osl, po=po, cb=cb:
                                           v.tensor_scalar(
                                osl, po, cb, None, ALU.mult),
                                waits=[("pe", o0)])
                        oa_init[idx] = True
                    else:
                        if cb is None:
                            oa_last = P.op("dve", lambda v, osl=osl, po=po:
                                           v.scalar_tensor_tensor(
                                osl, po, 1.0, osl, ALU.mult, ALU.add),
                                waits=[("pe", o0)])
                        else:
                            oa_last = P.op("dve", lambda v, osl=osl, po=po, cb=cb:
                                           v.scalar_tensor_tensor(
                                osl, po, cb, osl, ALU.mult, ALU.add),
                                waits=[("pe", o0)])
                    po_rd[ps] = oa_last
            wd_rd[slot] = P.count("pe")
        emit_combine(1)
        P.barrier()


# version 17 (final)
# speedup vs baseline: 1.0061x; 1.0061x over previous
import os
import numpy as np
import ml_dtypes
import concourse.bass as bass
import concourse.mybir as mybir
from concourse.bass_utils import run_bass_kernel_spmd
from contextlib import ExitStack

F32 = mybir.dt.float32
BF16 = mybir.dt.bfloat16
AF = mybir.ActivationFunctionType
ALU = mybir.AluOpType
AX = mybir.AxisListType.X

NCORES = 8
NTOK = 2048
NT = NTOK // 128
D = 1024
DC = D // 128
XSTR = 2052

BF = ml_dtypes.bfloat16
_cut = os.environ.get("L2_CUT", "0")
L2_CUTS = set(int(x) for x in _cut.split(","))
class _L2C:
    def __eq__(self, other):
        return other in L2_CUTS
    def __ne__(self, other):
        return other not in L2_CUTS
    def __contains__(self, other):
        return False
L2_CUT = _L2C()


class Prog:
    Q = ("pe", "act", "dve", "pool", "sp")

    def __init__(self, nc, stack):
        self.nc = nc
        self.stack = stack
        self.q = {e: [] for e in self.Q}
        self._sems = {}
        self._virt = {}
        self._rr = {}

    def sem(self, name):
        if name not in self._sems:
            h = self.stack.enter_context(self.nc.semaphore(name))
            self._sems[name] = [h, 0]
        return self._sems[name]

    DMA_Q = ("sp", "pool")
    NDMASEM = 2

    def _fence(self, waits):
        out = []
        for (n, v) in waits:
            if n in self.DMA_Q:
                if v > 0:
                    for k in range(self.NDMASEM):
                        sn = f"{n}@{k}"
                        if self.count(sn) > 0:
                            out.append((sn, self.count(sn)))
            else:
                out.append((n, v))
        return out

    def op(self, qn, build, waits=(), amt=1, sem=None, noinc=False):
        if noinc:
            wl = [(self.sem(n)[0], v) for (n, v) in self._fence(waits) if v > 0]

            def fno(eng, build=build, wl=wl):
                for (h, v) in wl:
                    eng.wait_ge(h, v)
                build(eng)
            self.q[qn].append(fno)
            return None
        if sem is None and qn == "pool" and amt == 1:
            sem = "poolc"
        guard = None
        if amt == 16 and (sem or qn) in self.DMA_Q:
            base = sem or qn
            k = self._rr.get(base, 0)
            self._rr[base] = (k + 1) % self.NDMASEM
            sem = f"{base}@{k}"
            s = self.sem(sem)
            if s[1] > 0:
                guard = (s[0], s[1])
            self._virt[base] = self._virt.get(base, 0) + 16
        else:
            s = self.sem(sem or qn)
        waits = self._fence(waits)
        s[1] += amt
        target = s[1] if guard is None else self._virt[sem.split("@")[0]]
        wl = [(self.sem(n)[0], v) for (n, v) in waits if v > 0]
        handle = s[0]

        def f(eng, build=build, wl=wl, handle=handle, amt=amt, guard=guard):
            if guard is not None:
                eng.wait_ge(guard[0], guard[1])
            for (h, v) in wl:
                eng.wait_ge(h, v)
            build(eng).then_inc(handle, amt)
        self.q[qn].append(f)
        return target

    def count(self, name):
        if name in self.DMA_Q:
            return self._virt.get(name, 0)
        return self.sem(name)[1]

    def waits(self, qn, waits):
        waits = self._fence(waits)
        wl = [(self.sem(n)[0], v) for (n, v) in waits if v > 0]

        def f(eng, wl=wl):
            for (h, v) in wl:
                eng.wait_ge(h, v)
        self.q[qn].append(f)

    def snap(self):
        return {k: v[1] for k, v in self._sems.items()}

    def barrier(self):
        snap = self.snap()
        for qn in self.Q:
            self.waits(qn, [(n, c) for n, c in snap.items() if n != qn])


def mm(P, out_ap, pairs, waits=()):
    n = len(pairs)
    r = None
    for i, (lh, rh) in enumerate(pairs):
        st, sp = (i == 0), (i == n - 1)
        r = P.op("pe", (lambda t, lh=lh, rh=rh, st=st, sp=sp: t.matmul(
            out_ap, lh, rh, start=st, stop=sp)),
            waits=(waits if i == 0 else ()), noinc=not sp)
    return r



def _prep_weights(inp):
    w = {}
    T = lambda a: np.ascontiguousarray(np.asarray(a).T)
    C = lambda a: np.ascontiguousarray(np.asarray(a))
    w["l0_wqT"] = T(inp["L0_attn_w_q"])
    w["l0_wkT"] = T(inp["L0_attn_w_k"])
    w["l0_wvT"] = T(inp["L0_attn_w_v"])
    w["l0_woT"] = T(inp["L0_attn_w_o"])
    w["l0_mem"] = C(inp["L0_attn_memory"])
    w["l0_mnorm"] = np.ascontiguousarray(
        np.asarray(inp["L0_attn_memory_norm"]).reshape(2, 128).T)
    w["l0_gate"] = np.ascontiguousarray(
        np.tile(np.asarray(inp["L0_attn_gate"]).reshape(1, 1), (128, 1)))
    w["l0_wshT"] = T(inp["L0_ffn_W_shared"])
    w["l0_wexT"] = T(inp["L0_ffn_W_expert"])
    w["l0_wg"] = C(inp["L0_ffn_W_gate"])
    w["l0_bias"] = C(inp["L0_ffn_bias"]).reshape(1, 1024)
    w["l1_win"] = C(inp["L1_attn_W_in"])
    w["l1_bin"] = np.ascontiguousarray(
        np.asarray(inp["L1_attn_b_in"]).reshape(32, 128).T)
    w["l1_wxp"] = C(inp["L1_attn_W_xp"])
    w["l1_bxp"] = C(inp["L1_attn_b_xp"]).reshape(64, 1)
    w["l1_wdt"] = C(inp["L1_attn_W_dt"])
    w["l1_bdt"] = C(inp["L1_attn_b_dt"]).reshape(1, 1024)
    w["l1_wout"] = C(inp["L1_attn_W_out"])
    w["l1_bout"] = C(inp["L1_attn_b_out"]).reshape(1, 1024)
    w["l1_alog"] = C(inp["L1_attn_A_log"]).reshape(64, 1)
    w["l1_D"] = np.ascontiguousarray(
        np.asarray(inp["L1_attn_D"]).reshape(8, 128).T)
    w["l1_nw"] = np.ascontiguousarray(
        np.asarray(inp["L1_ffn_norm_weight"]).reshape(8, 128).T)
    w["l1_rwT"] = T(inp["L1_ffn_router_w"])
    sw1 = np.asarray(inp["L1_ffn_sw1"])
    w["l1_sw1f"] = np.ascontiguousarray(
        sw1.reshape(8, 128, 4, 2, 512).transpose(2, 3, 1, 0, 4).reshape(
            4, 2, 128, 4096))
    w["l1_sb1"] = np.ascontiguousarray(
        np.asarray(inp["L1_ffn_sb1"]).reshape(32, 128).T)
    sw2 = np.asarray(inp["L1_ffn_sw2"])
    w["l1_sw2f"] = np.ascontiguousarray(
        sw2.reshape(4, 8, 128, 2, 512).transpose(0, 3, 2, 1, 4).reshape(
            4, 2, 128, 4096))
    w["l1_sb2"] = C(inp["L1_ffn_sb2"]).reshape(1, 1024)
    ew1 = np.asarray(inp["L1_ffn_ew1"])
    w["l1_ew1f"] = np.ascontiguousarray(
        ew1.reshape(8, 8, 128, 4, 2, 512).transpose(0, 3, 4, 2, 1, 5).reshape(
            8, 4, 2, 128, 4096))
    w["l1_eb1"] = np.ascontiguousarray(np.asarray(
        inp["L1_ffn_eb1"]).reshape(8, 32, 128).transpose(2, 0, 1).reshape(128, 256))
    ew2 = np.asarray(inp["L1_ffn_ew2"])
    w["l1_ew2f"] = np.ascontiguousarray(
        ew2.reshape(8, 4, 8, 128, 2, 512).transpose(0, 1, 4, 3, 2, 5).reshape(
            8, 4, 2, 128, 4096))
    w["l1_eb2"] = C(inp["L1_ffn_eb2"])
    w["l2_wkvT"] = np.ascontiguousarray(np.concatenate(
        [np.asarray(inp["L2_attn_w_K"]).reshape(512, 1024).T,
         np.asarray(inp["L2_attn_w_V"]).reshape(512, 1024).T], axis=1))
    w["l2_wqT"] = T(np.asarray(inp["L2_attn_w_Q"]).reshape(512, 1024))
    w["l2_wgT"] = T(np.asarray(inp["L2_attn_w_G"]).reshape(32, 1024))
    w["l2_bgb"] = np.ascontiguousarray(np.tile(
        np.asarray(inp["L2_attn_b_G"]).reshape(1, 32), (128, 1)))
    w["l2_wo"] = C(inp["L2_attn_w_O"])
    w["l2_ikb"] = np.ascontiguousarray(
        np.asarray(inp["L2_attn_init_K_rec"])[:, 1:32, :].reshape(124, 64).T)
    w["l2_ivb"] = np.ascontiguousarray(
        np.asarray(inp["L2_attn_init_V_rec"])[:, 1:32, :].transpose(1, 0, 2).reshape(31, 256))
    w["l2_wrT"] = T(inp["L2_ffn_W_r"])
    w["l2_wk2T"] = T(inp["L2_ffn_W_k"])
    w["l2_wv2T"] = T(inp["L2_ffn_W_v"])
    w["l2_mur"] = np.ascontiguousarray(
        np.asarray(inp["L2_ffn_mu_r"]).reshape(8, 128).T)
    w["l2_muk"] = np.ascontiguousarray(
        np.asarray(inp["L2_ffn_mu_k"]).reshape(8, 128).T)
    w["l3_wqkvT"] = T(inp["L3_attn_w_qkv"])
    w["l3_woutT"] = T(inp["L3_attn_w_out"])
    w["l3_bout"] = C(inp["L3_attn_b_out"]).reshape(1, 1024)
    w["l3_rwT"] = T(inp["L3_ffn_router_w"])
    w["l3_rb"] = C(inp["L3_ffn_router_bias"]).reshape(1, 16)
    up = np.asarray(inp["L3_ffn_up_proj"])
    w["l3_upf"] = C(up.reshape(16, 8, 128, 512).transpose(0, 2, 1, 3).reshape(
        16, 128, 4096))
    gate = np.asarray(inp["L3_ffn_gate_proj"])
    w["l3_gatef"] = C(gate.reshape(16, 8, 128, 512).transpose(0, 2, 1, 3).reshape(
        16, 128, 4096))
    down = np.asarray(inp["L3_ffn_down_proj"])
    w["l3_downf"] = C(down.reshape(16, 4, 128, 1024).transpose(0, 2, 1, 3).reshape(
        16, 128, 4096))
    shgT = np.asarray(inp["L3_ffn_sh_gate"]).T
    w["l3_shgf"] = C(shgT.reshape(8, 128, 2, 512).transpose(2, 1, 0, 3).reshape(
        2, 128, 4096))
    shuT = np.asarray(inp["L3_ffn_sh_up"]).T
    w["l3_shuf"] = C(shuT.reshape(8, 128, 2, 512).transpose(2, 1, 0, 3).reshape(
        2, 128, 4096))
    shdT = np.asarray(inp["L3_ffn_sh_down"]).T
    w["l3_shdf"] = C(shdT.reshape(2, 4, 128, 1024).transpose(0, 2, 1, 3).reshape(
        2, 128, 4096))
    return w


def _prep_core_meta(c, inp=None):
    m = {}
    batch, pos = c // 4, c % 4
    if inp is not None:
        lg0 = pos * 8
        m["l2_ikm"] = np.ascontiguousarray(np.asarray(
            inp["L2_attn_init_K_rec"])[:, lg0:lg0 + 8, :].reshape(32, 64).T)
        m["l2_ivm"] = np.ascontiguousarray(np.asarray(
            inp["L2_attn_init_V_rec"])[:, lg0:lg0 + 8, :].reshape(32, 64).T)
    pref = np.zeros((8,), np.float32)
    for r in range(8):
        if r // 4 == batch and r < c:
            pref[r] = 1.0
    m["mask_prefix"] = np.ascontiguousarray(np.tile(pref[None, :], (128, 1)))
    m["mask_prefix_inv"] = np.ascontiguousarray(
        np.tile((1.0 - pref)[None, :], (128, 1)))
    prev = np.zeros((8,), np.float32)
    if pos > 0:
        prev[c - 1] = 1.0
    m["mask_prev"] = np.ascontiguousarray(np.tile(prev[None, :], (128, 1)))
    m["is_first"] = np.full((128, 1), 1.0 if pos == 0 else 0.0, np.float32)
    t0 = pos * NTOK
    m["l3_qoff"] = np.ascontiguousarray(
        np.tile((t0 + 128 * np.arange(16, dtype=np.float32))[None, :], (128, 1)))
    return m


W_SHAPES = {
    "l0_wqT": (1024, 256), "l0_wkT": (1024, 256), "l0_wvT": (1024, 256),
    "l0_woT": (256, 1024), "l0_mem": (256, 256), "l0_mnorm": (128, 2),
    "l0_gate": (128, 1), "l0_wshT": (1024, 1024), "l0_wexT": (1024, 1024),
    "l0_wg": (1024, 7), "l0_bias": (1, 1024),
    "l1_win": (1024, 4096), "l1_bin": (128, 32), "l1_wxp": (2048, 64),
    "l1_bxp": (64, 1), "l1_wdt": (2048, 1024), "l1_bdt": (1, 1024),
    "l1_wout": (2048, 1024), "l1_bout": (1, 1024), "l1_alog": (64, 1),
    "l1_D": (128, 8), "l1_nw": (128, 8), "l1_rwT": (1024, 8),
    "l1_sw1f": (4, 2, 128, 4096), "l1_sb1": (128, 32),
    "l1_sw2f": (4, 2, 128, 4096),
    "l1_sb2": (1, 1024), "l1_ew1f": (8, 4, 2, 128, 4096), "l1_eb1": (128, 256),
    "l1_ew2f": (8, 4, 2, 128, 4096), "l1_eb2": (8, 1024),
    "l2_wkvT": (1024, 1024), "l2_wqT": (1024, 512), "l2_wgT": (1024, 32),
    "l2_bgb": (128, 32), "l2_wo": (512, 1024), "l2_ikm": (64, 32),
    "l2_ivm": (64, 32), "l2_ikb": (64, 124), "l2_ivb": (31, 256),
    "l2_wrT": (1024, 1024), "l2_wk2T": (1024, 1024), "l2_wv2T": (1024, 1024),
    "l2_mur": (128, 8), "l2_muk": (128, 8),
    "l3_wqkvT": (1024, 1536), "l3_woutT": (512, 1024), "l3_bout": (1, 1024),
    "l3_rwT": (1024, 16), "l3_rb": (1, 16), "l3_upf": (16, 128, 4096),
    "l3_gatef": (16, 128, 4096), "l3_downf": (16, 128, 4096),
    "l3_shgf": (2, 128, 4096), "l3_shuf": (2, 128, 4096),
    "l3_shdf": (2, 128, 4096),
}
L0_W = [k for k in W_SHAPES if k.startswith("l0_")]
L1_W = [k for k in W_SHAPES if k.startswith("l1_")]
L2_W = [k for k in W_SHAPES if k.startswith("l2_")]
L3_W = [k for k in W_SHAPES if k.startswith("l3_")]
LAYER_W = {0: L0_W, 1: L1_W, 2: L2_W, 3: L3_W}
META = ("mask_prefix", "mask_prefix_inv", "mask_prev", "is_first", "l3_qoff")
META_SHAPES = {"mask_prefix": (128, 8), "mask_prefix_inv": (128, 8),
               "mask_prev": (128, 8), "is_first": (128, 1), "l3_qoff": (128, 16)}



def build_program(layers):
    stack = ExitStack()
    nc = bass.Bass("TRN2", target_bir_lowering=False)
    P = Prog(nc, stack)

    dr = {}
    dr["x_in"] = nc.dram_tensor("x_in", [NTOK, D], F32, kind="ExternalInput").ap()
    dr["x_out"] = nc.dram_tensor("x_out", [NTOK, D], F32, kind="ExternalOutput").ap()
    for nm in META:
        dr[nm] = nc.dram_tensor(nm, list(META_SHAPES[nm]), F32,
                                kind="ExternalInput").ap()
    for L in layers:
        for nm in LAYER_W[L]:
            dr[nm] = nc.dram_tensor(nm, list(W_SHAPES[nm]), F32,
                                    kind="ExternalInput").ap()

    def S(n, sh, dt):
        return stack.enter_context(nc.sbuf_tensor(n, list(sh), dt))

    sb = {}
    sb["X"] = S("X", (128, NT * D), F32)
    sb["XNT"] = S("XNT", (128, DC * XSTR), BF16)
    sb["IDB"] = S("IDB", (128, 128), BF16)
    sb["IDF"] = S("IDF", (128, 128), F32)
    sb["TRI01"] = S("TRI01", (128, 128), BF16)
    sb["ONESC"] = S("ONESC", (128, 1), BF16)
    sb["ONESF"] = S("ONESF", (128, 1), F32)
    sb["ONESRB"] = S("ONESRB", (1, 128), BF16)
    sb["ONESRF"] = S("ONESRF", (1, 128), F32)
    sb["SCR1"] = S("SCR1", (128, 1024), F32)
    sb["SCRB"] = S("SCRB", (128, 1024), BF16)
    sb["STAT"] = S("STAT", (128, 8), F32)
    sb["MPREF"] = S("MPREF", (128, 8), F32)
    sb["MPREFI"] = S("MPREFI", (128, 8), F32)
    sb["MPREV"] = S("MPREV", (128, 8), F32)
    sb["ISF"] = S("ISF", (128, 1), F32)
    sb["QOFF"] = S("QOFF", (128, 16), F32)
    sb["EPS5"] = S("EPS5", (128, 1), F32)
    sb["EPS8"] = S("EPS8", (128, 1), F32)
    sb["ONE2K"] = S("ONE2K", (1, NTOK), BF16)
    sb["ONE1"] = S("ONE1", (128, 1), F32)
    sb["PLN"] = stack.enter_context(nc.psum_tensor("PLN", [128, 1024], BF16))

    P.op("sp", lambda e: e.dma_start(
        out=sb["X"][:].rearrange("p (t d) -> p t d", t=NT),
        in_=dr["x_in"].rearrange("(t p) d -> p t d", p=128)),
        amt=16)
    for nm, key in (("mask_prefix", "MPREF"), ("mask_prefix_inv", "MPREFI"),
                    ("mask_prev", "MPREV"),
                    ("is_first", "ISF"), ("l3_qoff", "QOFF")):
        P.op("sp", lambda e, t=sb[key], d_=dr[nm]: e.dma_start(out=t[:], in_=d_),
             amt=16)

    c0 = P.op("pool", lambda e: e.memset(sb["SCR1"][:, 0:128], 1.0))
    P.op("pool", lambda e: e.affine_select(
        sb["IDF"][:], sb["SCR1"][:, 0:128], pattern=[[1, 128]],
        compare_op=ALU.is_equal, fill=0.0, base=0, channel_multiplier=-1),
        waits=[("poolc", c0)])
    P.op("pool", lambda e: e.tensor_copy(sb["IDB"][:], sb["IDF"][:]),
         waits=[("poolc", P.count("poolc"))])
    P.op("pool", lambda e: e.affine_select(
        sb["TRI01"][:], sb["SCR1"][:, 0:128], pattern=[[1, 128]],
        compare_op=ALU.is_ge, fill=0.0, base=0, channel_multiplier=-1),
        waits=[("poolc", P.count("poolc"))])
    P.op("pool", lambda e: e.memset(sb["ONESC"][:], 1.0))
    P.op("pool", lambda e: e.memset(sb["ONESF"][:], 1.0))
    P.op("pool", lambda e: e.memset(sb["ONESRB"][:], 1.0))
    P.op("pool", lambda e: e.memset(sb["ONESRF"][:], 1.0))
    P.op("pool", lambda e: e.memset(sb["EPS5"][:], 1e-5))
    P.op("pool", lambda e: e.memset(sb["EPS8"][:], 1e-8))
    P.op("pool", lambda e: e.memset(sb["ONE2K"][:], 1.0))
    P.op("pool", lambda e: e.memset(sb["ONE1"][:], 1.0))
    P.barrier()

    if 1 in layers:
        dr["l1_wb1"] = nc.dram_tensor("l1_wb1", [8, 4, 2, 128, 4096], BF16).ap()
        dr["l1_wb2"] = nc.dram_tensor("l1_wb2", [8, 4, 2, 128, 4096], BF16).ap()
        dr["l1_wbs1"] = nc.dram_tensor("l1_wbs1", [4, 2, 128, 4096], BF16).ap()
        dr["l1_wbs2"] = nc.dram_tensor("l1_wbs2", [4, 2, 128, 4096], BF16).ap()
    if 3 in layers:
        dr["l3_wb3g"] = nc.dram_tensor("l3_wb3g", [18, 128, 4096], BF16).ap()
        dr["l3_wb3u"] = nc.dram_tensor("l3_wb3u", [18, 128, 4096], BF16).ap()
        dr["l3_wb3d"] = nc.dram_tensor("l3_wb3d", [18, 128, 4096], BF16).ap()
        sb["l3_cast_early"] = 2 in layers

    for L in layers:
        if L == 1:
            rms2 = S("RMS2", (128, NT), F32)
            emit_layernorm(P, sb, L, rms2=rms2)
            P.barrier()
            emit_l1(P, sb, dr, rms2)
        elif L == 0:
            emit_layernorm(P, sb, L)
            P.barrier()
            emit_l0(P, sb, dr)
        elif L == 2:
            emit_layernorm(P, sb, L)
            P.barrier()
            emit_l2(P, sb, dr)
        elif L == 3:
            emit_layernorm(P, sb, L)
            P.barrier()
            emit_l3(P, sb, dr)
        else:
            raise NotImplementedError(L)
        P.barrier()

    st = P.op("sp", lambda e: e.dma_start(
        out=dr["x_out"].rearrange("(t p) d -> p t d", p=128),
        in_=sb["X"][:].rearrange("p (t d) -> p t d", t=NT)),
        amt=16)
    P.waits("sp", [("sp", st)])

    with nc.Block() as block:
        @block.tensor
        def _(t):
            for f in P.q["pe"]:
                f(t)

        @block.scalar
        def _(s):
            for f in P.q["act"]:
                f(s)

        @block.vector
        def _(v):
            for f in P.q["dve"]:
                f(v)

        @block.gpsimd
        def _(g):
            for f in P.q["pool"]:
                f(g)

        @block.sync
        def _(s):
            for f in P.q["sp"]:
                f(s)

    return nc, stack



def emit_layernorm(P, sb, L, rms2=None):
    X, XNT, STAT, SCR1, SCRB, IDB, PLN = (
        sb["X"], sb["XNT"], sb["STAT"], sb["SCR1"], sb["SCRB"], sb["IDB"], sb["PLN"])
    prev_copies = P.count("act")
    prev_tp = P.count("pe")
    for t in range(NT):
        xt = X[:, t * D:(t + 1) * D]
        d0 = P.op("dve", lambda v, xt=xt: v.reduce_sum(STAT[:, 0:1], xt, axis=AX),
                  waits=[("act", prev_copies)])
        d1 = P.op("dve", lambda v: v.tensor_scalar_mul(
            STAT[:, 1:2], STAT[:, 0:1], 1.0 / D), waits=[("dve", d0)])
        d2 = P.op("dve", lambda v, xt=xt: v.tensor_scalar(
            SCR1[:], xt, STAT[:, 1:2], None, ALU.subtract), waits=[("dve", d1)])
        a0 = P.op("act", lambda s: s.activation(SCRB[:], SCR1[:], AF.Square,
                                                accum_out=STAT[:, 2:3]),
                  waits=[("dve", d2)])
        a1 = P.op("act", lambda s: s.activation(STAT[:, 3:4], STAT[:, 2:3], AF.Ln,
                                                bias=sb["EPS5"][:, 0:1],
                                                scale=1.0 / D),
                  waits=[("act", a0)])
        a2 = P.op("act", lambda s: s.activation(STAT[:, 4:5], STAT[:, 3:4], AF.Exp,
                                                scale=-0.5), waits=[("act", a1)])
        if rms2 is not None:
            r0 = P.op("dve", lambda v: v.tensor_scalar_mul(
                STAT[:, 5:6], STAT[:, 2:3], 1.0 / D), waits=[("act", a2)])
            r1 = P.op("dve", lambda v: v.tensor_mul(
                STAT[:, 6:7], STAT[:, 4:5], STAT[:, 4:5]), waits=[("dve", r0)])
            r2 = P.op("dve", lambda v: v.tensor_mul(
                STAT[:, 6:7], STAT[:, 5:6], STAT[:, 6:7]), waits=[("dve", r1)])
            r3 = P.op("act", lambda s: s.activation(
                STAT[:, 7:8], STAT[:, 6:7], AF.Ln, bias=sb["EPS8"][:, 0:1]),
                waits=[("dve", r2)])
            r4 = P.op("act", lambda s, t=t: s.activation(
                rms2[:, t:t + 1], STAT[:, 7:8], AF.Exp, scale=-0.5),
                waits=[("act", r3)])
        d3 = P.op("dve", lambda v: v.tensor_scalar(
            SCRB[:], SCR1[:], STAT[:, 4:5], None, ALU.mult),
            waits=[("act", a2), ("pe", prev_tp)])
        for dc in range(DC):
            w = []
            if dc == 0:
                w = [("dve", d3), ("act", prev_copies)]
            prev_tp = P.op("pe", lambda t_, dc=dc: t_.transpose(
                PLN[:, dc * 128:(dc + 1) * 128],
                SCRB[:, dc * 128:(dc + 1) * 128], IDB[:]), waits=w)
        for dc in range(DC):
            dst = XNT[:, dc * XSTR + 1 + t * 128: dc * XSTR + 1 + (t + 1) * 128]
            prev_copies = P.op("act", lambda s, dst=dst, dc=dc: s.copy(
                dst, PLN[:, dc * 128:(dc + 1) * 128]),
                waits=([("pe", prev_tp)] if dc == 0 else []))



def emit_l0(P, sb, dr):
    nc = P.nc
    X, XNT = sb["X"], sb["XNT"]
    IDF, IDB, TRI01 = sb["IDF"], sb["IDB"], sb["TRI01"]
    ONESC, ONESF, ONESRB, ONESRF = (sb["ONESC"], sb["ONESF"], sb["ONESRB"],
                                    sb["ONESRF"])
    MPREF = sb["MPREF"]
    SCR1, SCRB = sb["SCR1"], sb["SCRB"]
    DM, NDD = 256, 2

    with ExitStack() as ls:
        SS = lambda n, sh, dt: ls.enter_context(nc.sbuf_tensor(n, list(sh), dt))
        PS = lambda n, sh, dt: ls.enter_context(nc.psum_tensor(n, list(sh), dt))

        S_TM = SS("l0_S_TM", (128, NT * D), BF16)
        WG = SS("l0_WG", (128, DC * 7), BF16)
        BR = SS("l0_BR", (1, 1024), BF16)
        GC = SS("l0_GC", (128, 4), F32)
        F1 = SS("l0_F1", (1, 1), F32)
        GMU = SS("l0_GMU", (128, NT), F32)
        EP7A = SS("l0_EP7A", (128, 7), F32)
        EP7B = SS("l0_EP7B", (128, 7), F32)
        EP7C = SS("l0_EP7C", (128, 7), F32)
        EPX = SS("l0_EPX", (128, 8), F32)
        D16 = SS("l0_D16", (128, 512), BF16)

        cin = nc.dram_tensor("l0_cin", [128, 514], F32)
        cout = nc.dram_tensor("l0_cout", [1024, 514], F32)

        PA = PS("l0_PA", (128, 512), F32)
        PB = PS("l0_PB", (128, 512), F32)
        PC = PS("l0_PC", (128, 512), F32)
        PD = PS("l0_PD", (128, 512), F32)
        PT = PS("l0_PT", (128, 256), BF16)

        lsA = ExitStack()
        SA = lambda n, sh, dt: lsA.enter_context(nc.sbuf_tensor(n, list(sh), dt))
        WQT = SA("l0_WQT", (128, DC * DM), BF16)
        WKT = SA("l0_WKT", (128, DC * DM), BF16)
        WVT = SA("l0_WVT", (128, DC * DM), BF16)
        WOT = SA("l0_WOT", (128, NDD * D), BF16)
        MEM = SA("l0_MEM", (128, NDD * DM), BF16)
        MN = SA("l0_MN", (128, NDD), BF16)
        KVF = SA("l0_KVF", (128, NDD * DM), F32)
        KVB = SA("l0_KVB", (128, NDD * DM), BF16)
        KCF = SA("l0_KCF", (128, NDD), F32)
        KCB = SA("l0_KCB", (128, NDD), BF16)
        SMT = SA("l0_SMT", (128, 128), BF16)
        CTM = SA("l0_CTM", (128, 256), BF16)
        EP = SA("l0_EP", (1, 512), F32)
        EPC = SA("l0_EPC", (128, 2), F32)
        CFM = SA("l0_CFM", (128, NDD * NTOK), BF16)
        PAY = SA("l0_PAY", (128, 514), F32)
        GB = SA("l0_GB", (128, 514), F32)
        PFXA = SA("l0_PFXA", (128, 514), F32)

        def wload(dst, name):
            kdim, n = W_SHAPES[name]
            c = kdim // 128
            return P.op("pool", lambda e, dst=dst, name=name, c=c: e.dma_start(
                out=dst[:].rearrange("p (c n) -> p c n", c=c),
                in_=dr[name].rearrange("(c p) n -> p c n", p=128)),
                amt=16)
        wload(WQT, "l0_wqT")
        wload(WKT, "l0_wkT")
        wload(WVT, "l0_wvT")
        wload(WOT, "l0_woT")
        wload(MEM, "l0_mem")
        P.op("pool", lambda e: e.dma_start(out=MN[:], in_=dr["l0_mnorm"]), amt=16)
        wload(WG, "l0_wg")
        P.op("pool", lambda e: e.dma_start(out=BR[:], in_=dr["l0_bias"]), amt=16)
        wA = P.op("pool", lambda e: e.dma_start(out=GC[:, 2:3], in_=dr["l0_gate"]),
                  amt=16)

        m0 = mm(P, PC[0:1, 300:301],
                [(MN[:, 0:1], ONESC[:]), (MN[:, 1:2], ONESC[:])],
                waits=[("pool", wA)])
        f0 = P.op("dve", lambda v: v.tensor_scalar(
            F1[:], PC[0:1, 300:301], 1e-6, None, ALU.is_ge), waits=[("pe", m0)])
        f1 = mm(P, PD[:, 2:3], [(ONESRF[:], F1[:])], waits=[("dve", f0)])
        g0 = P.op("act", lambda s: s.activation(GC[:, 3:4], GC[:, 2:3], AF.Sigmoid),
                  waits=[("pool", wA)])
        g1 = P.op("dve", lambda v: v.tensor_mul(GC[:, 0:1], GC[:, 3:4], PD[:, 2:3]),
                  waits=[("pe", f1), ("act", g0)])
        g2 = P.op("dve", lambda v: v.tensor_scalar(
            GC[:, 1:2], GC[:, 3:4], -1.0, 1.0, ALU.mult, ALU.add),
            waits=[("dve", g1)])

        if True:
            SQ = SA("l0_SQ", (128, NDD * NTOK), BF16)
            SK = SA("l0_SK", (128, NDD * NTOK), BF16)
            SKC = SA("l0_SKC", (128, DM), BF16)
            VT = SA("l0_VT", (128, NT * DM), BF16)

            def sigma_fm(dst, wt, pbank):
                for dd in range(NDD):
                    for tt in range(4):
                        t0 = tt * 512
                        pairs = [(wt[:, dc * DM + dd * 128: dc * DM + (dd + 1) * 128],
                                  XNT[:, dc * XSTR + 1 + t0: dc * XSTR + 1 + t0 + 512])
                                 for dc in range(DC)]
                        e0 = mm(P, pbank[:], pairs,
                                waits=[("pool", wA), ("dve", P.count("dve"))])
                        e1 = P.op("dve", lambda v, pb=pbank: v.tensor_scalar(
                            SCR1[:, 0:512], pb[:], 0.0, None, ALU.min),
                            waits=[("pe", e0), ("act", P.count("act"))])
                        e2 = P.op("act", lambda s: s.activation(
                            SCRB[:, 0:512], SCR1[:, 0:512], AF.Exp), waits=[("dve", e1)])
                        dsl = dst[:, dd * NTOK + t0: dd * NTOK + t0 + 512]
                        P.op("dve", lambda v, pb=pbank, dsl=dsl: v.scalar_tensor_tensor(
                            dsl, pb[:], 0.0, SCRB[:, 0:512], ALU.max, ALU.add),
                            waits=[("act", e2)])
            sigma_fm(SQ, WQT, PA)
            sigma_fm(SK, WKT, PB)

            for t in range(NT):
                pairs_v = [(XNT[:, dc * XSTR + 1 + t * 128: dc * XSTR + 1 + (t + 1) * 128],
                            WVT[:, dc * DM:(dc + 1) * DM]) for dc in range(DC)]
                v0 = mm(P, PB[:, 0:256], pairs_v, waits=[("dve", P.count("dve"))])
                P.op("dve", lambda v, t=t: v.tensor_copy(
                    VT[:, t * DM:(t + 1) * DM], PB[:, 0:256]), waits=[("pe", v0)])

            def skc_tp(c, extra_waits=()):
                w0 = list(extra_waits) + [("act", P.count("act"))]
                tpa = P.op("pe", lambda t: t.transpose(
                    PT[:, 0:128], SK[:, 0 * NTOK + c * 128:0 * NTOK + (c + 1) * 128],
                    IDB[:]), waits=w0)
                tpb = P.op("pe", lambda t: t.transpose(
                    PT[:, 128:256], SK[:, 1 * NTOK + c * 128:1 * NTOK + (c + 1) * 128],
                    IDB[:]))
                ca = P.op("act", lambda s: s.copy(SKC[:, 0:128], PT[:, 0:128]),
                          waits=[("pe", tpb)])
                cb = P.op("act", lambda s: s.copy(SKC[:, 128:256], PT[:, 128:256]),
                          waits=[("act", ca)])
                return cb

            u0 = 0
            vt_all = P.count("dve")
            for c in range(NT):
                cb = skc_tp(c)
                for dd in range(NDD):
                    w0 = ([("act", cb)] if dd == 0 else [])
                    if c == 0 and dd == 0:
                        w0.append(("dve", vt_all))
                    u0 = P.op("pe", lambda t, c=c, dd=dd: t.matmul(
                        PD[:, dd * 256:(dd + 1) * 256],
                        SKC[:, dd * 128:(dd + 1) * 128],
                        VT[:, c * DM:(c + 1) * DM],
                        start=(c == 0), stop=(c == NT - 1),
                        skip_group_check=True),
                        waits=w0, noinc=(c < NT - 1))
            P.op("dve", lambda v: v.tensor_copy(PAY[:, 0:512], PD[:, 0:512]),
                 waits=[("pe", u0)])
            for dd in range(NDD):
                P.op("dve", lambda v, dd=dd: v.reduce_sum(
                    PAY[:, 512 + dd:513 + dd],
                    SK[:, dd * NTOK:(dd + 1) * NTOK], axis=AX))
            pay_done = P.count("dve")
            s0 = P.op("sp", lambda e: e.dma_start(out=cin.ap(), in_=PAY[:]),
                      waits=[("dve", pay_done)], amt=16)
            cc0 = P.op("pool", lambda e: e.collective_compute(
                "AllGather", ALU.bypass, replica_groups=[list(range(NCORES))],
                ins=[cin.ap().opt()], outs=[cout.ap().opt()]),
                waits=[("sp", s0)], sem="cc")
            last_d = 0
            for r in range(NCORES):
                gslot = GB[:]
                gw = [("cc", cc0)] if r == 0 else [("dve", last_d)]
                gr = P.op("sp", lambda e, r=r, gslot=gslot: e.dma_start(
                    out=gslot, in_=cout.ap()[r * 128:(r + 1) * 128, :]),
                    waits=gw, amt=16)
                if r == 0:
                    last_d = P.op("dve", lambda v, gslot=gslot: v.tensor_scalar(
                        PFXA[:], gslot, MPREF[:, 0:1], None, ALU.mult),
                        waits=[("sp", gr)])
                else:
                    last_d = P.op("dve",
                                  lambda v, gslot=gslot, r=r:
                                  v.scalar_tensor_tensor(
                                      PFXA[:], gslot, MPREF[:, r:r + 1], PFXA[:],
                                      ALU.mult, ALU.add),
                                  waits=[("sp", gr)])
            PFX = PFXA
            k0 = P.op("dve", lambda v: v.tensor_copy(KVF[:], PFX[:, 0:512]),
                      waits=[("dve", last_d)])
            P.op("dve", lambda v: v.tensor_copy(KVB[:], KVF[:]), waits=[("dve", k0)])
            k2 = P.op("dve", lambda v: v.tensor_copy(KCF[:], PFX[:, 512:514]),
                      waits=[("dve", k0)])
            P.op("dve", lambda v: v.tensor_copy(KCB[:], KCF[:]), waits=[("dve", k2)])

            kv_add = P.count("dve")
            cfm_cp = P.count("act")
            for c in range(NT):
                sqs = [SQ[:, dd * NTOK + c * 128: dd * NTOK + (c + 1) * 128]
                       for dd in range(NDD)]
                sks = [SK[:, dd * NTOK + c * 128: dd * NTOK + (c + 1) * 128]
                       for dd in range(NDD)]
                st0 = mm(P, PA[:, 0:128], [(sks[0], sqs[0]), (sks[1], sqs[1])],
                         waits=[("dve", P.count("dve"))])
                sm0 = P.op("dve", lambda v: v.tensor_mul(
                    SMT[:], PA[:, 0:128], TRI01[:]), waits=[("pe", st0)])
                n0 = mm(P, PB[:, 0:256],
                        [(sqs[0], KVB[:, 0:256]), (sqs[1], KVB[:, 256:512]),
                         (SMT[:], VT[:, c * DM:(c + 1) * DM])],
                        waits=[("dve", sm0)])
                n1 = mm(P, PB[:, 256:512],
                        [(sqs[0], MEM[:, 0:256]), (sqs[1], MEM[:, 256:512])])
                d0_ = mm(P, PC[0:1, 0:128],
                         [(KCB[:, 0:1], sqs[0]), (KCB[:, 1:2], sqs[1]),
                          (ONESC[:], SMT[:])],
                         waits=[("act", P.count("act")), ("dve", P.count("dve"))])
                d1_ = mm(P, PC[0:1, 128:256],
                         [(MN[:, 0:1], sqs[0]), (MN[:, 1:2], sqs[1])])
                e0 = P.op("dve", lambda v: v.tensor_scalar(
                    EP[:, 0:128], PC[0:1, 0:128], 1e-6, None, ALU.max),
                    waits=[("pe", d1_)])
                e1 = P.op("dve", lambda v: v.tensor_scalar(
                    EP[:, 128:256], PC[0:1, 128:256], 1e-6, None, ALU.max),
                    waits=[("dve", e0)])
                e2 = P.op("dve", lambda v: v.reciprocal(EP[:, 256:384], EP[:, 0:128]),
                          waits=[("dve", e1)])
                e3 = P.op("dve", lambda v: v.reciprocal(EP[:, 384:512],
                                                        EP[:, 128:256]),
                          waits=[("dve", e2)])
                t0_ = P.op("pe", lambda t: t.transpose(
                    PC[:, 256:257], EP[0:1, 256:384], IDF[0:1, 0:1]),
                    waits=[("dve", e3)])
                t1_ = P.op("pe", lambda t: t.transpose(
                    PC[:, 257:258], EP[0:1, 384:512], IDF[0:1, 0:1]))
                a0 = P.op("act", lambda s: s.copy(EPC[:, 0:1], PC[:, 256:257]),
                          waits=[("pe", t1_)])
                a1 = P.op("act", lambda s: s.copy(EPC[:, 1:2], PC[:, 257:258]),
                          waits=[("act", a0)])
                q0 = P.op("dve", lambda v: v.tensor_scalar(
                    SCR1[:, 0:256], PB[:, 0:256], EPC[:, 0:1], GC[:, 1:2],
                    ALU.mult, ALU.mult),
                    waits=[("pe", n1), ("act", a1)])
                q1 = P.op("dve", lambda v: v.tensor_scalar(
                    SCR1[:, 256:512], PB[:, 256:512], EPC[:, 1:2], GC[:, 0:1],
                    ALU.mult, ALU.mult), waits=[("dve", q0)])
                q2 = P.op("dve", lambda v: v.tensor_add(
                    CTM[:], SCR1[:, 0:256], SCR1[:, 256:512]),
                    waits=[("dve", q1), ("act", cfm_cp)])
                tp0 = P.op("pe", lambda t: t.transpose(
                    PT[:, 0:128], CTM[:, 0:128], IDB[:]), waits=[("dve", q2)])
                tp1 = P.op("pe", lambda t: t.transpose(
                    PT[:, 128:256], CTM[:, 128:256], IDB[:]))
                for dd in range(NDD):
                    dst = CFM[:, dd * NTOK + c * 128: dd * NTOK + (c + 1) * 128]
                    cfm_cp = P.op("act", lambda s, dst=dst, dd=dd: s.copy(
                        dst, PT[:, dd * 128:(dd + 1) * 128]),
                        waits=([("pe", tp1)] if dd == 0 else []))
                cb = skc_tp(c, extra_waits=[("dve", kv_add)])
                for dd in range(NDD):
                    u0 = mm(P, PD[:, dd * 256:(dd + 1) * 256],
                            [(SKC[:, dd * 128:(dd + 1) * 128],
                              VT[:, c * DM:(c + 1) * DM])],
                            waits=[("act", cb), ("dve", kv_add)] if dd == 0 else ())
                    kv_add = P.op("dve", lambda v, dd=dd: v.scalar_tensor_tensor(
                        KVF[:, dd * 256:(dd + 1) * 256], PD[:, dd * 256:(dd + 1) * 256],
                        1.0, KVF[:, dd * 256:(dd + 1) * 256], ALU.mult, ALU.add),
                        waits=[("pe", u0)])
                    kv_add = P.op("dve", lambda v, dd=dd: v.tensor_copy(
                        KVB[:, dd * 256:(dd + 1) * 256],
                        KVF[:, dd * 256:(dd + 1) * 256]), waits=[("dve", kv_add)])
                    kv_add = P.op("dve", lambda v, c=c, dd=dd: v.reduce_sum(
                        EPX[:, dd:dd + 1],
                        SK[:, dd * NTOK + c * 128:dd * NTOK + (c + 1) * 128], axis=AX),
                        waits=[("dve", kv_add)])
                    kv_add = P.op("dve", lambda v, dd=dd: v.scalar_tensor_tensor(
                        KCF[:, dd:dd + 1], EPX[:, dd:dd + 1], 1.0,
                        KCF[:, dd:dd + 1], ALU.mult, ALU.add),
                        waits=[("dve", kv_add)])
                    kv_add = P.op("dve", lambda v, dd=dd: v.tensor_copy(
                        KCB[:, dd:dd + 1], KCF[:, dd:dd + 1]),
                        waits=[("dve", kv_add)])

            sv = P.count("dve")
            for t in range(NT):
                for h in range(2):
                    wo = mm(P, PA[:, 0:512],
                            [(CFM[:, dd * NTOK + t * 128: dd * NTOK + (t + 1) * 128],
                              WOT[:, dd * D + h * 512: dd * D + h * 512 + 512])
                             for dd in range(NDD)],
                            waits=[("dve", sv), ("act", P.count("act"))])
                    dst = S_TM[:, t * D + h * 512: t * D + h * 512 + 512]
                    sv = P.op("dve", lambda v, dst=dst: v.tensor_copy(dst, PA[:, 0:512]),
                              waits=[("pe", wo)])

        lsA.close()
        P.barrier()
        with ExitStack() as lsB:
            SB_ = lambda n, sh, dt: lsB.enter_context(nc.sbuf_tensor(n, list(sh), dt))
            WSH = SB_("l0_WSH", (128, DC * D), BF16)
            WEX = SB_("l0_WEX", (128, DC * D), BF16)
            wB1 = P.op("pool", lambda e: e.dma_start(
                out=WSH[:].rearrange("p (c n) -> p c n", c=DC),
                in_=dr["l0_wshT"].rearrange("(c p) n -> p c n", p=128)),
                amt=16)
            wB = P.op("pool", lambda e: e.dma_start(
                out=WEX[:].rearrange("p (c n) -> p c n", c=DC),
                in_=dr["l0_wexT"].rearrange("(c p) n -> p c n", p=128)),
                amt=16)

            zprev_pb = P.count("dve")
            zprev_pa = P.count("dve")
            for t in range(NT):
                gm = mm(P, PC[:, 0:7],
                        [(XNT[:, dc * XSTR + 1 + t * 128: dc * XSTR + 1 + (t + 1) * 128],
                          WG[:, dc * 7:(dc + 1) * 7]) for dc in range(DC)],
                        waits=[("dve", P.count("dve"))])
                r0 = P.op("dve", lambda v: v.reduce_max(EPX[:, 0:1], PC[:, 0:7],
                                                        axis=AX), waits=[("pe", gm)])
                r1 = P.op("dve", lambda v: v.tensor_scalar_mul(
                    EPX[:, 1:2], EPX[:, 0:1], -1.0), waits=[("dve", r0)])
                r2 = P.op("act", lambda s: s.activation(
                    EP7A[:], PC[:, 0:7], AF.Exp, bias=EPX[:, 1:2],
                    accum_out=EPX[:, 2:3]), waits=[("dve", r1)])
                r3 = P.op("dve", lambda v: v.reciprocal(EPX[:, 3:4], EPX[:, 2:3]),
                          waits=[("act", r2)])
                r4 = P.op("dve", lambda v: v.tensor_scalar(
                    EP7B[:], EP7A[:], EPX[:, 3:4], None, ALU.mult),
                    waits=[("dve", r3)])
                r5 = P.op("dve", lambda v: v.reduce_max(EPX[:, 4:5], EP7B[:], axis=AX),
                          waits=[("dve", r4)])
                r6 = P.op("dve", lambda v: v.tensor_scalar(
                    EP7A[:], EP7B[:], EPX[:, 4:5], None, ALU.is_ge),
                    waits=[("dve", r5)])
                r7 = P.op("dve", lambda v: v.scalar_tensor_tensor(
                    EP7C[:], EP7A[:], -1e9, EP7B[:], ALU.mult, ALU.add),
                    waits=[("dve", r6)])
                r8 = P.op("dve", lambda v: v.reduce_max(EPX[:, 5:6], EP7C[:], axis=AX),
                          waits=[("dve", r7)])
                r9 = P.op("dve", lambda v: v.tensor_add(EPX[:, 6:7], EPX[:, 4:5],
                                                        EPX[:, 5:6]),
                          waits=[("dve", r8)])
                r10 = P.op("dve", lambda v: v.reciprocal(EPX[:, 7:8], EPX[:, 6:7]),
                           waits=[("dve", r9)])
                r11 = P.op("dve", lambda v, t=t: v.tensor_mul(
                    GMU[:, t:t + 1], EPX[:, 6:7], EPX[:, 7:8]), waits=[("dve", r10)])

                for h in range(2):
                    me = mm(P, PB[:, 0:512],
                            [(XNT[:, dc * XSTR + 1 + t * 128: dc * XSTR + 1 + (t + 1) * 128],
                              WEX[:, dc * D + h * 512: dc * D + h * 512 + 512])
                             for dc in range(DC)],
                            waits=[("pool", wB), ("dve", zprev_pb)])
                    sh = mm(P, PA[:, 0:512],
                            [(XNT[:, dc * XSTR + 1 + t * 128: dc * XSTR + 1 + (t + 1) * 128],
                              WSH[:, dc * D + h * 512: dc * D + h * 512 + 512])
                             for dc in range(DC)] +
                            [(ONESRB[:], BR[0:1, h * 512:h * 512 + 512])],
                            waits=[("dve", zprev_pa)])
                    z0 = P.op("dve", lambda v, t=t: v.tensor_scalar(
                        SCRB[:, 0:512], PB[:, 0:512], GMU[:, t:t + 1], None,
                        ALU.mult),
                        waits=[("pe", me), ("dve", r11), ("act", P.count("act"))])
                    zprev_pb = z0
                    z1 = P.op("dve", lambda v: v.scalar_tensor_tensor(
                        SCR1[:, 0:512], PA[:, 0:512], 1.0, SCRB[:, 0:512],
                        ALU.mult, ALU.add),
                        waits=[("pe", sh), ("dve", z0)])
                    zprev_pa = z1
                    z2 = P.op("act", lambda s: s.activation(
                        SCRB[:, 512:1024], SCR1[:, 0:512], AF.Silu),
                        waits=[("dve", z1)])
                    xsl = X[:, t * D + h * 512: t * D + h * 512 + 512]
                    ssl = S_TM[:, t * D + h * 512: t * D + h * 512 + 512]
                    z3 = P.op("dve", lambda v, ssl=ssl: v.tensor_mul(
                        D16[:], SCRB[:, 512:1024], ssl), waits=[("act", z2)])
                    P.op("dve", lambda v, xsl=xsl: v.tensor_add(xsl, xsl, D16[:]),
                         waits=[("dve", z3)])



LAUNCH_GROUPS = [[0, 1, 2, 3]]


def kernel(**inputs):
    x = np.ascontiguousarray(
        np.asarray(inputs["L0_attn_hidden_states"], dtype=np.float32).reshape(
            NCORES * NTOK, D))
    w = _prep_weights(inputs)
    metas = [_prep_core_meta(c, inputs) for c in range(NCORES)]

    for group in LAUNCH_GROUPS:
        nc, stack = build_program(group)
        in_maps = []
        for c in range(NCORES):
            m = {"x_in": x[c * NTOK:(c + 1) * NTOK]}
            m.update({k: v for k, v in metas[c].items()
                      if (2 in group) or not k.startswith("l2_")})
            for L in group:
                for nm in LAYER_W[L]:
                    if nm in w:
                        m[nm] = w[nm]
            in_maps.append(m)
        res = run_bass_kernel_spmd(nc, in_maps, core_ids=list(range(NCORES)))
        stack.close()
        x = np.ascontiguousarray(
            np.concatenate([res.results[c]["x_out"] for c in range(NCORES)], axis=0))
    return x.reshape(2, 8192, 1024).astype(np.float32)



def emit_l1(P, sb, dr, rms2):
    nc = P.nc
    X, XNT = sb["X"], sb["XNT"]
    IDF, IDB = sb["IDF"], sb["IDB"]
    ONESC, ONESF, ONESRB, ONESRF = (sb["ONESC"], sb["ONESF"], sb["ONESRB"],
                                    sb["ONESRF"])
    MPREF, MPREFI = sb["MPREF"], sb["MPREFI"]
    SCR1, SCRB = sb["SCR1"], sb["SCRB"]
    NI = 2048
    NIC = NI // 128

    with ExitStack() as ls:
        SS = lambda n, sh, dt: ls.enter_context(nc.sbuf_tensor(n, list(sh), dt))
        PS = lambda n, sh, dt: ls.enter_context(nc.psum_tensor(n, list(sh), dt))

        BIN = SS("l1_BIN", (128, 32), F32)
        BXP = SS("l1_BXP", (64, 1), F32)
        NA = SS("l1_NA", (64, 1), F32)
        MDH = SS("l1_MDH", (128, 2 * NT), F32)
        MDT = SS("l1_MDT", (128, NT), F32)
        MDR16 = SS("l1_MDR16", (16, 128), F32)
        WXP = SS("l1_WXP", (128, NIC * 64), BF16)
        DIAGD = SS("l1_DIAGD", (128, DC * 128), BF16)
        DCOL = SS("l1_DCOL", (128, 8), F32)
        PAY2 = SS("l1_PAY2", (64, 2), F32)
        GB2 = SS("l1_GB2", (64, 2), F32)
        SINIT = SS("l1_SINIT", (64, 1), F32)
        TMP64 = SS("l1_TMP64", (64, 2), F32)

        md_dram = nc.dram_tensor("l1_md", [NTOK], F32)
        s_dram = nc.dram_tensor("l1_sdram", [NTOK, D], BF16)
        cin1 = nc.dram_tensor("l1_cin", [64, 2], F32)
        cout1 = nc.dram_tensor("l1_cout", [512, 2], F32)


        P.op("pool", lambda e: e.dma_start(out=BIN[:], in_=dr["l1_bin"]), amt=16)
        P.op("pool", lambda e: e.dma_start(out=BXP[:], in_=dr["l1_bxp"]), amt=16)
        P.op("pool", lambda e: e.dma_start(out=DCOL[:], in_=dr["l1_D"]), amt=16)
        P.op("pool", lambda e: e.dma_start(
            out=WXP[:].rearrange("p (c n) -> p c n", c=NIC),
            in_=dr["l1_wxp"].rearrange("(c p) n -> p c n", p=128)), amt=16)
        al = P.op("pool", lambda e: e.dma_start(out=NA[:], in_=dr["l1_alog"]),
                  amt=16)
        sml = P.count("pool")
        na0 = P.op("act", lambda s: s.activation(NA[:], NA[:], AF.Exp),
                   waits=[("pool", sml)])
        na1 = P.op("dve", lambda v: v.tensor_scalar_mul(NA[:], NA[:], -1.0),
                   waits=[("act", na0)])
        for dc in range(DC):
            P.op("dve", lambda v, dc=dc: v.tensor_scalar(
                DIAGD[:, dc * 128:(dc + 1) * 128], IDB[:], DCOL[:, dc:dc + 1],
                None, ALU.mult), waits=[("pool", sml)])
        sini = P.op("pool", lambda e: e.memset(SINIT[:], 0.0))

        with ExitStack() as lsA:
            SA = lambda n, sh, dt: lsA.enter_context(nc.sbuf_tensor(n, list(sh), dt))
            XIT = SA("l1_XIT", (128, NIC * NTOK), BF16)
            SSB = SA("l1_SSB", (128, NTOK), BF16)
            PJ = lsA.enter_context(nc.psum_tensor("l1_PJ", [128, 2048], F32))
            PSC = lsA.enter_context(nc.psum_tensor("l1_PSC", [128, 512], F32))
            PSD = lsA.enter_context(nc.psum_tensor("l1_PSD", [128, 512], F32))

            with ExitStack() as ls1:
                WINJ = ls1.enter_context(nc.sbuf_tensor("l1_WINJ", [128, 2 * 1024], BF16))
                ev = P.count("act")
                wj_prev = 0
                for j in range(NIC):
                    slot = WINJ[:, (j % 2) * 1024:(j % 2) * 1024 + 1024]
                    wj = P.op("pool", lambda e, j=j, slot=slot: e.dma_start(
                        out=slot.rearrange("p (c n) -> p c n", c=DC),
                        in_=dr["l1_win"][:, j * 128:(j + 1) * 128].rearrange(
                            "(c p) n -> p c n", p=128)),
                        waits=[("pe", wj_prev)], amt=16)
                    for tt in range(4):
                        pairs = [(slot[:, dc * 128:(dc + 1) * 128],
                                  XNT[:, dc * XSTR + 1 + tt * 512: dc * XSTR + 1 + tt * 512 + 512])
                                 for dc in range(DC)]
                        wj_prev = mm(P, PJ[:, tt * 512:(tt + 1) * 512], pairs,
                                     waits=[("pool", wj), ("act", ev)])
                    for tt in range(4):
                        dst = XIT[:, j * NTOK + tt * 512: j * NTOK + tt * 512 + 512]
                        ev = P.op("act", lambda s, dst=dst, tt=tt, j=j: s.activation(
                            dst, PJ[:, tt * 512:(tt + 1) * 512], AF.Silu,
                            bias=BIN[:, j:j + 1]),
                            waits=([("pe", wj_prev)] if tt == 0 else []))

            P.barrier()
            with ExitStack() as lsD:
                WDTH = lsD.enter_context(nc.sbuf_tensor("l1_WDTH", [128, NIC * 512], BF16))
                BDTR = lsD.enter_context(nc.sbuf_tensor("l1_BDTR", [1, 1024], BF16))
                bd = P.op("pool", lambda e: e.dma_start(out=BDTR[:], in_=dr["l1_bdt"]),
                          amt=16)
                P.waits("pe", [("pool", bd)])
                for h in range(2):
                    wd = P.op("pool", lambda e, h=h: e.dma_start(
                        out=WDTH[:].rearrange("p (c n) -> p c n", c=NIC),
                        in_=dr["l1_wdt"][:, h * 512:(h + 1) * 512].rearrange(
                            "(c p) n -> p c n", p=128)),
                        waits=[("pe", P.count("pe"))], amt=16)
                    dprev = P.count("dve")
                    for t in range(NT):
                        pairs = [(XIT[:, k * NTOK + t * 128: k * NTOK + (t + 1) * 128],
                                  WDTH[:, k * 512:(k + 1) * 512]) for k in range(NIC)]
                        pairs.append((ONESRB[:], BDTR[0:1, h * 512:h * 512 + 512]))
                        dm = mm(P, PSC[:], pairs,
                                waits=[("pool", wd), ("act", P.count("act")),
                                       ("dve", dprev)])
                        ab = P.op("act", lambda s: s.activation(
                            SCR1[:, 0:512], PSC[:], AF.Abs), waits=[("pe", dm)])
                        ex = P.op("act", lambda s: s.activation(
                            SCR1[:, 512:1024], SCR1[:, 0:512], AF.Exp, scale=-1.0),
                            waits=[("act", ab)])
                        ln = P.op("act", lambda s: s.activation(
                            SCRB[:, 0:512], SCR1[:, 512:1024], AF.Ln,
                            bias=sb["ONE1"][:, 0:1]), waits=[("act", ex)])
                        dprev = P.op("dve", lambda v, h=h, t=t: v.scalar_tensor_tensor(
                            SCRB[:, 512:1024], PSC[:], 0.0, SCRB[:, 0:512],
                            ALU.max, ALU.add,
                            accum_out=MDH[:, h * NT + t:h * NT + t + 1]),
                            waits=[("act", ln)])
            md0 = P.op("dve", lambda v: v.tensor_add(
                MDT[:], MDH[:, 0:NT], MDH[:, NT:2 * NT]),
                waits=[("dve", P.count("dve"))])
            md1 = P.op("dve", lambda v: v.tensor_scalar_mul(MDT[:], MDT[:], 1.0 / 1024),
                       waits=[("dve", md0)])
            tp0 = P.op("pe", lambda t: t.transpose(PSD[0:16, 0:128], MDT[:], IDF[:]),
                       waits=[("dve", md1)])
            cp0 = P.op("act", lambda s: s.copy(MDR16[:], PSD[0:16, 0:128]),
                       waits=[("pe", tp0)])
            dm0 = P.op("sp", lambda e: e.dma_start(
                out=md_dram.ap().rearrange("(t p) -> t p", p=128), in_=MDR16[:]),
                waits=[("act", cp0)], amt=16)

            P.barrier()
            with ExitStack() as lsS:
                S2 = lambda n, sh, dt: lsS.enter_context(nc.sbuf_tensor(n, list(sh), dt))
                A_SC = S2("l1_ASC", (64, NTOK), BF16)
                BX = S2("l1_BX", (64, NTOK), F32)
                MDROW = S2("l1_MDROW", (1, NTOK), F32)
                ROW5 = S2("l1_ROW5", (1, 512), F32)
                ROW5B = S2("l1_ROW5B", (1, 512), F32)
                SINB = S2("l1_SINB", (64, 1), BF16)
                dm1 = P.op("sp", lambda e: e.dma_start(
                    out=MDROW[:], in_=md_dram.ap().rearrange("(a b) -> a b", a=1)),
                    waits=[("sp", dm0)], amt=16)
                for tt in range(4):
                    b0 = mm(P, PSC[0:64, 0:512],
                            [(ONESRF[0:1, 0:64], MDROW[0:1, tt * 512:tt * 512 + 512])],
                            waits=[("sp", dm1), ("dve", na1),
                                   ("act", P.count("act"))])
                    P.op("act", lambda s, tt=tt: s.activation(
                        A_SC[:, tt * 512:tt * 512 + 512], PSC[0:64, 0:512], AF.Exp,
                        scale=NA[:, 0:1]), waits=[("pe", b0)])
                for tt in range(4):
                    pairs = [(WXP[:, k * 64:(k + 1) * 64],
                              XIT[:, k * NTOK + tt * 512: k * NTOK + tt * 512 + 512])
                             for k in range(NIC)]
                    b1 = mm(P, PSD[0:64, 0:512], pairs,
                            waits=[("dve", P.count("dve")), ("pool", sml)])
                    P.op("dve", lambda v, tt=tt: v.tensor_scalar(
                        BX[:, tt * 512:tt * 512 + 512], PSD[0:64, 0:512],
                        BXP[:, 0:1], None, ALU.add),
                        waits=[("pe", b1)])
                ac0 = P.op("dve", lambda v: v.reduce_sum(
                    TMP64[0:1, 0:1], MDROW[:], axis=AX),
                    waits=[("sp", dm1), ("dve", P.count("dve"))])
                ac1 = mm(P, PSC[0:64, 0:1], [(ONESRF[0:1, 0:64], TMP64[0:1, 0:1])],
                         waits=[("dve", ac0), ("act", P.count("act"))])
                ac2 = P.op("dve", lambda v: v.tensor_mul(
                    TMP64[:, 0:1], NA[:], PSC[0:64, 0:1]), waits=[("pe", ac1)])
                ac3 = P.op("act", lambda s: s.activation(
                    PAY2[:, 0:1], TMP64[:, 0:1], AF.Exp), waits=[("dve", ac2)])
                sc0 = P.op("dve", lambda v: v.tensor_tensor_scan(
                    BX[:], A_SC[:], BX[:], 0.0, ALU.mult, ALU.add),
                    waits=[("act", P.count("act")), ("dve", P.count("dve"))])
                pb0 = P.op("dve", lambda v: v.tensor_copy(
                    PAY2[:, 1:2], BX[:, NTOK - 1:NTOK]), waits=[("dve", sc0)])
                cs0 = P.op("sp", lambda e: e.dma_start(out=cin1.ap(), in_=PAY2[:]),
                           waits=[("act", ac3), ("dve", P.count("dve"))], amt=16)
                cc1 = P.op("pool", lambda e: e.collective_compute(
                    "AllGather", ALU.bypass, replica_groups=[list(range(NCORES))],
                    ins=[cin1.ap().opt()], outs=[cout1.ap().opt()]),
                    waits=[("sp", cs0)], sem="cc")
                gprev = 0
                for r in range(NCORES):
                    gr = P.op("sp", lambda e, r=r: e.dma_start(
                        out=GB2[:], in_=cout1.ap()[r * 64:(r + 1) * 64, :]),
                        waits=([("cc", P.count("cc"))] if r == 0 else
                               [("dve", gprev)]), amt=16)
                    c0_ = P.op("dve", lambda v: v.tensor_mul(
                        TMP64[:, 0:1], GB2[:, 0:1], SINIT[:]),
                        waits=[("sp", gr), ("poolc", sini)])
                    c1_ = P.op("dve", lambda v: v.tensor_add(
                        TMP64[:, 0:1], TMP64[:, 0:1], GB2[:, 1:2]),
                        waits=[("dve", c0_)])
                    c2_ = P.op("dve", lambda v, r=r: v.tensor_scalar(
                        TMP64[:, 1:2], SINIT[:], MPREFI[0:64, r:r + 1], None,
                        ALU.mult), waits=[("dve", c1_)])
                    gprev = P.op("dve", lambda v, r=r: v.scalar_tensor_tensor(
                        SINIT[:], TMP64[:, 0:1], MPREF[0:64, r:r + 1],
                        TMP64[:, 1:2], ALU.mult, ALU.add), waits=[("dve", c2_)])
                cm0 = P.op("dve", lambda v: v.tensor_tensor_scan(
                    MDROW[:], sb["ONE2K"][:], MDROW[:], 0.0, ALU.mult, ALU.add),
                    waits=[("dve", P.count("dve"))])
                sb0 = P.op("dve", lambda v: v.tensor_copy(SINB[:], SINIT[:]),
                           waits=[("dve", gprev)])
                for tt in range(4):
                    s0 = mm(P, PSD[0:1, 0:512],
                            [(ONESF[0:64, 0:1], BX[:, tt * 512:tt * 512 + 512])],
                            waits=[("dve", P.count("dve")),
                                   ("act", P.count("act"))])
                    p0 = mm(P, PSC[0:64, 0:512],
                            [(ONESRF[0:1, 0:64], MDROW[0:1, tt * 512:tt * 512 + 512])],
                            waits=[("dve", cm0), ("act", P.count("act"))])
                    p1 = P.op("act", lambda s: s.activation(
                        A_SC[:, 0:512], PSC[0:64, 0:512], AF.Exp,
                        scale=NA[:, 0:1]), waits=[("pe", p0)])
                    p2 = mm(P, PJ[0:1, 0:512], [(SINB[:], A_SC[0:64, 0:512])],
                            waits=[("act", p1), ("dve", sb0)])
                    p3 = P.op("act", lambda s: s.copy(ROW5B[:], PJ[0:1, 0:512]),
                              waits=[("pe", p2)])
                    p4 = P.op("dve", lambda v: v.tensor_add(
                        ROW5[:], PSD[0:1, 0:512], ROW5B[:]),
                        waits=[("act", p3), ("pe", s0)])
                    q0 = mm(P, PSC[:, 0:512], [(ONESRF[:], ROW5[:])],
                            waits=[("dve", p4), ("act", P.count("act"))])
                    P.op("act", lambda s, tt=tt: s.copy(
                        SSB[:, tt * 512:tt * 512 + 512], PSC[:, 0:512]),
                        waits=[("pe", q0)])

            P.barrier()
            with ExitStack() as lsR:
                WINJ = lsR.enter_context(nc.sbuf_tensor("l1_WINJ2", [128, 2 * 1024], BF16))
                RSJ = lsR.enter_context(nc.sbuf_tensor("l1_RSJ", [128, NTOK], BF16))
                wj_prev = 0
                rv = P.count("dve")
                for j in range(NIC):
                    slot = WINJ[:, (j % 2) * 1024:(j % 2) * 1024 + 1024]
                    wj = P.op("pool", lambda e, j=j, slot=slot: e.dma_start(
                        out=slot.rearrange("p (c n) -> p c n", c=DC),
                        in_=dr["l1_win"][:, (16 + j) * 128:(17 + j) * 128].rearrange(
                            "(c p) n -> p c n", p=128)),
                        waits=[("pe", wj_prev)], amt=16)
                    for tt in range(4):
                        pairs = [(slot[:, dc * 128:(dc + 1) * 128],
                                  XNT[:, dc * XSTR + 1 + tt * 512: dc * XSTR + 1 + tt * 512 + 512])
                                 for dc in range(DC)]
                        wj_prev = mm(P, PJ[:, tt * 512:(tt + 1) * 512], pairs,
                                     waits=[("pool", wj), ("act", P.count("act")),
                                            ("dve", rv)])
                    for tt in range(4):
                        P.op("act", lambda s, tt=tt, j=j: s.activation(
                            RSJ[:, tt * 512:tt * 512 + 512],
                            PJ[:, tt * 512:(tt + 1) * 512], AF.Silu,
                            bias=BIN[:, 16 + j:17 + j]),
                            waits=([("pe", wj_prev)] if tt == 0 else []))
                    r0 = P.op("dve", lambda v, j=j: v.tensor_mul(
                        XIT[:, j * NTOK:(j + 1) * NTOK],
                        XIT[:, j * NTOK:(j + 1) * NTOK], RSJ[:]),
                        waits=[("act", P.count("act"))])
                    rv = P.op("dve", lambda v, j=j: v.tensor_mul(
                        XIT[:, j * NTOK:(j + 1) * NTOK],
                        XIT[:, j * NTOK:(j + 1) * NTOK], SSB[:]),
                        waits=[("dve", r0)])

            P.barrier()
            with ExitStack() as lsW:
                WOUTH = lsW.enter_context(
                    nc.sbuf_tensor("l1_WOUTH", [128, NIC * 512], BF16))
                BOUTR = lsW.enter_context(
                    nc.sbuf_tensor("l1_BOUTR", [1, 1024], BF16))
                SOUT = lsW.enter_context(
                    nc.sbuf_tensor("l1_SOUT", [128, 1024], BF16))
                DGB = lsW.enter_context(
                    nc.sbuf_tensor("l1_DGB", [128, 512], BF16))
                bo = P.op("pool", lambda e: e.dma_start(out=BOUTR[:], in_=dr["l1_bout"]),
                          amt=16)
                P.waits("pe", [("pool", bo)])
                for h in range(2):
                    wh = P.op("pool", lambda e, h=h: e.dma_start(
                        out=WOUTH[:].rearrange("p (c n) -> p c n", c=NIC),
                        in_=dr["l1_wout"][:, h * 512:(h + 1) * 512].rearrange(
                            "(c p) n -> p c n", p=128)),
                        waits=[("pe", P.count("pe"))], amt=16)
                    sv = P.count("dve")
                    dgv = P.count("act")
                    for t in range(NT):
                        w0 = [("pool", wh), ("dve", sv)]
                        for k in range(NIC):
                            P.op("pe", lambda t_, k=k, t=t: t_.matmul(
                                PSC[:], XIT[:, k * NTOK + t * 128: k * NTOK + (t + 1) * 128],
                                WOUTH[:, k * 512:(k + 1) * 512],
                                start=(k == 0), stop=False),
                                waits=(w0 if k == 0 else ()), noinc=True)
                        wlast = P.op("pe", lambda t_, h=h: t_.matmul(
                            PSC[:], ONESRB[:], BOUTR[0:1, h * 512:h * 512 + 512],
                            start=False, stop=True))
                        for i, dc in enumerate(range(4 * h, 4 * h + 4)):
                            wd0 = [("act", dgv)] if i == 0 else []
                            P.op("pe", lambda t_, dc=dc, i=i, t=t: t_.matmul(
                                PSD[:, i * 128:(i + 1) * 128],
                                XNT[:, dc * XSTR + 1 + t * 128: dc * XSTR + 1 + (t + 1) * 128],
                                DIAGD[:, dc * 128:(dc + 1) * 128],
                                start=True, stop=True), waits=wd0, noinc=(i < 3))
                        dlast = P.count("pe")
                        dgv = P.op("act", lambda s: s.copy(DGB[:], PSD[:]),
                                   waits=[("pe", dlast)])
                        slot = SOUT[:, (t % 2) * 512:(t % 2) * 512 + 512]
                        sv = P.op("dve", lambda v, slot=slot: v.scalar_tensor_tensor(
                            slot, PSC[:], 1.0, DGB[:], ALU.mult, ALU.add),
                            waits=[("pe", wlast), ("act", dgv),
                                   ("sp", P.count("sp"))])
                        P.op("sp", lambda e, t=t, h=h, slot=slot: e.dma_start(
                            out=s_dram.ap()[t * 128:(t + 1) * 128,
                                            h * 512:h * 512 + 512],
                            in_=slot), waits=[("dve", sv)], amt=16)

    P.barrier()
    emit_l1_ffn(P, sb, dr, rms2, s_dram, md_dram)


def emit_l1_ffn(P, sb, dr, rms2, s_dram, md_dram):
    nc = P.nc
    X, XNT = sb["X"], sb["XNT"]
    IDF, IDB = sb["IDF"], sb["IDB"]
    ONESRB, ONESRF = sb["ONESRB"], sb["ONESRF"]
    SCR1, SCRB = sb["SCR1"], sb["SCRB"]

    x_spill = nc.dram_tensor("l1_xspill", [NTOK, D], F32)

    with ExitStack() as lf:
        SF = lambda n, sh, dt: lf.enter_context(nc.sbuf_tensor(n, list(sh), dt))

        NW = SF("l1_NW", (128, 8), F32)
        RW = SF("l1_RW", (128, DC * 8), BF16)
        SB1 = SF("l1_SB1", (128, 32), F32)
        EB1 = SF("l1_EB1", (128, 256), F32)
        SB2R = SF("l1_SB2R", (1, 1024), BF16)
        EB2 = SF("l1_EB2", (8, 1024), BF16)
        COMB = SF("l1_COMB", (128, 8), BF16)
        P8F = SF("l1_P8F", (128, 8), F32)
        GE8 = SF("l1_GE8", (128, 8), F32)
        EPX = SF("l1_EPXF", (128, 8), F32)
        CF = SF("l1_CF", (8, 1024), BF16)
        CBA = SF("l1_CBA", (128, 64), F32)
        OA = SF("l1_OA", (128, 8 * 1024), F32)
        HGRP = SF("l1_HGRP", (128, 8 * 1024), BF16)
        EW1G = SF("l1_EW1G", (128, 8 * 1024), BF16)
        EW2G = SF("l1_EW2G", (128, 8 * 1024), BF16)
        SIN2 = SF("l1_SIN2", (128, 1024), BF16)

        ld0 = P.op("pool", lambda e: e.dma_start(out=NW[:], in_=dr["l1_nw"]), amt=16)
        P.op("pool", lambda e: e.dma_start(
            out=RW[:].rearrange("p (c n) -> p c n", c=DC),
            in_=dr["l1_rwT"].rearrange("(c p) n -> p c n", p=128)), amt=16)
        P.op("pool", lambda e: e.dma_start(out=SB1[:], in_=dr["l1_sb1"]), amt=16)
        P.op("pool", lambda e: e.dma_start(out=EB1[:], in_=dr["l1_eb1"]), amt=16)
        P.op("pool", lambda e: e.dma_start(out=SB2R[:], in_=dr["l1_sb2"]), amt=16)
        ld1 = P.op("pool", lambda e: e.dma_start(out=EB2[:], in_=dr["l1_eb2"]), amt=16)
        xs = P.op("sp", lambda e: e.dma_start(
            out=x_spill.ap().rearrange("(t p) d -> p t d", p=128),
            in_=X[:].rearrange("p (t d) -> p t d", t=NT)),
            waits=[("dve", P.count("dve")), ("act", P.count("act"))], amt=16)

        with ExitStack() as lr:
            RB16 = lr.enter_context(nc.sbuf_tensor("l1_RB16", [16, 128], F32))
            RSROW = lr.enter_context(nc.sbuf_tensor("l1_RSROW", [1, NTOK], BF16))
            PR0 = lr.enter_context(nc.psum_tensor("l1_PR0", [128, 512], F32))
            rs_dram = nc.dram_tensor("l1_rs", [NTOK], F32)
            tp = P.op("pe", lambda t: t.transpose(PR0[0:16, 0:128], rms2[:], IDF[:]),
                      waits=[("dve", P.count("dve")), ("act", P.count("act"))])
            cp = P.op("act", lambda s: s.copy(RB16[:], PR0[0:16, 0:128]),
                      waits=[("pe", tp)])
            d0 = P.op("sp", lambda e: e.dma_start(
                out=rs_dram.ap().rearrange("(t p) -> t p", p=128), in_=RB16[:]),
                waits=[("act", cp)], amt=16)
            d1 = P.op("pool", lambda e: e.dma_start(
                out=RSROW[:], in_=rs_dram.ap().rearrange("(a b) -> a b", a=1)),
                waits=[("sp", d0)], amt=16)
            nprev = P.count("dve")
            for tt in range(4):
                q0 = mm(P, PR0[:, 0:512],
                        [(ONESRB[:], RSROW[0:1, tt * 512:tt * 512 + 512])],
                        waits=[("pool", P.count("pool")), ("dve", nprev)])
                for dc in range(DC):
                    sl = XNT[:, dc * XSTR + 1 + tt * 512: dc * XSTR + 1 + tt * 512 + 512]
                    nprev = P.op("dve", lambda v, sl=sl, dc=dc: v.scalar_tensor_tensor(
                        sl, sl, NW[:, dc:dc + 1], PR0[:, 0:512], ALU.mult, ALU.mult),
                        waits=([("pe", q0), ("pool", ld0)] if dc == 0 else
                               [("dve", nprev)]))
            P.barrier()

        PRX = lf.enter_context(nc.psum_tensor("l1_PRX", [128, 512], F32))
        PB2 = sb["PLN"]
        elist = list(range(8)) + ["sh"]
        steps = [(e_, g) for e_ in elist for g in range(4)]

        semn = lambda e_: "castsh" if e_ == "sh" else f"cast{e_ % 2}"

        def _cast_chunk(e_, g, which):
            if which == 1:
                src = dr["l1_sw1f"][g] if e_ == "sh" else dr["l1_ew1f"][e_, g]
                dst = dr["l1_wbs1"][g] if e_ == "sh" else dr["l1_wb1"][e_, g]
            else:
                src = dr["l1_sw2f"][g] if e_ == "sh" else dr["l1_ew2f"][e_, g]
                dst = dr["l1_wbs2"][g] if e_ == "sh" else dr["l1_wb2"][e_, g]
            return P.op("pool", lambda e, dst=dst, src=src: e.dma_start(
                out=dst, in_=src), amt=16, sem=semn(e_))

        cast_done = {}
        for half in range(2):
            ho = half * 1024
            if half == 0:
                for wch in (1, 2):
                    for gg in range(4):
                        lastc = _cast_chunk(0, gg, wch)
                cast_done[0] = lastc

            def wload1h(e_, g, jh, extra=()):
                src = (dr["l1_wbs1"][g, jh] if e_ == "sh"
                       else dr["l1_wb1"][e_, g, jh])
                return P.op("pool", lambda e, src=src, jh=jh: e.dma_start(
                    out=EW1G[:, jh * 4096:(jh + 1) * 4096], in_=src),
                    waits=list(extra) + [("pe", ew1_rd[jh])], amt=16,
                    sem=f"ew1s{jh}")

            def wload2h(e_, g, hh, extra=()):
                src = (dr["l1_wbs2"][g, hh] if e_ == "sh"
                       else dr["l1_wb2"][e_, g, hh])
                return P.op("pool", lambda e, src=src, hh=hh: e.dma_start(
                    out=EW2G[:, hh * 4096:(hh + 1) * 4096], in_=src),
                    waits=list(extra) + [("pe", ew2_rd[hh])], amt=16,
                    sem=f"ew2s{hh}")

            def cwait(e_):
                return [(semn(e_), cast_done[e_])] if half == 0 else []

            l1s = [[None, None] for _ in steps]
            l2s = [[None, None] for _ in steps]
            rel = prev_pe if half == 1 else 0
            ew1_rd = [rel, rel]
            ew2_rd = [rel, rel]
            l1s[0][0] = wload1h(steps[0][0], steps[0][1], 0, extra=cwait(0))
            l1s[0][1] = wload1h(steps[0][0], steps[0][1], 1, extra=cwait(0))
            l2s[0][0] = wload2h(steps[0][0], steps[0][1], 0, extra=cwait(0))
            l2s[0][1] = wload2h(steps[0][0], steps[0][1], 1, extra=cwait(0))

            if True:
                PR = PRX
                if half == 0:
                    rprev, acop = P.count("dve"), P.count("act")
                else:
                    rprev, acop = exp_end
                for t8 in range(8):
                    toff = ho + t8 * 128
                    pairs = [(XNT[:, dc * XSTR + 1 + toff: dc * XSTR + 1 + toff + 128],
                              RW[:, dc * 8:(dc + 1) * 8]) for dc in range(DC)]
                    g0 = mm(P, PR[:, 0:8], pairs, waits=[("dve", rprev)])
                    a0 = P.op("act", lambda s: s.activation(P8F[:], PR[:, 0:8],
                                                            AF.Sigmoid),
                              waits=[("pe", g0)])
                    m0 = P.op("dve", lambda v: v.reduce_max(EPX[:, 0:1], P8F[:],
                                                            axis=AX),
                              waits=[("act", a0)])
                    m1 = P.op("dve", lambda v: v.tensor_scalar(
                        GE8[:], P8F[:], EPX[:, 0:1], None, ALU.is_ge),
                        waits=[("dve", m0)])
                    m2 = P.op("dve", lambda v: v.scalar_tensor_tensor(
                        GE8[:], GE8[:], -1e9, P8F[:], ALU.mult, ALU.add),
                        waits=[("dve", m1)])
                    m3 = P.op("dve", lambda v: v.reduce_max(EPX[:, 1:2], GE8[:],
                                                            axis=AX),
                              waits=[("dve", m2)])
                    m4 = P.op("dve", lambda v: v.tensor_add(EPX[:, 2:3], EPX[:, 0:1],
                                                            EPX[:, 1:2]),
                              waits=[("dve", m3)])
                    m5 = P.op("dve", lambda v: v.tensor_scalar_add(
                        EPX[:, 2:3], EPX[:, 2:3], 1e-6), waits=[("dve", m4)])
                    m6 = P.op("dve", lambda v: v.reciprocal(EPX[:, 3:4], EPX[:, 2:3]),
                              waits=[("dve", m5)])
                    m7 = P.op("dve", lambda v: v.tensor_scalar(
                        GE8[:], P8F[:], EPX[:, 1:2], None, ALU.is_ge),
                        waits=[("dve", m6)])
                    m8 = P.op("dve", lambda v: v.tensor_scalar(
                        P8F[:], P8F[:], EPX[:, 3:4], None, ALU.mult),
                        waits=[("dve", m7)])
                    m9 = P.op("dve", lambda v: v.tensor_mul(COMB[:], P8F[:], GE8[:]),
                              waits=[("dve", m8)])
                    m9b = P.op("dve", lambda v, t8=t8: v.tensor_mul(
                        CBA[:, t8 * 8:(t8 + 1) * 8], P8F[:], GE8[:]),
                        waits=[("dve", m9)])
                    tq = P.op("pe", lambda t_: t_.transpose(
                        PB2[0:8, 0:128], COMB[:], IDB[:]),
                        waits=[("dve", m9), ("act", acop)])
                    acop = P.op("act", lambda s, t8=t8: s.copy(
                        CF[:, t8 * 128:(t8 + 1) * 128], PB2[0:8, 0:128]),
                        waits=[("pe", tq)])
                    rprev = m9b
                cfdone = P.count("act")

            with ExitStack() as lxp:
                PH = lxp.enter_context(nc.psum_tensor(f"l1_PH{half}", [128, 2 * 1024], F32))
                PO = lxp.enter_context(nc.psum_tensor(f"l1_PO{half}", [128, 2 * 512], F32))
                PRC = PRX

                phslot_rd = [0, 0]
                poslot_rd = [0, 0]
                oa_init = [False] * 16
                operv = P.count("dve")
                pend = []
                ctgt = None
                for si, (e_, g) in enumerate(steps):
                    if half == 0 and g == 0:
                        ctgt = ((e_ + 1) if isinstance(e_, int) and e_ < 7
                                else ("sh" if e_ == 7 else None))
                        pend = ([(gg, wch) for wch in (1, 2)
                                 for gg in range(4)] if ctgt is not None else [])
                    hbase = 0
                    for jj in range(8):
                        j = g * 8 + jj
                        ph = PH[:, (jj % 2) * 1024:(jj % 2) * 1024 + 1024]
                        for q in range(2):
                            w = ()
                            if q == 0:
                                w = [("act", phslot_rd[jj % 2])]
                                if jj == 0:
                                    w.append(("ew1s0", l1s[si][0]))
                                if jj == 4:
                                    w.append(("ew1s1", l1s[si][1]))
                            pq = [(EW1G[:, (jj // 4) * 4096 + dc * 512 +
                                        (jj % 4) * 128:
                                        (jj // 4) * 4096 + dc * 512 +
                                        (jj % 4) * 128 + 128],
                                   XNT[:, dc * XSTR + 1 + ho + q * 512:
                                       dc * XSTR + 1 + ho + q * 512 + 512])
                                  for dc in range(DC)]
                            h0 = mm(P, ph[:, q * 512:(q + 1) * 512], pq, waits=w)
                        bias = (SB1[:, j:j + 1] if e_ == "sh"
                                else EB1[:, e_ * 32 + j:e_ * 32 + j + 1])
                        ha = P.op("act", lambda s, jj=jj, bias=bias, ph=ph, hbase=hbase:
                                  s.activation(
                            HGRP[:, hbase + jj * 1024: hbase + (jj + 1) * 1024],
                            ph, AF.Relu, bias=bias), waits=[("pe", h0)])
                        phslot_rd[jj % 2] = ha
                        if jj == 3:
                            ew1_rd[0] = P.count("pe")
                            if si + 1 < len(steps):
                                l1s[si + 1][0] = wload1h(
                                    steps[si + 1][0], steps[si + 1][1], 0,
                                    extra=cwait(steps[si + 1][0]))
                        if jj == 7:
                            ew1_rd[1] = P.count("pe")
                            if si + 1 < len(steps):
                                l1s[si + 1][1] = wload1h(
                                    steps[si + 1][0], steps[si + 1][1], 1,
                                    extra=cwait(steps[si + 1][0]))
                    hall_act = P.count("act")
                    hall_dve = P.count("dve")
                    for hh in range(2):
                        for t8 in range(8):
                            po = PO[:, (t8 % 2) * 512:(t8 % 2) * 512 + 512]
                            pairs = [(HGRP[:, hbase + jj * 1024 + t8 * 128:
                                          hbase + jj * 1024 + (t8 + 1) * 128],
                                      EW2G[:, hh * 4096 + jj * 512:
                                           hh * 4096 + jj * 512 + 512])
                                     for jj in range(8)]
                            if e_ == "sh" and g == 0:
                                pairs.append((CF[:, t8 * 128:(t8 + 1) * 128],
                                              EB2[:, hh * 512:hh * 512 + 512]))
                                pairs.append((ONESRB[:],
                                              SB2R[0:1, hh * 512:hh * 512 + 512]))
                            o0 = mm(P, po, pairs,
                                    waits=[("act", hall_act), ("dve", hall_dve),
                                           (f"ew2s{hh}", l2s[si][hh]),
                                           ("dve", poslot_rd[t8 % 2])])
                            osl = OA[:, t8 * 1024 + hh * 512: t8 * 1024 + hh * 512 + 512]
                            idx = t8 * 2 + hh
                            cb = (None if e_ == "sh"
                                  else CBA[:, t8 * 8 + e_: t8 * 8 + e_ + 1])
                            if not oa_init[idx]:
                                if cb is None:
                                    operv = P.op("dve", lambda v, osl=osl, po=po:
                                                 v.tensor_copy(osl, po),
                                                 waits=[("pe", o0)])
                                else:
                                    operv = P.op("dve", lambda v, osl=osl, po=po,
                                                 cb=cb: v.tensor_scalar(
                                        osl, po, cb, None, ALU.mult),
                                        waits=[("pe", o0)])
                                oa_init[idx] = True
                            else:
                                if cb is None:
                                    operv = P.op("dve", lambda v, osl=osl, po=po:
                                                 v.scalar_tensor_tensor(
                                        osl, po, 1.0, osl, ALU.mult, ALU.add),
                                        waits=[("pe", o0)])
                                else:
                                    operv = P.op("dve", lambda v, osl=osl, po=po,
                                                 cb=cb: v.scalar_tensor_tensor(
                                        osl, po, cb, osl, ALU.mult, ALU.add),
                                        waits=[("pe", o0)])
                            poslot_rd[t8 % 2] = operv
                        if hh == 0:
                            ew2_rd[0] = P.count("pe")
                            if si + 1 < len(steps):
                                l2s[si + 1][0] = wload2h(
                                    steps[si + 1][0], steps[si + 1][1], 0,
                                    extra=cwait(steps[si + 1][0]))
                    ew2_rd[1] = P.count("pe")
                    if si + 1 < len(steps):
                        l2s[si + 1][1] = wload2h(
                            steps[si + 1][0], steps[si + 1][1], 1,
                            extra=cwait(steps[si + 1][0]))
                    if half == 0 and pend:
                        for _ in range(3 if g < 2 else 2):
                            if pend:
                                gg, wch = pend.pop(0)
                                lastc = _cast_chunk(ctgt, gg, wch)
                        if not pend:
                            cast_done[ctgt] = lastc
            exp_end = (P.count("dve"), P.count("act"))
            prev_pe = P.count("pe")

            xr = P.op("sp", lambda e, half=half: e.dma_start(
                out=X[:].rearrange("p (t d) -> p t d", t=NT)[:, half * 8:(half + 1) * 8, :],
                in_=x_spill.ap().rearrange("(t p) d -> p t d", p=128)[:, half * 8:(half + 1) * 8, :]),
                waits=[("sp", xs)], amt=16)
            spv = P.count("sp")
            for t8 in range(8):
                for hh in range(2):
                    osl = OA[:, t8 * 1024 + hh * 512: t8 * 1024 + hh * 512 + 512]
                    t_ = (ho + t8 * 128) // 128
                    slot = SIN2[:, hh * 512:hh * 512 + 512]
                    sl = P.op("sp", lambda e, t_=t_, hh=hh, slot=slot: e.dma_start(
                        out=slot, in_=s_dram.ap()[t_ * 128:(t_ + 1) * 128,
                                                  hh * 512:hh * 512 + 512]),
                        waits=[("dve", P.count("dve")), ("sp", xr)], amt=16)
                    gsl = P.op("act", lambda s, osl=osl: s.activation(
                        SCRB[:, 0:512], osl, AF.Silu),
                        waits=[("dve", P.count("dve")), ("act", P.count("act"))])
                    xsl = X[:, t_ * D + hh * 512: t_ * D + hh * 512 + 512]
                    z3 = P.op("dve", lambda v, slot=slot: v.tensor_mul(
                        SCRB[:, 512:1024], SCRB[:, 0:512], slot),
                        waits=[("act", gsl), ("sp", sl)])
                    P.op("dve", lambda v, xsl=xsl: v.tensor_add(
                        xsl, xsl, SCRB[:, 512:1024]), waits=[("dve", z3)])
            if half == 1:
                P.barrier()


def emit_l3_casts(P, dr, e0, e1):
    for e_ in range(e0, e1):
        for dst, srcs in (("l3_wb3g", ("l3_gatef", "l3_shgf")),
                          ("l3_wb3u", ("l3_upf", "l3_shuf")),
                          ("l3_wb3d", ("l3_downf", "l3_shdf"))):
            csrc = (dr[srcs[0]][e_] if e_ < 16 else dr[srcs[1]][e_ - 16])
            P.op("pool", lambda e, d=dr[dst][e_], src=csrc: e.dma_start(
                out=d, in_=src), amt=16, sem="cast3")
    return P.count("cast3")



def emit_l2(P, sb, dr):
    nc = P.nc
    X, XNT = sb["X"], sb["XNT"]
    IDF, IDB = sb["IDF"], sb["IDB"]
    ONESRB, ONESRF = sb["ONESRB"], sb["ONESRF"]
    MPREV, ISF = sb["MPREV"], sb["ISF"]
    SCR1, SCRB, STAT = sb["SCR1"], sb["SCRB"], sb["STAT"]
    E, H = 4, 8
    KSW = 2176

    s_dram = nc.dram_tensor("l2_sdram", [NTOK, D], BF16)
    cin = nc.dram_tensor("l2_cin", [128, 512], BF16)
    cout = nc.dram_tensor("l2_cout", [1024, 512], BF16)
    vb_dram = nc.dram_tensor("l2_vb", [31, 256], BF16)

    with ExitStack() as lo:
        SO = lambda n, sh, dt: lo.enter_context(nc.sbuf_tensor(n, list(sh), dt))
        KS = SO("l2_KS", (64, E * KSW), BF16)
        VSK = SO("l2_VSK", (128, E * 17 * 64), BF16)

        with ExitStack() as la:
            SA = lambda n, sh, dt: la.enter_context(nc.sbuf_tensor(n, list(sh), dt))
            PA_ = lambda n, sh, dt: la.enter_context(nc.psum_tensor(n, list(sh), dt))
            WKV = SA("l2_WKV", (128, DC * 1024), BF16)
            WGT = SA("l2_WGT", (128, DC * 32), BF16)
            BGB = SA("l2_BGB", (128, 32), F32)
            KCH = SA("l2_KCH", (128, 512), BF16)
            VCH = SA("l2_VCH", (128, 512), BF16)
            GCH = SA("l2_GCH", (128, 32), F32)
            GKTM = SA("l2_GKTM", (128, NT * 256), BF16)
            GVTM = SA("l2_GVTM", (128, NT * 256), BF16)
            ATM = SA("l2_ATM", (128, 64), F32)
            ACM = SA("l2_ACM", (4, NTOK), BF16)
            ABC = SA("l2_ABC", (64, NTOK), F32)
            GKC = SA("l2_GKC", (64, 256), F32)
            GVC = SA("l2_GVC", (64, 256), F32)
            VSF = SA("l2_VSF", (64, 256), F32)
            IKM = SA("l2_IKM", (64, 32), F32)
            IVM = SA("l2_IVM", (64, 32), F32)
            IKB = SA("l2_IKB", (64, 124), BF16)
            IVB = SA("l2_IVB", (31, 256), BF16)
            SEL4 = SA("l2_SEL4", (4, 64), BF16)
            IOT4 = SA("l2_IOT4", (4, 64), F32)
            PKY = SA("l2_PKY", (64, 124), BF16)
            XB8 = SA("l2_XB8", (128, 8), BF16)
            KBS = SA("l2_KBS", (64, 124), BF16)
            VBS = SA("l2_VBS", (31, 256), BF16)
            GBL = SA("l2_GBL", (128, 512), BF16)

            PK = PA_("l2_PK", (128, 512), F32)
            PV = PA_("l2_PV", (128, 512), F32)
            PG = PA_("l2_PG", (128, 64), F32)
            PTR = PA_("l2_PTR", (128, 128), F32)
            PTB = PA_("l2_PTB", (128, 128), BF16)

            def wload3(dst, name, c):
                return P.op("pool", lambda e, dst=dst, name=name, c=c: e.dma_start(
                    out=dst[:].rearrange("p (c n) -> p c n", c=c),
                    in_=dr[name].rearrange("(c p) n -> p c n", p=128)), amt=16)
            wload3(WKV, "l2_wkvT", DC)
            wload3(WGT, "l2_wgT", DC)
            P.op("pool", lambda e: e.dma_start(out=BGB[:], in_=dr["l2_bgb"]), amt=16)
            P.op("pool", lambda e: e.dma_start(out=IKM[:], in_=dr["l2_ikm"]), amt=16)
            P.op("pool", lambda e: e.dma_start(out=IVM[:], in_=dr["l2_ivm"]), amt=16)
            P.op("pool", lambda e: e.dma_start(out=IKB[:], in_=dr["l2_ikb"]), amt=16)
            wA = P.op("pool", lambda e: e.dma_start(out=IVB[:], in_=dr["l2_ivb"]),
                      amt=16)
            i4 = P.op("pool", lambda e: e.iota(
                IOT4[:], pattern=[[0, 64]], base=0, channel_multiplier=1,
                allow_small_or_imprecise_dtypes=True))
            zz = 0
            for e_ in range(E):
                P.op("pool", lambda g, e_=e_: g.memset(
                    KS[:, e_ * KSW: e_ * KSW + 128], 0.0))
                zz = P.op("pool", lambda g, e_=e_: g.memset(
                    VSK[:, e_ * 1088: e_ * 1088 + 64], 0.0))
            P.waits("pe", [("pool", wA)])

            gk_d = P.count("dve")
            for c in range(0 if 13 in L2_CUTS else NT):
                xsl = [XNT[:, dc * XSTR + 1 + c * 128: dc * XSTR + 1 + (c + 1) * 128]
                       for dc in range(DC)]
                k0 = mm(P, PK[:], [(xsl[dc], WKV[:, dc * 1024: dc * 1024 + 512])
                                   for dc in range(DC)],
                        waits=[("act", P.count("act"))])
                if L2_CUT == 16:
                    gk_d = P.count("dve")
                    continue
                v0 = mm(P, PV[:], [(xsl[dc], WKV[:, dc * 1024 + 512:(dc + 1) * 1024])
                                   for dc in range(DC)])
                if L2_CUT != 15:
                    g0 = mm(P, PG[:, 0:32], [(xsl[dc], WGT[:, dc * 32:(dc + 1) * 32])
                                             for dc in range(DC)])
                ck = P.op("act", lambda s: s.copy(KCH[:], PK[:]),
                          waits=[("pe", k0), ("dve", gk_d)])
                if L2_CUT == 17:
                    gk_d = P.count("dve")
                    continue
                cv = P.op("act", lambda s: s.copy(VCH[:], PV[:]),
                          waits=[("act", ck), ("pe", v0)])
                if (14 in L2_CUTS) or (15 in L2_CUTS):
                    gk_d = P.count("dve")
                    continue
                gb = P.op("dve", lambda v: v.tensor_add(SCR1[:, 0:32], PG[:, 0:32],
                                                        BGB[:]),
                          waits=[("pe", g0), ("dve", gk_d)])
                gs = P.op("act", lambda s: s.activation(GCH[:], SCR1[:, 0:32],
                                                        AF.Sigmoid),
                          waits=[("dve", gb)])
                if L2_CUT == 10:
                    gk_d = P.count("dve")
                    continue
                aa = 0
                for e_ in range(E):
                    aa = P.op("dve", lambda v, e_=e_, c=c: v.reduce_sum(
                        ATM[:, c * 4 + e_:c * 4 + e_ + 1], GCH[:, e_:32:4], axis=AX),
                        waits=([("act", gs)] if e_ == 0 else []))
                aa = P.op("dve", lambda v, c=c: v.tensor_scalar(
                    ATM[:, c * 4:(c + 1) * 4], ATM[:, c * 4:(c + 1) * 4],
                    1.0, -1.0, ALU.min, ALU.mult), waits=[("dve", aa)])
                aa = P.op("dve", lambda v, c=c: v.tensor_scalar_add(
                    ATM[:, c * 4:(c + 1) * 4], ATM[:, c * 4:(c + 1) * 4], 1.0),
                    waits=[("dve", aa)])
                if L2_CUT == 11:
                    gk_d = P.count("dve")
                    continue
                for e_ in range(E):
                    dsl = GKTM[:, c * 256 + e_ * 64: c * 256 + (e_ + 1) * 64]
                    vsl = GVTM[:, c * 256 + e_ * 64: c * 256 + (e_ + 1) * 64]
                    for h in range(H):
                        kslh = KCH[:, h * 64:(h + 1) * 64]
                        vslh = VCH[:, h * 64:(h + 1) * 64]
                        gcol = GCH[:, h * 4 + e_:h * 4 + e_ + 1]
                        if h == 0:
                            gk_d = P.op("dve", lambda v, dsl=dsl, kslh=kslh,
                                        gcol=gcol: v.tensor_scalar(
                                dsl, kslh, gcol, None, ALU.mult),
                                waits=[("act", gs)] if e_ == 0 else [])
                            gk_d = P.op("dve", lambda v, vsl=vsl, vslh=vslh,
                                        gcol=gcol: v.tensor_scalar(
                                vsl, vslh, gcol, None, ALU.mult))
                        else:
                            gk_d = P.op("dve", lambda v, dsl=dsl, kslh=kslh,
                                        gcol=gcol: v.scalar_tensor_tensor(
                                dsl, kslh, gcol, dsl, ALU.mult, ALU.add),
                                waits=[("dve", gk_d)])
                            gk_d = P.op("dve", lambda v, vsl=vsl, vslh=vslh,
                                        gcol=gcol: v.scalar_tensor_tensor(
                                vsl, vslh, gcol, vsl, ALU.mult, ALU.add),
                                waits=[("dve", gk_d)])
                if L2_CUT == 12:
                    continue
                ta = P.op("pe", lambda t, c=c: t.transpose(
                    PTR[0:4, 0:128], ATM[:, c * 4:(c + 1) * 4], IDF[:]),
                    waits=[("dve", aa), ("act", P.count("act"))])
                P.op("act", lambda s, c=c: s.copy(
                    ACM[:, c * 128:(c + 1) * 128], PTR[0:4, 0:128]),
                    waits=[("pe", ta)])

            if L2_CUT == 5:
                return
            scn = P.count("dve")
            for e_ in range(E):
                se = P.op("pool", lambda g, e_=e_: g.tensor_scalar(
                    SEL4[:], IOT4[:], float(e_), None, ALU.is_equal),
                    waits=[("poolc", i4), ("pe", P.count("pe"))])
                for tt in range(4):
                    ab = mm(P, PK[0:64, 0:512],
                            [(SEL4[:], ACM[:, tt * 512:(tt + 1) * 512])],
                            waits=[("poolc", se), ("act", P.count("act")),
                                   ("dve", P.count("dve"))])
                    P.op("act", lambda s, tt=tt: s.copy(
                        ABC[:, tt * 512:(tt + 1) * 512], PK[0:64, 0:512]),
                        waits=[("pe", ab)])
                abdone = P.count("act")
                if L2_CUT == 6:
                    continue
                for l in range(8):
                    for c2 in range(2):
                        cchunk = 2 * l + c2
                        tk = P.op("pe", lambda t, cchunk=cchunk, e_=e_: t.transpose(
                            PTB[0:64, 0:128],
                            GKTM[:, cchunk * 256 + e_ * 64: cchunk * 256 + (e_ + 1) * 64],
                            IDB[:]),
                            waits=[("dve", P.count("dve")), ("pool", P.count("pool")),
                                   ("act", P.count("act"))])
                        ck_ = P.op("act", lambda s, c2=c2: s.copy(
                            GKC[0:64, c2 * 128:(c2 + 1) * 128], PTB[0:64, 0:128]),
                            waits=[("pe", tk)])
                        tv = P.op("pe", lambda t, cchunk=cchunk, e_=e_: t.transpose(
                            PTB[0:64, 0:128],
                            GVTM[:, cchunk * 256 + e_ * 64: cchunk * 256 + (e_ + 1) * 64],
                            IDB[:]), waits=[("act", ck_)])
                        cv_ = P.op("act", lambda s, c2=c2: s.copy(
                            GVC[0:64, c2 * 128:(c2 + 1) * 128], PTB[0:64, 0:128]),
                            waits=[("pe", tv)])
                    if L2_CUT == 7:
                        continue
                    kdst = KS[0:64, e_ * KSW + 128 + l * 256: e_ * KSW + 128 + (l + 1) * 256]
                    s1 = P.op("dve", lambda v, kdst=kdst, e_=e_, l=l: v.tensor_tensor_scan(
                        kdst, ABC[0:64, l * 256:(l + 1) * 256], GKC[:],
                        IKM[:, e_ * 8 + l:e_ * 8 + l + 1], ALU.mult, ALU.add),
                        waits=[("act", P.count("act")), ("pool", wA)])
                    s2 = P.op("dve", lambda v, e_=e_, l=l: v.tensor_tensor_scan(
                        VSF[:], ABC[0:64, l * 256:(l + 1) * 256], GVC[:],
                        IVM[:, e_ * 8 + l:e_ * 8 + l + 1], ALU.mult, ALU.add),
                        waits=[("dve", s1)])
                    for c2 in range(2):
                        tvs = P.op("pe", lambda t, c2=c2: t.transpose(
                            PTR[0:128, 0:64], VSF[0:64, c2 * 128:(c2 + 1) * 128],
                            IDF[0:64, 0:64]),
                            waits=[("dve", s2), ("act", P.count("act"))])
                        P.op("act", lambda s, e_=e_, l=l, c2=c2: s.copy(
                            VSK[:, e_ * 1088 + (2 * l + 1 + c2) * 64:
                                e_ * 1088 + (2 * l + 2 + c2) * 64],
                            PTR[0:128, 0:64]), waits=[("pe", tvs)])

            if L2_CUT == 4:
                return
            zc = P.op("pool", lambda g: g.memset(GBL[:], 0.0))
            zc2 = P.op("sp", lambda e: e.dma_start(out=cin.ap(), in_=GBL[:]),
                       waits=[("poolc", zc)], amt=16)
            pk0 = 0
            for e_ in range(E):
                pk0 = P.op("dve", lambda v, e_=e_: v.tensor_copy(
                    PKY[:, e_ * 31:(e_ + 1) * 31],
                    KS[0:64, e_ * KSW + 2145: e_ * KSW + 2176]),
                    waits=[("act", P.count("act")), ("dve", P.count("dve"))])
            xb = P.op("dve", lambda v: v.tensor_copy(
                XB8[:].rearrange("p (c n) -> p c n", c=8),
                XNT[:].rearrange("p (c n) -> p c n", c=DC)[:, :, 2048:2049]),
                waits=[("dve", pk0)])
            c0_ = P.op("sp", lambda e: e.dma_start(
                out=cin.ap()[0:64, 0:124], in_=PKY[:]),
                waits=[("dve", pk0), ("sp", zc2)], amt=16)
            c1_ = P.op("sp", lambda e: e.dma_start(
                out=cin.ap()[0:128, 384:392], in_=XB8[:]),
                waits=[("dve", xb)], amt=16)
            cvv = c1_
            for e_ in range(E):
                cvv = P.op("sp", lambda e, e_=e_: e.dma_start(
                    out=cin.ap()[0:31, 128 + e_ * 64:128 + (e_ + 1) * 64],
                    in_=VSK[97:128, e_ * 1088 + 16 * 64: e_ * 1088 + 17 * 64]),
                    waits=[("act", P.count("act"))], amt=16)
            cc2 = P.op("pool", lambda e: e.collective_compute(
                "AllGather", ALU.bypass, replica_groups=[list(range(NCORES))],
                ins=[cin.ap().opt()], outs=[cout.ap().opt()]),
                waits=[("sp", cvv)], sem="cc")
            gprev = 0
            for r in range(NCORES):
                gr = P.op("sp", lambda e, r=r: e.dma_start(
                    out=GBL[:], in_=cout.ap()[r * 128:(r + 1) * 128, :]),
                    waits=([("cc", P.count("cc"))] if r == 0 else
                           [("dve", gprev)]), amt=16)
                if r == 0:
                    P.op("dve", lambda v: v.tensor_scalar(
                        KBS[:], GBL[0:64, 0:124], MPREV[0:64, 0:1], None, ALU.mult),
                        waits=[("sp", gr)])
                    P.op("dve", lambda v: v.tensor_scalar(
                        VBS[:], GBL[0:31, 128:384], MPREV[0:31, 0:1], None, ALU.mult))
                    gprev = P.op("dve", lambda v: v.tensor_scalar(
                        XB8[:], GBL[:, 384:392], MPREV[:, 0:1], None, ALU.mult))
                else:
                    P.op("dve", lambda v, r=r: v.scalar_tensor_tensor(
                        KBS[:], GBL[0:64, 0:124], MPREV[0:64, r:r + 1], KBS[:],
                        ALU.mult, ALU.add), waits=[("sp", gr)])
                    P.op("dve", lambda v, r=r: v.scalar_tensor_tensor(
                        VBS[:], GBL[0:31, 128:384], MPREV[0:31, r:r + 1], VBS[:],
                        ALU.mult, ALU.add))
                    gprev = P.op("dve", lambda v, r=r: v.scalar_tensor_tensor(
                        XB8[:], GBL[:, 384:392], MPREV[:, r:r + 1], XB8[:],
                        ALU.mult, ALU.add))
            gprev = P.op("dve", lambda v: v.scalar_tensor_tensor(
                KBS[:], IKB[:], ISF[0:64, 0:1], KBS[:], ALU.mult, ALU.add),
                waits=[("dve", gprev), ("pool", wA)])
            gprev = P.op("dve", lambda v: v.scalar_tensor_tensor(
                VBS[:], IVB[:], ISF[0:31, 0:1], VBS[:], ALU.mult, ALU.add),
                waits=[("dve", gprev)])
            for e_ in range(E):
                gprev = P.op("dve", lambda v, e_=e_: v.tensor_copy(
                    KS[0:64, e_ * KSW + 97: e_ * KSW + 128],
                    KBS[:, e_ * 31:(e_ + 1) * 31]), waits=[("dve", gprev)])
            gprev = P.op("dve", lambda v: v.tensor_copy(
                XNT[:].rearrange("p (c n) -> p c n", c=DC)[:, :, 0:1],
                XB8[:].rearrange("p (c n) -> p c n", c=8)), waits=[("dve", gprev)])
            vb0 = P.op("sp", lambda e: e.dma_start(out=vb_dram.ap(), in_=VBS[:]),
                       waits=[("dve", gprev)], amt=16)
            for e_ in range(E):
                vb0 = P.op("sp", lambda e, e_=e_: e.dma_start(
                    out=VSK[97:128, e_ * 1088: e_ * 1088 + 64],
                    in_=vb_dram.ap()[:, e_ * 64:(e_ + 1) * 64]),
                    waits=([("sp", vb0), ("poolc", zz)] if e_ == 0 else []), amt=16)
            if os.environ.get("L2DUMP"):
                dks = nc.dram_tensor("dbg_ks", [64, E * KSW], BF16,
                                     kind="ExternalOutput").ap()
                dvs = nc.dram_tensor("dbg_vsk", [128, E * 17 * 64], BF16,
                                     kind="ExternalOutput").ap()
                dgk = nc.dram_tensor("dbg_gk", [128, NT * 256], BF16,
                                     kind="ExternalOutput").ap()
                dac = nc.dram_tensor("dbg_acm", [4, NTOK], BF16,
                                     kind="ExternalOutput").ap()
                dd = P.op("sp", lambda e: e.dma_start(out=dks, in_=KS[:]),
                          waits=[("dve", P.count("dve")), ("act", P.count("act"))],
                          amt=16)
                dd = P.op("sp", lambda e: e.dma_start(out=dvs, in_=VSK[:]), amt=16)
                dd = P.op("sp", lambda e: e.dma_start(out=dgk, in_=GKTM[:]), amt=16)
                dd = P.op("sp", lambda e: e.dma_start(out=dac, in_=ACM[:]), amt=16)
                P.waits("sp", [("sp", dd)])
            P.waits("pe", [("sp", vb0)])
        P.barrier()

        if L2_CUT == 1:
            return
        with ExitStack() as lb:
            SB_ = lambda n, sh, dt: lb.enter_context(nc.sbuf_tensor(n, list(sh), dt))
            PB_ = lambda n, sh, dt: lb.enter_context(nc.psum_tensor(n, list(sh), dt))
            QF = SB_("l2_QF", (64, H * NTOK), BF16)
            WQTS = SB_("l2_WQTS", (128, DC * 512), BF16)
            MSKB = SB_("l2_MSKB", (128, 256), F32)
            SM = SB_("l2_SM", (128, 1024), F32)
            PSCL = SB_("l2_PSCL", (128, 1024), BF16)
            PTS = SB_("l2_PTS", (128, 256), BF16)
            YF2 = SB_("l2_YF2", (128, 4 * NTOK), BF16)
            WO = SB_("l2_WO", (128, 4 * 1024), BF16)
            SOUT = SB_("l2_SOUT", (128, 1024), BF16)

            PS1 = PB_("l2_PS1", (128, 512), F32)
            PS2 = PB_("l2_PS2", (128, 512), F32)
            PTA = PB_("l2_PTA", (128, 256), BF16)
            POU = PB_("l2_POU", (64, 128), F32)
            PQ = PB_("l2_PQ", (64, 512), F32)
            PST = PB_("l2_PST", (128, 512), F32)

            wq = P.op("pool", lambda e: e.dma_start(
                out=WQTS[:].rearrange("p (c n) -> p c n", c=DC),
                in_=dr["l2_wqT"].rearrange("(c p) n -> p c n", p=128)), amt=16)
            wB = P.op("pool", lambda e: e.dma_start(
                out=WO[:].rearrange("p (c n) -> p c n", c=4),
                in_=dr["l2_wo"].rearrange("(c p) n -> p c n", p=128)), amt=16)
            z0 = P.op("pool", lambda e: e.memset(SCR1[:, 0:256], 0.0))
            z1 = P.op("pool", lambda e: e.affine_select(
                MSKB[:], SCR1[:, 0:256], pattern=[[1, 256]], compare_op=ALU.is_ge,
                fill=-1e30, base=-97, channel_multiplier=-1), waits=[("poolc", z0)])
            z2 = P.op("pool", lambda e: e.affine_select(
                MSKB[:], MSKB[:], pattern=[[-1, 256]], compare_op=ALU.is_ge,
                fill=-1e30, base=128, channel_multiplier=1), waits=[("poolc", z1)])

            for h in range(H):
                for tt in range(4):
                    q0 = mm(P, PQ[:, 0:512],
                            [(WQTS[:, dc * 512 + h * 64: dc * 512 + (h + 1) * 64],
                              XNT[:, dc * XSTR + 1 + tt * 512: dc * XSTR + 1 + tt * 512 + 512])
                             for dc in range(DC)],
                            waits=[("act", P.count("act")), ("pool", wq)])
                    P.op("act", lambda s, h=h, tt=tt: s.activation(
                        QF[0:64, h * NTOK + tt * 512: h * NTOK + tt * 512 + 512],
                        PQ[0:64, 0:512], AF.Copy, scale=0.125), waits=[("pe", q0)])

            for c in range(NT):
                for h in range(H):
                    lq = QF[0:64, h * NTOK + c * 128: h * NTOK + (c + 1) * 128]
                    smw = P.count("dve")
                    ps = []
                    for e_ in range(E):
                        bank = PS1 if e_ < 2 else PS2
                        out = bank[:, (e_ % 2) * 256:(e_ % 2) * 256 + 256]
                        r0 = mm(P, out, [(lq, KS[0:64, e_ * KSW + 128 * c:
                                                  e_ * KSW + 128 * c + 256])],
                                waits=[("dve", smw)] if e_ == 0 else [])
                        ps.append((out, r0))
                    sm0 = 0
                    for e_ in range(E):
                        sm0 = P.op("dve", lambda v, e_=e_, o=ps[e_][0]: v.tensor_add(
                            SM[:, e_ * 256:(e_ + 1) * 256], o, MSKB[:]),
                            waits=[("pe", ps[e_][1]), ("poolc", z2)])
                    mx = P.op("dve", lambda v: v.reduce_max(STAT[:, 0:1], SM[:],
                                                            axis=AX),
                              waits=[("dve", sm0)])
                    ng = P.op("dve", lambda v: v.tensor_scalar_mul(
                        STAT[:, 1:2], STAT[:, 0:1], -1.0), waits=[("dve", mx)])
                    ex = P.op("act", lambda s: s.activation(
                        PSCL[:], SM[:], AF.Exp, bias=STAT[:, 1:2],
                        accum_out=STAT[:, 2:3]), waits=[("dve", ng)])
                    rc = P.op("dve", lambda v: v.reciprocal(STAT[:, 3:4],
                                                            STAT[:, 2:3]),
                              waits=[("act", ex)])
                    sc = P.op("dve", lambda v: v.tensor_scalar(
                        PSCL[:], PSCL[:], STAT[:, 3:4], None, ALU.mult),
                        waits=[("dve", rc)])
                    pvlast = None
                    for e_ in range(E):
                        t0 = P.op("pe", lambda t, e_=e_: t.transpose(
                            PTA[:, 0:128], PSCL[:, e_ * 256: e_ * 256 + 128],
                            IDB[:]),
                            waits=[("dve", sc)] if e_ == 0 else
                                  [("act", P.count("act"))])
                        t1 = P.op("pe", lambda t, e_=e_: t.transpose(
                            PTA[:, 128:256], PSCL[:, e_ * 256 + 128:(e_ + 1) * 256],
                            IDB[:]))
                        a0 = P.op("act", lambda s: s.copy(PTS[:, 0:128],
                                                          PTA[:, 0:128]),
                                  waits=[("pe", t1)])
                        a1 = P.op("act", lambda s: s.copy(PTS[:, 128:256],
                                                          PTA[:, 128:256]),
                                  waits=[("act", a0)])
                        for c2 in range(2):
                            st_ = (e_ == 0 and c2 == 0)
                            sp_ = (e_ == E - 1 and c2 == 1)
                            pvlast = P.op("pe", lambda t, e_=e_, c2=c2, st_=st_,
                                          sp_=sp_, c=c: t.matmul(
                                POU[:],
                                VSK[:, e_ * 1088 + (c + c2) * 64:
                                    e_ * 1088 + (c + c2 + 1) * 64],
                                PTS[:, c2 * 128:(c2 + 1) * 128],
                                start=st_, stop=sp_, skip_group_check=True),
                                waits=[("act", a1)] if c2 == 0 else (),
                                noinc=not sp_)
                    P.op("act", lambda s, h=h, c=c: s.copy(
                        YF2[(h % 2) * 64:(h % 2) * 64 + 64,
                            (h // 2) * NTOK + c * 128:(h // 2) * NTOK + (c + 1) * 128],
                        POU[:]), waits=[("pe", pvlast)])

            if os.environ.get("L2DUMP"):
                dqf = nc.dram_tensor("dbg_qf", [64, H * NTOK], BF16,
                                     kind="ExternalOutput").ap()
                dyf = nc.dram_tensor("dbg_yf", [128, 4 * NTOK], BF16,
                                     kind="ExternalOutput").ap()
                dd2 = P.op("sp", lambda e: e.dma_start(out=dqf, in_=QF[:]),
                           waits=[("act", P.count("act")), ("pe", P.count("pe"))],
                           amt=16)
                dd2 = P.op("sp", lambda e: e.dma_start(out=dyf, in_=YF2[:]), amt=16)
                P.waits("sp", [("sp", dd2)])
            sv = P.count("dve")
            spv = P.count("sp")
            for t in range(NT):
                for hh in range(2):
                    w0 = mm(P, PST[:],
                            [(YF2[:, p * NTOK + t * 128: p * NTOK + (t + 1) * 128],
                              WO[:, p * 1024 + hh * 512: p * 1024 + hh * 512 + 512])
                             for p in range(4)],
                            waits=[("pool", wB), ("dve", sv),
                                   ("act", P.count("act"))])
                    slot = SOUT[:, (t % 2) * 512:(t % 2) * 512 + 512]
                    sv = P.op("dve", lambda v, slot=slot: v.tensor_copy(slot, PST[:]),
                              waits=[("pe", w0), ("sp", spv)])
                    spv = P.op("sp", lambda e, t=t, hh=hh, slot=slot: e.dma_start(
                        out=s_dram.ap()[t * 128:(t + 1) * 128,
                                        hh * 512:hh * 512 + 512],
                        in_=slot), waits=[("dve", sv)], amt=16)
        P.barrier()

    if (0 in L2_CUTS) or (3 in L2_CUTS):
        emit_l2_ffn(P, sb, dr, s_dram)


def emit_l2_ffn(P, sb, dr, s_dram):
    nc = P.nc
    X, XNT = sb["X"], sb["XNT"]
    SCR1, SCRB = sb["SCR1"], sb["SCRB"]

    with ExitStack() as lc:
        SC = lambda n, sh, dt: lc.enter_context(nc.sbuf_tensor(n, list(sh), dt))
        PC_ = lambda n, sh, dt: lc.enter_context(nc.psum_tensor(n, list(sh), dt))
        KACT = SC("l2_KACT", (128, DC * NTOK), BF16)
        TMPB = SC("l2_TMPB", (128, NTOK), BF16)
        MUR = SC("l2_MUR", (128, 8), F32)
        MUK = SC("l2_MUK", (128, 8), F32)
        OMUR = SC("l2_OMUR", (128, 8), F32)
        OMUK = SC("l2_OMUK", (128, 8), F32)
        WJ = SC("l2_WJ", (128, 2 * 1024), BF16)
        WRH = SC("l2_WRH", (128, DC * 512), BF16)
        WVH = SC("l2_WVH", (128, DC * 512), BF16)
        RT = SC("l2_RT", (128, 512), BF16)
        SIN = SC("l2_SIN", (128, 1024), BF16)

        PC4 = PC_("l2_PC4", (128, 2048), F32)
        PSr = PC_("l2_PSr", (128, 512), F32)
        PSv = PC_("l2_PSv", (128, 512), F32)

        mu0 = P.op("pool", lambda e: e.dma_start(out=MUR[:], in_=dr["l2_mur"]),
                   amt=16)
        mu1 = P.op("pool", lambda e: e.dma_start(out=MUK[:], in_=dr["l2_muk"]),
                   amt=16)
        o0 = P.op("dve", lambda v: v.tensor_scalar(
            OMUR[:], MUR[:], -1.0, 1.0, ALU.mult, ALU.add), waits=[("pool", mu1)])
        o1 = P.op("dve", lambda v: v.tensor_scalar(
            OMUK[:], MUK[:], -1.0, 1.0, ALU.mult, ALU.add), waits=[("dve", o0)])

        lsk = ExitStack()
        XKT = lsk.enter_context(nc.sbuf_tensor("l2_XKT", [128, DC * NTOK], BF16))
        mprev = o1
        for dc in range(DC):
            prev = XNT[:, dc * XSTR: dc * XSTR + NTOK]
            cur = XNT[:, dc * XSTR + 1: dc * XSTR + 1 + NTOK]
            m2 = P.op("dve", lambda v, prev=prev, dc=dc: v.tensor_scalar(
                TMPB[:], prev, OMUK[:, dc:dc + 1], None, ALU.mult),
                waits=[("dve", mprev)])
            mprev = P.op("dve", lambda v, cur=cur, dc=dc: v.scalar_tensor_tensor(
                XKT[:, dc * NTOK:(dc + 1) * NTOK], cur, MUK[:, dc:dc + 1], TMPB[:],
                ALU.mult, ALU.add), waits=[("dve", m2)])

        kv = P.count("dve")
        for j in range(DC):
            slot = WJ[:, (j % 2) * 1024:(j % 2) * 1024 + 1024]
            wj = P.op("pool", lambda e, j=j, slot=slot: e.dma_start(
                out=slot.rearrange("p (c n) -> p c n", c=DC),
                in_=dr["l2_wk2T"][:, j * 128:(j + 1) * 128].rearrange(
                    "(c p) n -> p c n", p=128)),
                waits=[("pe", P.count("pe"))], amt=16)
            wlast = 0
            for tt in range(4):
                pairs = [(slot[:, dc * 128:(dc + 1) * 128],
                          XKT[:, dc * NTOK + tt * 512: dc * NTOK + tt * 512 + 512])
                         for dc in range(DC)]
                wlast = mm(P, PC4[:, tt * 512:(tt + 1) * 512], pairs,
                           waits=[("pool", wj), ("dve", mprev),
                                  ("act", P.count("act")), ("dve", kv)])
            for tt in range(4):
                r0 = P.op("act", lambda s, tt=tt: s.activation(
                    SCRB[:, 0:512], PC4[:, tt * 512:(tt + 1) * 512], AF.Relu),
                    waits=([("pe", wlast)] if tt == 0 else []) +
                          [("dve", P.count("dve"))])
                kv = P.op("dve", lambda v, j=j, tt=tt: v.tensor_mul(
                    KACT[:, j * NTOK + tt * 512: j * NTOK + tt * 512 + 512],
                    SCRB[:, 0:512], SCRB[:, 0:512]), waits=[("act", r0)])

        lsk.close()
        P.barrier()
        XRT = lc.enter_context(nc.sbuf_tensor("l2_XRT", [128, DC * NTOK], BF16))
        mprev = P.count("dve")
        for dc in range(DC):
            prev = XNT[:, dc * XSTR: dc * XSTR + NTOK]
            cur = XNT[:, dc * XSTR + 1: dc * XSTR + 1 + NTOK]
            m0 = P.op("dve", lambda v, prev=prev, dc=dc: v.tensor_scalar(
                TMPB[:], prev, OMUR[:, dc:dc + 1], None, ALU.mult),
                waits=[("dve", mprev)])
            mprev = P.op("dve", lambda v, cur=cur, dc=dc: v.scalar_tensor_tensor(
                XRT[:, dc * NTOK:(dc + 1) * NTOK], cur, MUR[:, dc:dc + 1], TMPB[:],
                ALU.mult, ALU.add), waits=[("dve", m0)])

        for hh in range(2):
            wr = P.op("pool", lambda e, hh=hh: e.dma_start(
                out=WRH[:].rearrange("p (c n) -> p c n", c=DC),
                in_=dr["l2_wrT"][:, hh * 512:(hh + 1) * 512].rearrange(
                    "(c p) n -> p c n", p=128)),
                waits=[("pe", P.count("pe"))], amt=16)
            wv = P.op("pool", lambda e, hh=hh: e.dma_start(
                out=WVH[:].rearrange("p (c n) -> p c n", c=DC),
                in_=dr["l2_wv2T"][:, hh * 512:(hh + 1) * 512].rearrange(
                    "(c p) n -> p c n", p=128)), amt=16)
            cprev = P.count("dve")
            spv = P.count("sp")
            for t in range(NT):
                r0 = mm(P, PSr[:],
                        [(XRT[:, dc * NTOK + t * 128: dc * NTOK + (t + 1) * 128],
                          WRH[:, dc * 512:(dc + 1) * 512]) for dc in range(DC)],
                        waits=[("pool", wr), ("act", P.count("act")),
                               ("dve", cprev)])
                ra = P.op("act", lambda s: s.activation(RT[:], PSr[:], AF.Sigmoid),
                          waits=[("pe", r0)])
                v0 = mm(P, PSv[:],
                        [(KACT[:, dc * NTOK + t * 128: dc * NTOK + (t + 1) * 128],
                          WVH[:, dc * 512:(dc + 1) * 512]) for dc in range(DC)],
                        waits=[("pool", wv)])
                g0 = P.op("dve", lambda v: v.tensor_mul(SCR1[:, 0:512], PSv[:],
                                                        RT[:]),
                          waits=[("pe", v0), ("act", ra)])
                ga = P.op("act", lambda s: s.activation(SCRB[:, 0:512],
                                                        SCR1[:, 0:512], AF.Silu),
                          waits=[("dve", g0)])
                sl = P.op("sp", lambda e, t=t, hh=hh: e.dma_start(
                    out=SIN[:, (t % 2) * 512:(t % 2) * 512 + 512],
                    in_=s_dram.ap()[t * 128:(t + 1) * 128, hh * 512:hh * 512 + 512]),
                    waits=[("dve", cprev)] if t < 2 else [("dve", cprev)], amt=16)
                _dbg = os.environ.get("L2DBG", "")
                if _dbg == "g":
                    d0 = P.op("dve", lambda v, t=t: v.tensor_copy(
                        SCRB[:, 512:1024], SCRB[:, 0:512]),
                        waits=[("act", ga), ("sp", sl)])
                elif _dbg == "s":
                    d0 = P.op("dve", lambda v, t=t: v.tensor_copy(
                        SCRB[:, 512:1024], SIN[:, (t % 2) * 512:(t % 2) * 512 + 512]),
                        waits=[("act", ga), ("sp", sl)])
                else:
                    d0 = P.op("dve", lambda v, t=t: v.tensor_mul(
                        SCRB[:, 512:1024], SCRB[:, 0:512],
                        SIN[:, (t % 2) * 512:(t % 2) * 512 + 512]),
                        waits=[("act", ga), ("sp", sl)])
                xsl = X[:, t * D + hh * 512: t * D + hh * 512 + 512]
                cprev = P.op("dve", lambda v, xsl=xsl: v.tensor_add(
                    xsl, xsl, SCRB[:, 512:1024]), waits=[("dve", d0)])
            if hh == 0 and sb.get("l3_cast_early"):
                emit_l3_casts(P, dr, 0, 9)
        if sb.get("l3_cast_early"):
            sb["cast3_tot"] = emit_l3_casts(P, dr, 9, 18)



def emit_l3(P, sb, dr):
    nc = P.nc
    X, XNT = sb["X"], sb["XNT"]
    IDB, IDF = sb["IDB"], sb["IDF"]
    ONESRB = sb["ONESRB"]
    QOFF = sb["QOFF"]
    SCR1, SCRB = sb["SCR1"], sb["SCRB"]
    SCALE = 512.0 ** -0.5
    T0MAX = 3 * NTOK
    GROUPS = [[0, 1, 2, 3], [4, 5, 6, 7]]

    s_dram = nc.dram_tensor("l3_sdram", [NTOK, D], BF16)
    cin3 = nc.dram_tensor("l3_cin", [128, 16384], BF16)
    cout3 = nc.dram_tensor("l3_cout", [512, 16384], BF16)

    with ExitStack() as lo:
        SO = lambda n, sh, dt: lo.enter_context(nc.sbuf_tensor(n, list(sh), dt))
        QF3 = SO("l3_QF3", (128, 4 * NTOK), BF16)

        with ExitStack() as la:
            SA = lambda n, sh, dt: la.enter_context(nc.sbuf_tensor(n, list(sh), dt))
            WQKV = SA("l3_WQKV", (128, DC * 1536), BF16)
            KF3 = SA("l3_KF3", (128, 4 * NTOK), BF16)
            VTM3 = SA("l3_VTM3", (128, NT * 512), BF16)
            PQ3 = la.enter_context(nc.psum_tensor("l3_PQ3", [128, 512], F32))
            PK3 = la.enter_context(nc.psum_tensor("l3_PK3", [128, 512], F32))

            wq = P.op("pool", lambda e: e.dma_start(
                out=WQKV[:].rearrange("p (c n) -> p c n", c=DC),
                in_=dr["l3_wqkvT"].rearrange("(c p) n -> p c n", p=128)), amt=16)
            for ci in range(4):
                for tt in range(4):
                    q0 = mm(P, PQ3[:],
                            [(WQKV[:, dc * 1536 + ci * 128: dc * 1536 + (ci + 1) * 128],
                              XNT[:, dc * XSTR + 1 + tt * 512: dc * XSTR + 1 + tt * 512 + 512])
                             for dc in range(DC)],
                            waits=[("pool", wq), ("act", P.count("act"))])
                    P.op("act", lambda s, ci=ci, tt=tt: s.activation(
                        QF3[:, ci * NTOK + tt * 512: ci * NTOK + tt * 512 + 512],
                        PQ3[:], AF.Copy, scale=SCALE), waits=[("pe", q0)])
                    k0 = mm(P, PK3[:],
                            [(WQKV[:, dc * 1536 + 512 + ci * 128: dc * 1536 + 512 + (ci + 1) * 128],
                              XNT[:, dc * XSTR + 1 + tt * 512: dc * XSTR + 1 + tt * 512 + 512])
                             for dc in range(DC)],
                            waits=[("act", P.count("act"))])
                    P.op("act", lambda s, ci=ci, tt=tt: s.copy(
                        KF3[:, ci * NTOK + tt * 512: ci * NTOK + tt * 512 + 512],
                        PK3[:]), waits=[("pe", k0)])
            for t in range(NT):
                v0 = mm(P, PQ3[:],
                        [(XNT[:, dc * XSTR + 1 + t * 128: dc * XSTR + 1 + (t + 1) * 128],
                          WQKV[:, dc * 1536 + 1024: dc * 1536 + 1536])
                         for dc in range(DC)],
                        waits=[("act", P.count("act"))])
                P.op("act", lambda s, t=t: s.copy(
                    VTM3[:, t * 512:(t + 1) * 512], PQ3[:]), waits=[("pe", v0)])
            kd = P.op("sp", lambda e: e.dma_start(
                out=cin3.ap()[:, 0:4 * NTOK], in_=KF3[:]),
                waits=[("act", P.count("act"))], amt=16)
            kd = P.op("sp", lambda e: e.dma_start(
                out=cin3.ap()[:, 4 * NTOK:8 * NTOK], in_=VTM3[:]), amt=16)
            cc3 = P.op("pool", lambda e: e.collective_compute(
                "AllGather", ALU.bypass, replica_groups=GROUPS,
                ins=[cin3.ap().opt()], outs=[cout3.ap().opt()]),
                waits=[("sp", kd)], sem="cc")
            P.waits("sp", [("cc", P.count("cc"))])
        P.barrier()

        with ExitStack() as lb:
            SB_ = lambda n, sh, dt: lb.enter_context(nc.sbuf_tensor(n, list(sh), dt))
            KBUF = SB_("l3_KBUF", (128, 4 * 2048), BF16)
            VBUF = SB_("l3_VBUF", (128, 4 * 2048), BF16)
            OACC = SB_("l3_OACC", (128, 2 * 512), F32)
            SM3 = SB_("l3_SM3", (128, 2 * 512), F32)
            MG = SB_("l3_MG", (128, 2 * 512), F32)
            CIOT = SB_("l3_CIOT", (128, 512), F32)
            PSCL3 = SB_("l3_PSCL3", (128, 6 * 512), BF16)
            PTS3 = SB_("l3_PTS3", (128, 2 * 512), BF16)
            FST = SB_("l3_FST", (128, 32), F32)
            FSTB = SB_("l3_FSTB", (128, 2), F32)
            OB = SB_("l3_OB", (128, 512), BF16)
            OUTF = SB_("l3_OUTF", (128, 512), BF16)
            WT3 = SB_("l3_WT3", (128, 4 * 1024), BF16)
            BOUT3 = SB_("l3_BOUT3", (1, 1024), BF16)
            SOUT = SB_("l3_SOUT", (128, 1024), BF16)

            PS3 = lb.enter_context(nc.psum_tensor("l3_PS3", [128, 2 * 512], F32))
            PT3 = lb.enter_context(nc.psum_tensor("l3_PT3", [128, 2 * 512], BF16))
            PV3 = lb.enter_context(nc.psum_tensor("l3_PV3", [128, 2 * 512], F32))
            PSO = lb.enter_context(nc.psum_tensor("l3_PSO", [128, 512], F32))

            wt = P.op("pool", lambda e: e.dma_start(
                out=WT3[:].rearrange("p (c n) -> p c n", c=4),
                in_=dr["l3_woutT"].rearrange("(c p) n -> p c n", p=128)), amt=16)
            wt2 = P.op("pool", lambda e: e.dma_start(out=BOUT3[:], in_=dr["l3_bout"]),
                       amt=16)
            ci0 = P.op("pool", lambda e: e.iota(
                CIOT[:], pattern=[[1, 512]], base=0, channel_multiplier=-1,
                allow_small_or_imprecise_dtypes=True))

            castq = []
            if "cast3_tot" not in sb:
                for e_ in range(18):
                    for dst, srcs in (("l3_wb3g", ("l3_gatef", "l3_shgf")),
                                      ("l3_wb3u", ("l3_upf", "l3_shuf")),
                                      ("l3_wb3d", ("l3_downf", "l3_shdf"))):
                        csrc = (dr[srcs[0]][e_] if e_ < 16
                                else dr[srcs[1]][e_ - 16])
                        castq.append((dr[dst][e_], csrc))

            spv = P.count("sp")
            kb_rd = [0, 0, 0, 0]
            ps_rd = [0, 0]
            pv_rd = [0, 0]
            pscl_rd = [[0, 0, 0], [0, 0, 0]]
            pts_rd = [0, 0]
            pt3_rd = [0, 0]
            exprev = [0, 0]
            kvld = {}
            exv = {}

            def kload(sb_):
                kslot = sb_ % 4
                kb = KBUF[:, kslot * 2048:(kslot + 1) * 2048]
                vb = VBUF[:, kslot * 2048:(kslot + 1) * 2048]
                r = sb_ // 4
                col = (sb_ % 4) * 512
                P.op("sp", lambda e, r=r, col=col, kb=kb: e.dma_start(
                    out=kb.rearrange("p (c n) -> p c n", c=4),
                    in_=cout3.ap()[r * 128:(r + 1) * 128, 0:4 * NTOK].rearrange(
                        "p (c n) -> p c n", c=4)[:, :, col:col + 512]),
                    waits=[("pe", kb_rd[kslot])], amt=16, sem=f"kv{kslot}")
                kvld[sb_] = P.op("sp", lambda e, r=r, col=col, vb=vb:
                                 e.dma_start(
                    out=vb,
                    in_=cout3.ap()[r * 128:(r + 1) * 128,
                                   4 * NTOK + (col // 128) * 512:
                                   4 * NTOK + (col // 128) * 512 + 2048]),
                    amt=16, sem=f"kv{kslot}")

            def fstep_a(s, i, sb_):
                fs = s * 16
                acol = fs + 8 + (sb_ % 3)
                kslot = sb_ % 4
                kb = KBUF[:, kslot * 2048:(kslot + 1) * 2048]
                sm3 = SM3[:, s * 512:(s + 1) * 512]
                mgs = MG[:, s * 512:(s + 1) * 512]
                ps3 = PS3[:, s * 512:(s + 1) * 512]
                s0 = mm(P, ps3,
                        [(QF3[:, ci * NTOK + i * 128: ci * NTOK + (i + 1) * 128],
                          kb[:, ci * 512:(ci + 1) * 512]) for ci in range(4)],
                        waits=[(f"kv{kslot}", kvld[sb_]),
                               ("dve", ps_rd[s])])
                th = P.op("dve", lambda v, i=i, sb_=sb_: v.tensor_scalar(
                    FSTB[:, s:s + 1], QOFF[:, i:i + 1], float(-512.0 * sb_), None,
                    ALU.add), waits=[("poolc", ci0)])
                mg = P.op("dve", lambda v: v.tensor_scalar(
                    mgs, CIOT[:], FSTB[:, s:s + 1], None, ALU.is_gt),
                    waits=[("dve", th)])
                sm = P.op("dve", lambda v, ps3=ps3: v.scalar_tensor_tensor(
                    sm3, mgs, -1e30, ps3, ALU.mult, ALU.add),
                    waits=[("dve", mg), ("pe", s0), ("act", exprev[s])])
                ps_rd[s] = sm
                m0 = P.op("dve", lambda v: v.reduce_max(FST[:, fs + 2:fs + 3], sm3,
                                                        axis=AX),
                          waits=[("dve", sm)])
                if sb_ == 0:
                    mn = P.op("dve", lambda v: v.tensor_copy(
                        FST[:, fs + 3:fs + 4], FST[:, fs + 2:fs + 3]),
                        waits=[("dve", m0)])
                else:
                    mn = P.op("dve", lambda v: v.tensor_max(
                        FST[:, fs + 3:fs + 4], FST[:, fs + 0:fs + 1],
                        FST[:, fs + 2:fs + 3]),
                        waits=[("dve", m0)])
                ng = P.op("dve", lambda v: v.tensor_scalar_mul(
                    FST[:, fs + 5:fs + 6], FST[:, fs + 3:fs + 4], -1.0),
                    waits=[("dve", mn), ("act", exprev[s])])
                pscl = PSCL3[:, (3 * s + sb_ % 3) * 512:
                             (3 * s + sb_ % 3) * 512 + 512]
                ex = P.op("act", lambda s_, pscl=pscl: s_.activation(
                    pscl, sm3, AF.Exp, bias=FST[:, fs + 5:fs + 6],
                    accum_out=FST[:, fs + 6:fs + 7]),
                    waits=[("dve", ng), ("pe", pscl_rd[s][sb_ % 3])])
                exprev[s] = ex
                if sb_ == 0:
                    dn = P.op("dve", lambda v: v.tensor_copy(
                        FST[:, fs + 1:fs + 2], FST[:, fs + 6:fs + 7]),
                        waits=[("act", ex)])
                else:
                    al0 = P.op("dve", lambda v, acol=acol: v.tensor_sub(
                        FST[:, acol:acol + 1], FST[:, fs + 0:fs + 1],
                        FST[:, fs + 3:fs + 4]),
                        waits=[("act", ex)])
                    al1 = P.op("act", lambda s_, acol=acol: s_.activation(
                        FST[:, acol:acol + 1], FST[:, acol:acol + 1], AF.Exp),
                        waits=[("dve", al0)])
                    dn = P.op("dve", lambda v, acol=acol: v.scalar_tensor_tensor(
                        FST[:, fs + 1:fs + 2], FST[:, fs + 1:fs + 2],
                        FST[:, acol:acol + 1], FST[:, fs + 6:fs + 7],
                        ALU.mult, ALU.add), waits=[("act", al1)])
                P.op("dve", lambda v: v.tensor_copy(FST[:, fs + 0:fs + 1],
                                                    FST[:, fs + 3:fs + 4]),
                     waits=[("dve", dn)])
                exv[(s, sb_)] = ex

            def fstep_b(s, i, sb_):
                fs = s * 16
                acol = fs + 8 + (sb_ % 3)
                kslot = sb_ % 4
                vb = VBUF[:, kslot * 2048:(kslot + 1) * 2048]
                pscl = PSCL3[:, (3 * s + sb_ % 3) * 512:
                             (3 * s + sb_ % 3) * 512 + 512]
                ex = exv.pop((s, sb_))
                pts = PTS3[:, s * 512:(s + 1) * 512]
                for j in range(4):
                    t0 = P.op("pe", lambda t, j=j, pscl=pscl: t.transpose(
                        PT3[:, s * 512 + j * 128: s * 512 + (j + 1) * 128],
                        pscl[:, j * 128:(j + 1) * 128], IDB[:]),
                        waits=([("act", ex), ("act", pt3_rd[s])] if j == 0
                               else []),
                        noinc=(j < 3))
                pscl_rd[s][sb_ % 3] = t0
                a0 = P.op("act", lambda s_, pts=pts: s_.copy(
                    pts, PT3[:, s * 512:(s + 1) * 512]),
                    waits=[("pe", t0), ("pe", pts_rd[s])])
                pt3_rd[s] = a0
                pv3 = PV3[:, s * 512:(s + 1) * 512]
                pvl = None
                for j in range(4):
                    pvl = P.op("pe", lambda t, j=j, vb=vb, pts=pts, pv3=pv3:
                               t.matmul(
                        pv3, pts[:, j * 128:(j + 1) * 128],
                        vb[:, j * 512:(j + 1) * 512],
                        start=(j == 0), stop=(j == 3)),
                        waits=([("act", a0), ("dve", pv_rd[s])]
                               if j == 0 else ()),
                        noinc=(j < 3))
                pts_rd[s] = pvl
                kb_rd[sb_ % 4] = pvl
                oac = OACC[:, s * 512:(s + 1) * 512]
                if sb_ == 0:
                    ou = P.op("dve", lambda v, pv3=pv3, oac=oac: v.tensor_copy(
                        oac, pv3), waits=[("pe", pvl)])
                else:
                    ou = P.op("dve", lambda v, pv3=pv3, oac=oac, acol=acol:
                              v.scalar_tensor_tensor(
                        oac, oac, FST[:, acol:acol + 1], pv3,
                        ALU.mult, ALU.add), waits=[("pe", pvl)])
                pv_rd[s] = ou

            def fepilogue(s, i):
                nonlocal spv
                fs = s * 16
                oac = OACC[:, s * 512:(s + 1) * 512]
                rc = P.op("dve", lambda v: v.reciprocal(FST[:, fs + 7:fs + 8],
                                                        FST[:, fs + 1:fs + 2]),
                          waits=[("dve", P.count("dve"))])
                ob = P.op("dve", lambda v, oac=oac: v.tensor_scalar(
                    OB[:], oac, FST[:, fs + 7:fs + 8], None, ALU.mult),
                    waits=[("dve", rc)])
                for j in range(4):
                    tq = P.op("pe", lambda t, j=j: t.transpose(
                        PT3[:, s * 512 + j * 128: s * 512 + (j + 1) * 128],
                        OB[:, j * 128:(j + 1) * 128], IDB[:]),
                        waits=([("dve", ob), ("act", P.count("act"))] if j == 0
                               else []))
                oc = P.op("act", lambda s_: s_.copy(
                    OUTF[:], PT3[:, s * 512:(s + 1) * 512]),
                    waits=[("pe", P.count("pe"))])
                pt3_rd[s] = oc
                for hh in range(2):
                    w0 = mm(P, PSO[:],
                            [(OUTF[:, ci * 128:(ci + 1) * 128],
                              WT3[:, ci * 1024 + hh * 512: ci * 1024 + hh * 512 + 512])
                             for ci in range(4)] +
                            [(ONESRB[:], BOUT3[0:1, hh * 512:hh * 512 + 512])],
                            waits=[("pool", wt2), ("act", oc),
                                   ("dve", P.count("dve"))])
                    slot = SOUT[:, hh * 512:hh * 512 + 512]
                    sv = P.op("dve", lambda v, slot=slot: v.tensor_copy(
                        slot, PSO[:]), waits=[("pe", w0), ("sp", spv)])
                    spv = P.op("sp", lambda e, i=i, hh=hh, slot=slot: e.dma_start(
                        out=s_dram.ap()[i * 128:(i + 1) * 128,
                                        hh * 512:hh * 512 + 512],
                        in_=slot), waits=[("dve", sv)], amt=16)

            for ia in range(0, NT, 2):
                ib = ia + 1
                first = True
                for _ in range(7):
                    if castq:
                        cdst, csrc = castq.pop(0)
                        P.op("pool", lambda e, d=cdst, src=csrc: e.dma_start(
                            out=d, in_=src),
                            waits=([("dve", P.count("dve"))] if first else []),
                            amt=16, sem="cast3")
                        first = False
                na = (T0MAX + 128 * (ia + 1) + 511) // 512
                nb = (T0MAX + 128 * (ib + 1) + 511) // 512
                for k in range(max(na, nb)):
                    kload(k)
                    if k < na:
                        fstep_a(0, ia, k)
                    if k < nb:
                        fstep_a(1, ib, k)
                    if k >= 2 and k - 2 < na:
                        fstep_b(0, ia, k - 2)
                    if k >= 2 and k - 2 < nb:
                        fstep_b(1, ib, k - 2)
                for kk in range(max(0, max(na, nb) - 2), max(na, nb)):
                    if (0, kk) in exv:
                        fstep_b(0, ia, kk)
                    if (1, kk) in exv:
                        fstep_b(1, ib, kk)
                fepilogue(0, ia)
                fepilogue(1, ib)
            if "cast3_tot" not in sb:
                sb["cast3_tot"] = P.count("cast3")
        P.barrier()

    emit_l3_ffn(P, sb, dr, s_dram)


def emit_l3_ffn(P, sb, dr, s_dram):
    nc = P.nc
    X, XNT = sb["X"], sb["XNT"]
    ONESRB = sb["ONESRB"]
    SCRB = sb["SCRB"]

    with ExitStack() as lc:
        SC = lambda n, sh, dt: lc.enter_context(nc.sbuf_tensor(n, list(sh), dt))
        RW3 = SC("l3_RW3", (128, DC * 16), BF16)
        RB3 = SC("l3_RB3", (1, 16), BF16)
        Z16 = SC("l3_Z16", (128, 16), F32)
        GE16 = SC("l3_GE16", (128, 16), F32)
        EPF = SC("l3_EPF", (128, 8), F32)
        CBA3 = SC("l3_CBA3", (128, 256), F32)
        OA3 = SC("l3_OA3", (128, 8 * 1024), F32)
        HID = SC("l3_HID", (128, 4 * 1024), BF16)
        WUP = SC("l3_WUP", (128, 2 * 4096), BF16)
        WGT3 = SC("l3_WGT3", (128, 2 * 4096), BF16)
        WDN = SC("l3_WDN", (128, 2 * 4096), BF16)
        SIN3 = SC("l3_SIN3", (128, 1024), BF16)

        PR3 = lc.enter_context(nc.psum_tensor("l3_PR3", [128, 512], F32))
        PU3 = lc.enter_context(nc.psum_tensor("l3_PU3", [128, 2 * 512], F32))
        PG3 = lc.enter_context(nc.psum_tensor("l3_PG3", [128, 2 * 512], F32))
        PO3 = lc.enter_context(nc.psum_tensor("l3_PO3", [128, 2 * 512], F32))

        rl = P.op("pool", lambda e: e.dma_start(
            out=RW3[:].rearrange("p (c n) -> p c n", c=DC),
            in_=dr["l3_rwT"].rearrange("(c p) n -> p c n", p=128)), amt=16)
        rl2 = P.op("pool", lambda e: e.dma_start(out=RB3[:], in_=dr["l3_rb"]),
                   amt=16)

        wg_rd = [0, 0]
        wu_rd = [0, 0]
        wd_rd = [0, 0]
        c3t = sb.get("cast3_tot", 0)

        def wload3(e_, extra=()):
            s_ = e_ % 2
            a = P.op("pool", lambda e, e_=e_, s_=s_: e.dma_start(
                out=WGT3[:, s_ * 4096:(s_ + 1) * 4096], in_=dr["l3_wb3g"][e_]),
                waits=list(extra) + [("pe", wg_rd[s_])], amt=16, sem=f"w3g{s_}")
            b = P.op("pool", lambda e, e_=e_, s_=s_: e.dma_start(
                out=WUP[:, s_ * 4096:(s_ + 1) * 4096], in_=dr["l3_wb3u"][e_]),
                waits=[("pe", wu_rd[s_])], amt=16, sem=f"w3u{s_}")
            c = P.op("pool", lambda e, e_=e_, s_=s_: e.dma_start(
                out=WDN[:, s_ * 4096:(s_ + 1) * 4096], in_=dr["l3_wb3d"][e_]),
                waits=[("pe", wd_rd[s_])], amt=16, sem=f"w3d{s_}")
            return (a, b, c)

        seq = [(half, e_) for half in range(2) for e_ in range(18)]
        ldsmap = {}
        ldsmap[0] = wload3(0, extra=[("cast3", c3t)])
        ldsmap[1] = wload3(1)

        for t in range(NT):
            g0 = mm(P, PR3[:, 0:16],
                    [(XNT[:, dc * XSTR + 1 + t * 128: dc * XSTR + 1 + (t + 1) * 128],
                      RW3[:, dc * 16:(dc + 1) * 16]) for dc in range(DC)] +
                    [(ONESRB[:], RB3[:])],
                    waits=[("pool", rl2), ("dve", P.count("dve"))])
            z0 = P.op("dve", lambda v: v.tensor_copy(Z16[:], PR3[:, 0:16]),
                      waits=[("pe", g0)])
            m0 = P.op("dve", lambda v: v.reduce_max(EPF[:, 0:1], Z16[:], axis=AX),
                      waits=[("dve", z0)])
            m1 = P.op("dve", lambda v: v.tensor_scalar(
                GE16[:], Z16[:], EPF[:, 0:1], None, ALU.is_ge), waits=[("dve", m0)])
            m2 = P.op("dve", lambda v: v.scalar_tensor_tensor(
                GE16[:], GE16[:], -1e9, Z16[:], ALU.mult, ALU.add),
                waits=[("dve", m1)])
            m3 = P.op("dve", lambda v: v.reduce_max(EPF[:, 1:2], GE16[:], axis=AX),
                      waits=[("dve", m2)])
            c0 = P.op("dve", lambda v: v.tensor_scalar(
                GE16[:], Z16[:], EPF[:, 1:2], None, ALU.is_ge),
                waits=[("dve", m3)])
            d0 = P.op("dve", lambda v: v.tensor_sub(EPF[:, 2:3], EPF[:, 1:2],
                                                    EPF[:, 0:1]),
                      waits=[("dve", c0)])
            d1 = P.op("act", lambda s: s.activation(EPF[:, 3:4], EPF[:, 2:3],
                                                    AF.Exp),
                      waits=[("dve", d0)])
            d2 = P.op("dve", lambda v: v.tensor_scalar_add(EPF[:, 3:4],
                                                           EPF[:, 3:4], 1.0),
                      waits=[("act", d1)])
            d3 = P.op("dve", lambda v: v.reciprocal(EPF[:, 4:5], EPF[:, 3:4]),
                      waits=[("dve", d2)])
            n0 = P.op("dve", lambda v: v.tensor_scalar_mul(EPF[:, 5:6],
                                                           EPF[:, 0:1], -1.0),
                      waits=[("dve", d3)])
            e0 = P.op("act", lambda s: s.activation(
                Z16[:], Z16[:], AF.Exp, bias=EPF[:, 5:6]), waits=[("dve", n0)])
            c1 = P.op("dve", lambda v: v.tensor_scalar(
                Z16[:], Z16[:], EPF[:, 4:5], None, ALU.mult), waits=[("act", e0)])
            P.op("dve", lambda v, t=t: v.tensor_mul(
                CBA3[:, t * 16:(t + 1) * 16], Z16[:], GE16[:]),
                waits=[("dve", c1)])

        pg_rd = [0, 0]
        pu_rd = [0, 0]
        po_rd = [0, 0]
        scrb_rd = [0, 0]
        oa_last = 0

        def emit_combine(half):
            ov = oa_last
            for t8 in range(8):
                for hh in range(2):
                    t_ = (half * 1024 + t8 * 128) // 128
                    osl = OA3[:, t8 * 1024 + hh * 512: t8 * 1024 + hh * 512 + 512]
                    slot = SIN3[:, hh * 512:hh * 512 + 512]
                    sl = P.op("sp", lambda e, t_=t_, hh=hh, slot=slot: e.dma_start(
                        out=slot, in_=s_dram.ap()[t_ * 128:(t_ + 1) * 128,
                                                  hh * 512:hh * 512 + 512]),
                        waits=[("dve", ov)], amt=16)
                    ga = P.op("act", lambda s, osl=osl: s.activation(
                        SCRB[:, 0:512], osl, AF.Silu),
                        waits=[("dve", ov), ("act", P.count("act"))])
                    z3 = P.op("dve", lambda v, slot=slot: v.tensor_mul(
                        SCRB[:, 512:1024], SCRB[:, 0:512], slot),
                        waits=[("act", ga), ("sp", sl)])
                    xsl = X[:, t_ * D + hh * 512: t_ * D + hh * 512 + 512]
                    ov = P.op("dve", lambda v, xsl=xsl: v.tensor_add(
                        xsl, xsl, SCRB[:, 512:1024]), waits=[("dve", z3)])
            scrb_rd[0] = ov
            scrb_rd[1] = ov
            return ov

        oa_init = [False] * 16
        for si, (half, e_) in enumerate(seq):
            if half == 1 and e_ == 0:
                emit_combine(0)
                oa_init = [False] * 16
            ho = half * 1024
            slot = e_ % 2
            shared = e_ >= 16
            if si + 1 < len(seq) and (si + 1) not in ldsmap:
                ldsmap[si + 1] = wload3(seq[si + 1][1])
            lg, lu, ld_ = ldsmap[si]
            for hc in range(4):
                for tt in range(2):
                    ps = (hc * 2 + tt) % 2
                    xsl = [XNT[:, dc * XSTR + 1 + ho + tt * 512:
                               dc * XSTR + 1 + ho + tt * 512 + 512]
                           for dc in range(DC)]
                    gw = [("act", pg_rd[ps])]
                    uw = [("dve", pu_rd[ps])]
                    if hc == 0 and tt == 0:
                        gw.append((f"w3g{slot}", lg))
                        uw.append((f"w3u{slot}", lu))
                    gm = mm(P, PG3[:, ps * 512:(ps + 1) * 512],
                            [(WGT3[:, slot * 4096 + dc * 512 + hc * 128:
                                   slot * 4096 + dc * 512 + (hc + 1) * 128],
                              xsl[dc]) for dc in range(DC)], waits=gw)
                    um = mm(P, PU3[:, ps * 512:(ps + 1) * 512],
                            [(WUP[:, slot * 4096 + dc * 512 + hc * 128:
                                  slot * 4096 + dc * 512 + (hc + 1) * 128],
                              xsl[dc]) for dc in range(DC)], waits=uw)
                    sg = P.op("act", lambda s, ps=ps: s.activation(
                        SCRB[:, ps * 512:(ps + 1) * 512],
                        PG3[:, ps * 512:(ps + 1) * 512], AF.Silu),
                        waits=[("pe", gm), ("dve", scrb_rd[ps])])
                    dst = HID[:, hc * 1024 + tt * 512: hc * 1024 + tt * 512 + 512]
                    hm = P.op("dve", lambda v, dst=dst, ps=ps: v.tensor_mul(
                        dst, PU3[:, ps * 512:(ps + 1) * 512],
                        SCRB[:, ps * 512:(ps + 1) * 512]),
                        waits=[("pe", um), ("act", sg)])
                    pg_rd[ps] = sg
                    pu_rd[ps] = hm
                    scrb_rd[ps] = hm
            wg_rd[slot] = P.count("pe")
            wu_rd[slot] = P.count("pe")
            hall_dve = P.count("dve")
            for t8 in range(8):
                for hh in range(2):
                    ps = (t8 * 2 + hh) % 2
                    dw = [("dve", hall_dve), ("dve", po_rd[ps])]
                    if t8 == 0 and hh == 0:
                        dw.append((f"w3d{slot}", ld_))
                    o0 = mm(P, PO3[:, ps * 512:(ps + 1) * 512],
                            [(HID[:, hc * 1024 + t8 * 128:
                                  hc * 1024 + (t8 + 1) * 128],
                              WDN[:, slot * 4096 + hc * 1024 + hh * 512:
                                  slot * 4096 + hc * 1024 + hh * 512 + 512])
                             for hc in range(4)], waits=dw)
                    osl = OA3[:, t8 * 1024 + hh * 512: t8 * 1024 + hh * 512 + 512]
                    idx = t8 * 2 + hh
                    cb = (None if shared
                          else CBA3[:, (ho // 128 + t8) * 16 + e_:
                                    (ho // 128 + t8) * 16 + e_ + 1])
                    po = PO3[:, ps * 512:(ps + 1) * 512]
                    if not oa_init[idx]:
                        if cb is None:
                            oa_last = P.op("dve", lambda v, osl=osl, po=po:
                                           v.tensor_copy(osl, po),
                                           waits=[("pe", o0)])
                        else:
                            oa_last = P.op("dve", lambda v, osl=osl, po=po, cb=cb:
                                           v.tensor_scalar(
                                osl, po, cb, None, ALU.mult),
                                waits=[("pe", o0)])
                        oa_init[idx] = True
                    else:
                        if cb is None:
                            oa_last = P.op("dve", lambda v, osl=osl, po=po:
                                           v.scalar_tensor_tensor(
                                osl, po, 1.0, osl, ALU.mult, ALU.add),
                                waits=[("pe", o0)])
                        else:
                            oa_last = P.op("dve", lambda v, osl=osl, po=po, cb=cb:
                                           v.scalar_tensor_tensor(
                                osl, po, cb, osl, ALU.mult, ALU.add),
                                waits=[("pe", o0)])
                    po_rd[ps] = oa_last
            wd_rd[slot] = P.count("pe")
        emit_combine(1)
        P.barrier()
